# version 26
# baseline (speedup 1.0000x reference)
import numpy as np
import contextlib
import concourse.bass as bass
import concourse.mybir as mybir
from concourse.bass_utils import run_bass_kernel_spmd

F32 = mybir.dt.float32
F32R = mybir.dt.float32r
I32 = mybir.dt.int32
AF = mybir.ActivationFunctionType
ALU = mybir.AluOpType
AX = mybir.AxisListType

D = 1024
NTOK = 2064
SLABS = [(0, 384), (384, 384), (768, 384), (1152, 384), (1536, 384), (1920, 144)]
FSS = [(i * 256, 256) for i in range(8)] + [(2048, 16)]
CVELT = 8193.0
S2 = 1.0 / 16129.0
QSCALE = 0.125
EPS = 1e-5
DEBUG_TAPS = False


def tiles_of(T):
    out, b = [], 0
    while b < T:
        tt = 96 if T - b >= 96 else T - b
        out.append((b, tt))
        b += tt
    return out


def build_program():
    nc = bass.Bass("TRN2", target_bir_lowering=False, detect_race_conditions=False,
                   dynamic_dma_scratch_size=16384)

    xT = nc.dram_tensor("xT", [D, NTOK], F32, kind="ExternalInput").ap()
    wqT = nc.dram_tensor("wqT", [D, D], F32, kind="ExternalInput").ap()
    wkT = nc.dram_tensor("wkT", [D, D], F32, kind="ExternalInput").ap()
    wvT = nc.dram_tensor("wvT", [D, D], F32, kind="ExternalInput").ap()
    woT = nc.dram_tensor("woT", [D, D], F32, kind="ExternalInput").ap()
    biasm = nc.dram_tensor("biasm", [128, 24], F32, kind="ExternalInput").ap()
    bvrow = nc.dram_tensor("bvrow", [1, D], F32, kind="ExternalInput").ap()
    masks = nc.dram_tensor("masks", [96, 192], F32, kind="ExternalInput").ap()
    identc = nc.dram_tensor("identc", [128, 256], F32, kind="ExternalInput").ap()
    wgTi = nc.dram_tensor("wgTi", [D, 3072], I32, kind="ExternalInput").ap()
    wuTi = nc.dram_tensor("wuTi", [D, 3072], I32, kind="ExternalInput").ap()
    wdTi = nc.dram_tensor("wdTi", [3072, D], I32, kind="ExternalInput").ap()
    out_d = nc.dram_tensor("out", [D, NTOK], F32, kind="ExternalOutput").ap()
    wg_r = nc.dram_tensor("wg_r", [24, D, 128], F32R, kind="Internal").ap()
    wu_r = nc.dram_tensor("wu_r", [24, D, 128], F32R, kind="Internal").ap()
    wd_r = nc.dram_tensor("wd_r", [3072, D], F32R, kind="Internal").ap()
    ops = {k: [] for k in ("sync", "gpsimd", "tensor", "vector", "scalar")}
    SYNC_LANES = ["q%d" % i for i in range(6)]
    GP_LANES = ["g%d" % i for i in range(3)]
    cnt = {"tt": 0, "vv": 0, "ss": 0}
    for nm_ in SYNC_LANES + GP_LANES + ["gc"]:
        cnt[nm_] = 0
    lane_rr = {"sync": 0, "gpsimd": 0}
    seen = {k: {} for k in ops}
    sems = {}

    def W(eng, *waits):
        w = []
        for name, val in waits:
            if val <= 0 or seen[eng].get(name, -1) >= val:
                continue
            seen[eng][name] = val
            w.append((name, val))
        return w

    def emit(eng, waits, build, inc=None, inc_amt=1, chain=False):
        if chain:
            own = {"vector": "vv", "scalar": "ss", "tensor": "tt",
                   "gpsimd": "gc"}[eng]
            waits = list(waits) + W(eng, (own, cnt[own]))
        def fn(e, waits=waits, build=build, inc=inc, inc_amt=inc_amt):
            for name, val in waits:
                e.wait_ge(sems[name], val)
            ins = build(e)
            if inc is not None:
                ins.then_inc(sems[inc], inc_amt)
        ops[eng].append(fn)
        if inc is not None:
            cnt[inc] += inc_amt
            return cnt[inc]
        return None

    def dmae(eng, waits, build):
        lanes = SYNC_LANES if eng == "sync" else GP_LANES
        L = lanes[lane_rr[eng] % len(lanes)]
        lane_rr[eng] += 1
        waits = list(waits) + W(eng, (L, cnt[L]))
        val = emit(eng, waits, build, inc=L, inc_amt=16)
        return (L, val)

    es = contextlib.ExitStack()
    with es:
        sb = lambda name, shape, dt=F32: es.enter_context(nc.sbuf_tensor(name, shape, dt))
        x1f = sb("x1f", [128, 8 * NTOK])
        qsb = sb("qsb", [128, 8 * 384])
        ksb = sb("ksb", [128, 8 * 384])
        vsb = sb("vsb", [128, 4 * 1024])
        laT = sb("laT", [128, 8 * 384])
        arena_e = sb("arena_e", [128, 3840])
        wbuf = sb("wbuf", [128, 4 * 1024])
        x1hT = sb("x1hT", [128, 2048], F32R)
        x1lT = sb("x1lT", [128, 2048], F32R)
        hhT = sb("hhT", [128, 768], F32R)
        hlT = sb("hlT", [128, 768], F32R)
        wgfT = sb("wgfT", [128, 1024], F32R)
        wufT = sb("wufT", [128, 1024], F32R)
        wdfT = sb("wdfT", [128, 2048], F32R)
        id_sb = sb("id_sb", [128, 128])
        ones_sb = sb("ones_sb", [128, 128])
        mask_sb = sb("mask_sb", [96, 192])
        bias_sb = sb("bias_sb", [128, 32])
        bv_sb = sb("bv_sb", [1, D])
        nm_sb = sb("nm_sb", [96, 2])
        sums_sb = sb("sums_sb", [96, 64])
        srow = sb("srow", [1, 1536])

        x1v = x1f[:].rearrange("p (c t) -> p c t", c=8)
        lav = laT[:].rearrange("p (c t) -> p c t", c=8)
        mL = mask_sb[:, 0:96]
        mA = mask_sb[:, 96:192]

        esb = arena_e[0:96, 0:1536].rearrange("p (s x) -> p s x", s=16)
        eAb = arena_e[0:96, 1536:3072].rearrange("p (s x) -> p s x", s=16)
        psbv = esb
        ptsb8 = arena_e[0:96, 3072:3840].rearrange("p (g x) -> p g x", g=8)
        ytv = arena_e[:, 0:3072].rearrange("p (c t) -> p c t", c=8)
        sqscr = [arena_e[:, 3072 + 384 * i: 3072 + 384 * (i + 1)] for i in range(2)]
        sqf = [qsb[:, 0:256], qsb[:, 256:512]]
        x1h = x1hT[:].rearrange("p (c t) -> p c t", c=8)
        x1l = x1lT[:].rearrange("p (c t) -> p c t", c=8)
        x1h_f = x1hT[:].bitcast(F32).rearrange("p (c t) -> p c t", c=8)

        wslot = [wbuf[:, 1024 * i: 1024 * (i + 1)].rearrange("p (c o) -> p c o", c=8)
                 for i in range(4)]
        wg2s = [wgfT[:].rearrange("p (c j) -> p c j", c=8)]
        wu2s = [wufT[:].rearrange("p (c j) -> p c j", c=8)]
        wd2s = [wdfT[:, 1024 * i:1024 * (i + 1)] for i in range(2)]
        hscr = [qsb[:, 512 + 256 * i: 512 + 256 * (i + 1)] for i in range(3)]
        hful = [qsb[:, 1280 + 256 * i: 1280 + 256 * (i + 1)] for i in range(3)]
        uh = [qsb[:, 2048 + 256 * i: 2048 + 256 * (i + 1)] for i in range(3)]
        hh = [hhT[:, 256 * i: 256 * (i + 1)] for i in range(3)]
        hl = [hlT[:, 256 * i: 256 * (i + 1)] for i in range(3)]
        tscr = vsb[:, 0:2048].rearrange("p (c t) -> p c t", c=8)
        uscr = vsb[:, 2048:4096].rearrange("p (c t) -> p c t", c=8)
        outsb = laT[:, 0:2048].rearrange("p (c t) -> p c t", c=8)

        pBIG = es.enter_context(nc.psum_tensor("pBIG", [128, 2048], F32))
        pB4 = es.enter_context(nc.psum_tensor("pB4", [128, 512], F32))
        pB5 = es.enter_context(nc.psum_tensor("pB5", [128, 512], F32))
        pB67 = es.enter_context(nc.psum_tensor("pB67", [128, 1024], F32))
        pq = [pBIG[:, 0:512], pBIG[:, 512:1024]]
        pv = [pBIG[0:96, 1024:1536], pBIG[0:96, 1536:2048]]
        psv = pBIG[0:96, 0:1024].rearrange("p (s x) -> p s x", s=8)
        SLOT2HEAD = [0, 2, 4, 6, 1, 3, 5, 7, 8, 10, 12, 14, 9, 11, 13, 15]
        po = pq
        pstat1 = pBIG[0:1, 1024:1504]
        pstat2 = pBIG[0:1, 1536:2016]
        ptp = [pB4[:].rearrange("p (i x) -> p i x", i=4),
               pB5[:].rearrange("p (i x) -> p i x", i=4)]
        pfav = pB67[:].rearrange("p (c x) -> p c x", c=8)
        pAb = pB4
        pBb = pB5
        pg = [pBIG[:, 0:512], pBIG[:, 512:1024]]
        pu = [pBIG[:, 1024:1536], pBIG[:, 1536:2048]]

        def pd_slice(dc, Tss):
            if dc < 2:
                return pB4[:, 256 * dc: 256 * dc + Tss]
            if dc < 4:
                return pB5[:, 256 * (dc - 2): 256 * (dc - 2) + Tss]
            return pB67[:, 256 * (dc - 4): 256 * (dc - 4) + Tss]
        pstatA = pB67[0:1, 0:256]
        pstatB = pB67[0:1, 512:768]

        for name in ["tt", "vv", "ss", "gc"] + SYNC_LANES + GP_LANES:
            sems[name] = es.enter_context(nc.semaphore(name))
        block = es.enter_context(nc.Block())

        consts_ld = []
        for dst, src in ((id_sb[:], identc[:, 0:128]), (ones_sb[:], identc[:, 128:256]),
                         (mask_sb[:], masks), (bias_sb[:, 0:24], biasm), (bv_sb[:], bvrow)):
            consts_ld.append(dmae("sync", [],
                                  (lambda e, d=dst, s=src: e.dma_start(out=d, in_=s))))

        bqs_prep = None
        x_lds = []
        for t0, T in ((0, 1032), (1032, 1032)):
            x_lds.append(dmae("sync", [],
                              (lambda e, t0=t0, T=T: e.dma_start(
                                  out=x1v[:, :, t0:t0 + T],
                                  in_=xT[:, t0:t0 + T].rearrange("(c p) t -> p c t", p=128)))))

        bqs_prep = emit("vector", W("vector", *consts_ld),
                        (lambda e: e.tensor_scalar(bias_sb[:, 24:32], bias_sb[:, 0:8],
                                                   QSCALE, None, ALU.mult)),
                        inc="vv")

        conv_toks = []
        for wsrc, wdst in ((wgTi, wg_r), (wuTi, wu_r)):
            for hc in range(24):
                conv_toks.append(dmae("gpsimd", [],
                                      (lambda e, s=wsrc, d=wdst, hc=hc: e.dma_start(
                                          out=d[hc, :, :],
                                          in_=s[:, 128 * hc:128 * (hc + 1)]))))
        for qtr in range(4):
            conv_toks.append(dmae("gpsimd", [],
                                  (lambda e, qtr=qtr: e.dma_start(
                                      out=wd_r[768 * qtr:768 * (qtr + 1), :],
                                      in_=wdTi[768 * qtr:768 * (qtr + 1), :]))))
        conv_all_done = [(L, cnt[L]) for L in GP_LANES]

        wslot_last_use = [0, 0, 0, 0]
        evac_qk = [0, 0]
        prev_exp_ss = 0
        prev_nmax_vv = 0
        prev_slab_rows_vv = 0
        prev_apply_vv = 0
        prev_pfa_evac_vv = 0
        prev_ptsb_evac = [0, 0, 0, 0]
        av_groupslot = [0, 0]
        prev_tile_trans_tt = 0
        prev_yt_evac = [0, 0]

        def emit_qk_chunk(name, wT, outb, bcol, do_scale, oc, jt0, jT):
            st_ = qkstate
            ldc, slot_j = load_eighth_g(wT, oc, st_["nq"] % 4), st_["nq"] % 4
            st_["nq"] += 1
            bank = oc % 2
            wts = W("tensor", ldc, *x_lds) + W("tensor", ("ss", evac_qk[bank]),
                    ("vv", prev_yt_evac[bank]), ("vv", prev_nmax_vv),
                    ("ss", prev_exp_ss))
            last = None
            for k in range(8):
                last = emit("tensor", wts if k == 0 else [],
                            (lambda e, j=slot_j, k=k, b=bank, T=jT, t0=jt0:
                             e.matmul(pq[b][:, 0:T],
                                      wslot[j][:, k, 0:128],
                                      x1v[:, k, t0:t0 + T],
                                      start=(k == 0), stop=(k == 7))),
                            inc="tt" if k == 7 else None)
            wslot_last_use[slot_j] = last
            ev = emit("scalar", W("scalar", ("tt", last), ("vv", bqs_prep)),
                      (lambda e, oc=oc, b=bank, T=jT, outb=outb, bcol=bcol,
                              sc=do_scale:
                       e.activation(outb[:, 384 * oc:384 * oc + T],
                                    pq[b][:, 0:T], AF.Identity,
                                    bias=bias_sb[:, (24 if sc else bcol) + oc:
                                                 (24 if sc else bcol) + oc + 1],
                                    scale=QSCALE if sc else 1.0)),
                      inc="ss")
            evac_qk[bank] = ev
            qkstate["done"] = ev

        def qk_jobs_for(t0_, T_):
            return ([("q", wqT, qsb, 0, True, oc, t0_, T_) for oc in range(8)] +
                    [("k", wkT, ksb, 8, False, oc, t0_, T_) for oc in range(8)])

        qkstate = {"nq": 0, "done": 0}
        load_eighth_g = None
        pending_qk = qk_jobs_for(*SLABS[0])

        for si, (t0, T) in enumerate(SLABS):
            TILES = tiles_of(T)

            def load_eighth(wT, oc, slot_j):
                return dmae("sync", W("sync", ("tt", wslot_last_use[slot_j])),
                            (lambda e, wT=wT, oc=oc, j=slot_j: e.dma_start(
                                out=wslot[j][:, :, :],
                                in_=wT[:, 128 * oc:128 * (oc + 1)].rearrange(
                                    "(c p) o -> p c o", p=128))))

            load_eighth_g = load_eighth
            for job in pending_qk:
                emit_qk_chunk(*job)
            qk_evac_done = qkstate["done"]
            pending_next = (qk_jobs_for(*SLABS[si + 1]) if si + 1 < len(SLABS)
                            else [])

            def emit_v_tile(ti):
                nonlocal_state = None
                tb, TT = TILES[ti]
                for ot in range(8):
                    ldc, slot_j = load_eighth(wvT, ot, qkstate["nq"] % 4), qkstate["nq"] % 4
                    qkstate["nq"] += 1
                    b = vstate["bank"]
                    wts = W("tensor", ldc,
                            ("ss", pv_rd[b]), ("vv", prev_slab_rows_vv))
                    for k in range(8):
                        emit("tensor", wts if k == 0 else [],
                             (lambda e, j=slot_j, k=k, tb=tb, TT=TT, b=b, t0=t0:
                              e.matmul(pv[b][0:TT, 0:128],
                                       x1v[:, k, t0 + tb:t0 + tb + TT],
                                       wslot[j][:, k, 0:128],
                                       start=(k == 0), stop=False)),
                             inc=None)
                    last = emit("tensor", [],
                                (lambda e, ot=ot, tb=tb, TT=TT, b=b:
                                 e.matmul(pv[b][0:TT, 0:128],
                                          ones_sb[0:1, 0:TT],
                                          bv_sb[0:1, 128 * ot:128 * (ot + 1)],
                                          start=False, stop=True)),
                                inc="tt")
                    wslot_last_use[slot_j] = last
                    ev = emit("scalar", W("scalar", ("tt", last)),
                              (lambda e, ti=ti, ot=ot, TT=TT, b=b:
                               e.copy(vsb[0:TT, 1024 * ti + 128 * ot:1024 * ti + 128 * (ot + 1)],
                                      pv[b][0:TT, 0:128])),
                              inc="ss")
                    pv_rd[b] = ev
                    v_evac_tile[ti] = ev
                    vstate["bank"] ^= 1

            vstate = {"bank": 0}
            v_evac_tile = {}
            pv_rd = [0, 0]
            emit_v_tile(0)

            for ti, (tb, TT) in enumerate(TILES):
                for half in range(2):
                    first_wts = W("tensor", ("ss", qk_evac_done),
                                  ("ss", prev_exp_ss), ("vv", prev_nmax_vv))
                    sc_done = 0
                    order = [0, 4, 1, 5, 2, 6, 3, 7]
                    for oi, sl8 in enumerate(order):
                        s = 8 * half + sl8
                        h = SLOT2HEAD[s]
                        pp = h % 2
                        c = h // 2
                        pos = sl8 % 4
                        sc_done = emit(
                            "tensor", first_wts if oi == 0 else [],
                            (lambda e, sl8=sl8, pp=pp, c=c, tb=tb, TT=TT,
                                    st=(pos == 0), sp=(pos == 3):
                             e.matmul(psv[0:TT, sl8, 0:TT],
                                      qsb[64 * pp:64 * pp + 64, 384 * c + tb:384 * c + tb + TT],
                                      ksb[64 * pp:64 * pp + 64, 384 * c + tb:384 * c + tb + TT],
                                      start=st, stop=sp,
                                      tile_position=(64 * pp, 0))),
                            inc="tt")
                    nm = emit("vector", W("vector", ("tt", sc_done)),
                              (lambda e, TT=TT, half=half: e.tensor_reduce(
                                  nm_sb[0:TT, half:half + 1], psv[0:TT, :, 0:TT],
                                  AX.XY, ALU.max, negate=True)),
                              inc="vv")
                    prev_nmax_vv = nm
                    ex = emit("scalar", W("scalar", ("vv", nm),
                                          ("tt", prev_tile_trans_tt)),
                              (lambda e, TT=TT, half=half: e.activation(
                                  esb[0:TT, 8 * half:8 * half + 8, 0:TT],
                                  psv[0:TT, :, 0:TT], AF.Exp,
                                  bias=nm_sb[0:TT, half:half + 1], scale=1.0)),
                              inc="ss")
                    prev_exp_ss = ex
                if ti + 1 < len(TILES):
                    emit_v_tile(ti + 1)
                emit("vector", W("vector", ("ss", prev_exp_ss)),
                     (lambda e, TT=TT: e.tensor_tensor(
                         eAb[0:TT, :, 0:TT], esb[0:TT, :, 0:TT],
                         mA[0:TT, 0:TT].unsqueeze(1).broadcast_to((TT, 16, TT)),
                         ALU.mult)),
                     inc="vv")
                emit("vector", [],
                     (lambda e, TT=TT: e.tensor_tensor(
                         esb[0:TT, :, 0:TT], esb[0:TT, :, 0:TT],
                         mL[0:TT, 0:TT].unsqueeze(1).broadcast_to((TT, 16, TT)),
                         ALU.mult)),
                     inc="vv")
                emit("vector", [],
                     (lambda e, TT=TT: e.tensor_reduce(
                         sums_sb[0:TT, 0:16], esb[0:TT, :, 0:TT], AX.X, ALU.add)),
                     inc="vv")
                emit("vector", [],
                     (lambda e, TT=TT: e.tensor_reduce(
                         sums_sb[0:TT, 16:32], eAb[0:TT, :, 0:TT], AX.X, ALU.add)),
                     inc="vv")
                emit("vector", [],
                     (lambda e, TT=TT: e.reciprocal(sums_sb[0:TT, 32:48],
                                                    sums_sb[0:TT, 0:16])),
                     inc="vv", chain=True)
                emit("vector", [],
                     (lambda e, TT=TT: e.reciprocal(sums_sb[0:TT, 48:64],
                                                    sums_sb[0:TT, 16:32])),
                     inc="vv", chain=True)
                p_done = {}
                for s in range(16):
                    emit("vector", [],
                         (lambda e, s=s, TT=TT: e.tensor_scalar(
                             psbv[0:TT, s, 0:TT], esb[0:TT, s, 0:TT],
                             sums_sb[0:TT, 32 + s:33 + s], None, ALU.mult)),
                         inc="vv", chain=(s == 0))
                for s in range(16):
                    p_done[s] = emit("vector", [],
                                     (lambda e, s=s, TT=TT: e.scalar_tensor_tensor(
                                         psbv[0:TT, s, 0:TT], eAb[0:TT, s, 0:TT],
                                         sums_sb[0:TT, 48 + s:49 + s],
                                         psbv[0:TT, s, 0:TT], ALU.mult, ALU.add)),
                                     inc="vv")
                av_first_wts = W("tensor", ("vv", prev_pfa_evac_vv))
                trans_last = 0
                started = set()
                last_touch = {}
                for s in range(16):
                    h = SLOT2HEAD[s]
                    last_touch[(6 + (h // 2) // 4, h % 2)] = s
                evac_group = {}
                for g in range(4):
                    par = g % 2
                    gs = g % 2
                    for i in range(4):
                        s = 4 * g + i
                        wts = W("tensor", ("vv", p_done[s]),
                                ("tt", av_groupslot[gs]))
                        trans_last = emit(
                            "tensor", wts,
                            (lambda e, s=s, par=par, i=i, TT=TT: e.transpose(
                                ptp[par][0:96, i, 0:TT], psbv[0:TT, s, 0:96],
                                id_sb[0:TT, 0:TT])),
                            inc="tt")
                    ev = emit("vector", W("vector", ("tt", trans_last)),
                              (lambda e, gs=gs, par=par, TT=TT:
                               e.tensor_copy(ptsb8[0:TT, 4 * gs:4 * gs + 4, 0:TT],
                                             ptp[par][0:TT, :, 0:TT])),
                              inc="vv")
                    prev_ptsb_evac[gs] = ev
                    evac_group[g] = ev
                    av_last = 0
                    for i in range(4):
                        s = 4 * g + i
                        h = SLOT2HEAD[s]
                        pp = h % 2
                        c = h // 2
                        key = (6 + c // 4, pp)
                        startb = key not in started
                        started.add(key)
                        stopb = (last_touch[key] == s)
                        wts = W("tensor", ("vv", ev), ("ss", v_evac_tile[ti])) + (
                            av_first_wts if s == 0 else [])
                        av_last = emit("tensor", wts,
                             (lambda e, s=s, h=h, pp=pp, c=c, ti=ti, TT=TT, gs=gs, i=i,
                                     stb=startb, spb=stopb:
                              e.matmul(pfav[64 * pp:64 * pp + 64, c, 0:TT],
                                       vsb[0:TT, 1024 * ti + 64 * h:1024 * ti + 64 * h + 64],
                                       ptsb8[0:TT, 4 * gs + i, 0:TT],
                                       start=stb, stop=spb,
                                       tile_position=(0, 64 * pp))),
                             inc="tt")
                    av_groupslot[gs] = av_last
                    pass
                prev_tile_trans_tt = trans_last
                av_all = cnt["tt"]
                prev_pfa_evac_vv = emit(
                    "vector", W("vector", ("tt", av_all)),
                    (lambda e, tb=tb, TT=TT: e.tensor_copy(
                        lav[:, :, tb:tb + TT], pfav[:, :, 0:TT])),
                    inc="vv")

            pending_qk = pending_next
            o_first_wts = W("tensor", ("vv", prev_pfa_evac_vv),
                            ("ss", prev_exp_ss), ("vv", prev_nmax_vv))
            yt_done = 0
            for oc in range(8):
                ldc, slot_j = load_eighth(woT, oc, qkstate["nq"] % 4), qkstate["nq"] % 4
                qkstate["nq"] += 1
                bank = oc % 2
                wts = W("tensor", ldc, ("ss", qkstate["done"]),
                        ("vv", prev_yt_evac[bank])) + (o_first_wts if oc == 0 else [])
                last = None
                for k in range(8):
                    last = emit("tensor", wts if k == 0 else [],
                                (lambda e, j=slot_j, k=k, b=bank, T=T:
                                 e.matmul(po[b][:, 0:T],
                                          wslot[j][:, k, 0:128],
                                          lav[:, k, 0:T],
                                          start=(k == 0), stop=(k == 7))),
                                inc="tt" if k == 7 else None)
                wslot_last_use[slot_j] = last
                ev = emit("vector", W("vector", ("tt", last)),
                          (lambda e, oc=oc, b=bank, T=T, t0=t0: e.scalar_tensor_tensor(
                              ytv[:, oc, 0:T], po[b][:, 0:T],
                              bias_sb[:, 16 + oc:17 + oc], x1v[:, oc, t0:t0 + T],
                              ALU.add, ALU.add)),
                          inc="vv")
                prev_yt_evac[bank] = ev
                yt_done = ev
            sq_tt = [0, 0]
            st1 = st2 = 0
            for oc in range(8):
                sl = oc % 2
                sq = emit("scalar", W("scalar", ("vv", prev_yt_evac[oc % 2]),
                                      ("tt", sq_tt[sl])),
                          (lambda e, oc=oc, sl=sl, T=T: e.activation(
                              sqscr[sl][:, 0:T], ytv[:, oc, 0:T], AF.Square)),
                          inc="ss")
                st1 = emit("tensor", W("tensor", ("vv", prev_yt_evac[oc % 2]),
                                       ("vv", prev_slab_rows_vv)),
                           (lambda e, oc=oc, T=T: e.matmul(
                               pstat1[:, 0:T], ones_sb[:, 0:1], ytv[:, oc, 0:T],
                               start=(oc == 0), stop=(oc == 7))),
                           inc="tt")
                st2 = emit("tensor", W("tensor", ("ss", sq)),
                           (lambda e, sl=sl, oc=oc, T=T: e.matmul(
                               pstat2[:, 0:T], ones_sb[:, 0:1], sqscr[sl][:, 0:T],
                               start=(oc == 0), stop=(oc == 7))),
                           inc="tt")
                sq_tt[sl] = st2
            emit("vector", W("vector", ("tt", st1), ("tt", st2)),
                 (lambda e, T=T: e.tensor_scalar(srow[:, 0:T], pstat1[:, 0:T],
                                                 1.0 / 1024.0, None, ALU.mult)),
                 inc="vv")
            emit("vector", [],
                 (lambda e, T=T: e.tensor_scalar(srow[:, 384:384 + T], pstat2[:, 0:T],
                                                 1.0 / 1024.0, None, ALU.mult)),
                 inc="vv")
            emit("vector", [],
                 (lambda e, T=T: e.tensor_tensor(srow[:, 768:768 + T], srow[:, 0:T],
                                                 srow[:, 0:T], ALU.mult)),
                 inc="vv", chain=True)
            emit("vector", [],
                 (lambda e, T=T: e.tensor_tensor(
                     srow[:, 1152:1152 + T], srow[:, 384:384 + T],
                     srow[:, 768:768 + T], ALU.subtract)),
                 inc="vv", chain=True)
            rows_v = emit("vector", [],
                          (lambda e, T=T: e.tensor_scalar(
                              srow[:, 1152:1152 + T], srow[:, 1152:1152 + T],
                              EPS, None, ALU.add)),
                          inc="vv", chain=True)
            emit("scalar", W("scalar", ("vv", rows_v)),
                 (lambda e, T=T: e.activation(srow[:, 768:768 + T],
                                              srow[:, 1152:1152 + T], AF.Ln)),
                 inc="ss")
            r_row = emit("scalar", [],
                         (lambda e, T=T: e.activation(srow[:, 384:384 + T],
                                                      srow[:, 768:768 + T], AF.Exp,
                                                      scale=-0.5)),
                         inc="ss", chain=True)
            b_row = emit("vector", W("vector", ("ss", r_row)),
                         (lambda e, T=T: e.scalar_tensor_tensor(
                             srow[:, 1152:1152 + T], srow[:, 0:T], -1.0,
                             srow[:, 384:384 + T], ALU.mult, ALU.mult)),
                         inc="vv", chain=True)
            bc1 = emit("tensor", W("tensor", ("ss", r_row), ("vv", prev_apply_vv),
                                   ("vv", prev_ptsb_evac[0]), ("vv", prev_ptsb_evac[1]),
                                   ("vv", prev_ptsb_evac[2]), ("vv", prev_ptsb_evac[3])),
                       (lambda e, T=T: e.matmul(pAb[:, 0:T], ones_sb[0:1, :],
                                                srow[0:1, 384:384 + T],
                                                start=True, stop=True)),
                       inc="tt")
            bc2 = emit("tensor", W("tensor", ("vv", b_row)),
                       (lambda e, T=T: e.matmul(pBb[:, 0:T], ones_sb[0:1, :],
                                                srow[0:1, 1152:1152 + T],
                                                start=True, stop=True)),
                       inc="tt")
            ap_v = 0
            for oc in range(8):
                sl = oc % 2
                emit("vector", W("vector", ("tt", bc1), ("tt", bc2)),
                     (lambda e, oc=oc, sl=sl, T=T: e.tensor_tensor(
                         sqscr[sl][:, 0:T], ytv[:, oc, 0:T], pAb[:, 0:T], ALU.mult)),
                     inc="vv")
                ap_v = emit("vector", [],
                            (lambda e, oc=oc, sl=sl, T=T, t0=t0: e.tensor_tensor(
                                x1v[:, oc, t0:t0 + T], sqscr[sl][:, 0:T], pBb[:, 0:T],
                                ALU.add)),
                            inc="vv", chain=True)
            prev_apply_vv = ap_v
            prev_slab_rows_vv = rows_v

        slab_done_vv = prev_apply_vv
        o_last_tt = cnt["tt"]
        wf_lduse = {"g": [o_last_tt, o_last_tt], "u": [o_last_tt, o_last_tt],
                    "d": [o_last_tt, o_last_tt]}
        prev_y2_evac = 0
        prev_ln2rows_vv = 0
        hbuf_lastmm = [o_last_tt, o_last_tt, o_last_tt]
        pgu_read = [[0, 0], [0, 0]]
        out_dma = []
        prev_out_evac = 0
        def emit_split(fi):
            f0, Tss = FSS[fi]
            if Tss < 256:
                return cnt["vv"]
            emit("vector", [],
                 (lambda e, f0=f0, Tss=Tss: e.tensor_scalar(
                     tscr[:, :, 0:Tss], x1v[:, :, f0:f0 + Tss], CVELT, None, ALU.mult)),
                 inc="vv")
            emit("vector", [],
                 (lambda e, f0=f0, Tss=Tss: e.scalar_tensor_tensor(
                     uscr[:, :, 0:Tss], tscr[:, :, 0:Tss], 1.0,
                     x1v[:, :, f0:f0 + Tss], ALU.mult, ALU.subtract)),
                 inc="vv", chain=True)
            emit("vector", [],
                 (lambda e, Tss=Tss: e.scalar_tensor_tensor(
                     x1h[:, :, 0:Tss], tscr[:, :, 0:Tss], 1.0,
                     uscr[:, :, 0:Tss], ALU.mult, ALU.subtract)),
                 inc="vv", chain=True)
            return emit("vector", [],
                        (lambda e, f0=f0, Tss=Tss: e.tensor_tensor(
                            x1l[:, :, 0:Tss], x1v[:, :, f0:f0 + Tss],
                            x1h_f[:, :, 0:Tss], ALU.subtract)),
                        inc="vv", chain=True)

        def emit_tail_ffn():
            tf_vv = cnt["vv"]
            tf_ss = cnt["ss"]
            gbuf = [wgfT[:, 0:512], wgfT[:, 512:1024]]
            ubuf = [wufT[:, 0:512], wufT[:, 512:1024]]
            g_last = {0: wf_lduse["g"][0], 1: wf_lduse["g"][0]}
            u_last = {0: wf_lduse["u"][0], 1: wf_lduse["u"][0]}
            ldd_t = {0: wf_lduse["d"][0], 1: wf_lduse["d"][1]}
            g_sil = qsb[:, 2048:2560]
            hp_t = qsb[:, 1280:1792]
            hT_sb = qsb[:, 512:896]
            y_sc = qsb[:, 512:1536]
            ptr = [pB67[:, 256:272], pB67[:, 768:784]]
            silu_rd = [0, 0]
            mult_rd = [0, 0]
            tr_evac = {}
            mus = {}
            st_t = {"dn": 0, "first": True}

            def emit_td(c):
                for q in range(4):
                    hc = 4 * c + q
                    sp_ = hc % 2
                    trl = emit("tensor",
                               W("tensor", ("vv", mus[c]),
                                 ("vv", tr_evac.get(hc - 2, 0))),
                               (lambda e, q=q, sp_=sp_: e.transpose(
                                   ptr[sp_][:, 0:16],
                                   hp_t[0:16, 128 * q:128 * (q + 1)],
                                   id_sb[0:16, 0:16])),
                               inc="tt")
                    ev = emit("vector", W("vector", ("tt", trl)),
                              (lambda e, hc=hc, sp_=sp_: e.tensor_copy(
                                  hT_sb[:, 16 * hc:16 * hc + 16],
                                  ptr[sp_][:, 0:16])),
                              inc="vv")
                    tr_evac[hc] = ev
                    jd = hc % 2
                    ldd = dmae("gpsimd",
                               W("gpsimd", ("tt", ldd_t[jd]), *conv_all_done),
                               (lambda e, hc=hc, jd=jd: e.dma_start(
                                   out=wd2s[jd][:, :],
                                   in_=wd_r[128 * hc:128 * (hc + 1), :])))
                    dnl = None
                    for h in range(2):
                        dnl = emit("tensor",
                                   (W("tensor", ldd, ("vv", ev))
                                    if h == 0 else []),
                                   (lambda e, hc=hc, jd=jd, h=h: e.matmul(
                                       (pB4 if h == 0 else pB5)[0:16, 0:512],
                                       hT_sb[:, 16 * hc:16 * hc + 16],
                                       wd2s[jd][:, 512 * h:512 * (h + 1)]
                                       .bitcast(F32),
                                       start=(hc == 0), stop=(hc == 23))),
                                   inc="tt")
                    ldd_t[jd] = dnl
                    st_t["dn"] = dnl

            for c in range(6):
                par = c % 2
                gl = ul = None
                for k in range(8):
                    kp = k % 2
                    ldg = dmae("sync", W("sync", ("tt", g_last[kp])),
                               (lambda e, c=c, k=k, kp=kp: e.dma_start(
                                   out=gbuf[kp].rearrange(
                                       "p (h j) -> p h j", h=4),
                                   in_=wg_r[4 * c:4 * c + 4,
                                            128 * k:128 * (k + 1), :]
                                   .rearrange("h p j -> p h j"))))
                    wts = W("tensor", ldg)
                    if k == 0:
                        wts += W("tensor", ("ss", silu_rd[par]))
                        if st_t["first"]:
                            wts += W("tensor", ("vv", tf_vv), ("ss", tf_ss))
                            st_t["first"] = False
                    gl = emit("tensor", wts,
                              (lambda e, k=k, kp=kp, par=par: e.matmul(
                                  pg[par][0:16, 0:512],
                                  x1v[:, k, 2048:2064],
                                  gbuf[kp].bitcast(F32),
                                  start=(k == 0), stop=(k == 7))),
                              inc="tt")
                    g_last[kp] = gl
                for k in range(8):
                    kp = k % 2
                    ldu = dmae("sync", W("sync", ("tt", u_last[kp])),
                               (lambda e, c=c, k=k, kp=kp: e.dma_start(
                                   out=ubuf[kp].rearrange(
                                       "p (h j) -> p h j", h=4),
                                   in_=wu_r[4 * c:4 * c + 4,
                                            128 * k:128 * (k + 1), :]
                                   .rearrange("h p j -> p h j"))))
                    wts = W("tensor", ldu)
                    if k == 0:
                        wts += W("tensor", ("vv", mult_rd[par]))
                    ul = emit("tensor", wts,
                              (lambda e, k=k, kp=kp, par=par: e.matmul(
                                  pu[par][0:16, 0:512],
                                  x1v[:, k, 2048:2064],
                                  ubuf[kp].bitcast(F32),
                                  start=(k == 0), stop=(k == 7))),
                              inc="tt")
                    u_last[kp] = ul
                if c >= 1:
                    emit_td(c - 1)
                sl = emit("scalar", W("scalar", ("tt", gl), ("vv", tf_vv)),
                          (lambda e, par=par: e.activation(
                              g_sil[0:16, 0:512], pg[par][0:16, 0:512],
                              AF.Silu, scale=1.0 / 127.0)),
                          inc="ss")
                silu_rd[par] = sl
                mu = emit("vector", W("vector", ("ss", sl), ("tt", ul)),
                          (lambda e, par=par: e.tensor_tensor(
                              hp_t[0:16, 0:512], g_sil[0:16, 0:512],
                              pu[par][0:16, 0:512], ALU.mult)),
                          inc="vv")
                mult_rd[par] = mu
                mus[c] = mu
            emit_td(5)
            ye = emit("vector", W("vector", ("tt", st_t["dn"])),
                      (lambda e: e.tensor_copy(y_sc[0:16, 0:512],
                                               pB4[0:16, 0:512])),
                      inc="vv")
            ye = emit("vector", [],
                      (lambda e: e.tensor_copy(y_sc[0:16, 512:1024],
                                               pB5[0:16, 0:512])),
                      inc="vv", chain=True)
            y2_ev = 0
            ytr = {}
            for dc in range(8):
                sp_ = dc % 2
                trl = emit("tensor",
                           W("tensor", ("vv", ye), ("vv", ytr.get(dc - 2, 0))),
                           (lambda e, dc=dc, sp_=sp_: e.transpose(
                               ptr[sp_][:, 0:16],
                               y_sc[0:16, 128 * dc:128 * (dc + 1)],
                               id_sb[0:16, 0:16])),
                           inc="tt")
                y2_ev = emit("vector", W("vector", ("tt", trl)),
                             (lambda e, dc=dc, sp_=sp_: e.scalar_tensor_tensor(
                                 x1v[:, dc, 2048:2064], ptr[sp_][:, 0:16], S2,
                                 x1v[:, dc, 2048:2064], ALU.mult, ALU.add)),
                             inc="vv", chain=True)
                ytr[dc] = y2_ev
            return y2_ev

        split_done_fi = {0: emit_split(0)}
        for fi, (f0, Tss) in enumerate(FSS):
            tail = (Tss < 256)
            split_done = split_done_fi[fi]
            dn_last = 0
            down_args = [None]
            def emit_down(hc_d, j_d, hsp_d, ldd_d, hc_w, tail, Tss):
                wts = W("tensor", ldd_d, ("vv", hsp_d), ("vv", slab_done_vv),
                        ("vv", prev_y2_evac), ("vv", prev_ln2rows_vv))
                hops = [hful[j_d]] if tail else [hh[j_d], hl[j_d]]
                nh = len(hops)
                dl = None
                for dc in range(8):
                    for pi, hop in enumerate(hops):
                        st = (hc_d == 0 and pi == 0 and dc % 2 == 0)
                        sp = (hc_d == 23 and pi == nh - 1 and dc % 2 == 1)
                        dl = emit(
                            "tensor", wts if (dc == 0 and pi == 0) else [],
                            (lambda e, hc_w=hc_w, dc=dc, hop=hop, Tss=Tss, st=st, sp=sp,
                                    tl=tail:
                             e.matmul(pd_slice(dc, Tss),
                                      wd2s[hc_w % 2][:, 128 * dc:128 * (dc + 1)].bitcast(F32)
                                      if tl else wd2s[hc_w % 2][:, 128 * dc:128 * (dc + 1)],
                                      hop[:, 0:Tss], start=st, stop=sp)),
                            inc="tt" if (dc == 7 and pi == nh - 1) else None)
                return dl

            y2_ev = emit_tail_ffn() if tail else 0
            pending = []
            for hc in ([] if tail else range(24)):
                j = hc % 3
                jd = hc % 2
                lg = dmae("sync", W("sync", ("tt", wf_lduse["g"][0]),
                                    *conv_all_done),
                          (lambda e, hc=hc: e.dma_start(
                              out=wg2s[0][:, :, :],
                              in_=wg_r[hc, :, :].rearrange("(c p) j -> p c j", p=128))))
                lu = dmae("sync", W("sync", ("tt", wf_lduse["u"][0])),
                          (lambda e, hc=hc: e.dma_start(
                              out=wu2s[0][:, :, :],
                              in_=wu_r[hc, :, :].rearrange("(c p) j -> p c j", p=128))))
                bank = hc % 2
                if tail:
                    xops = [None]
                else:
                    xops = [x1h, x1l]
                npass = len(xops)
                wts = W("tensor", lg, ("vv", split_done),
                        ("ss", prev_exp_ss), ("vv", prev_nmax_vv),
                        ("vv", prev_yt_evac[0]), ("vv", prev_yt_evac[1]),
                        ("ss", pgu_read[0][bank]))
                if tail:
                    wts += W("tensor", ("vv", slab_done_vv))
                gl = None
                for k in range(8):
                    for pi, xop in enumerate(xops):
                        first = (k == 0 and pi == 0)
                        lastm = (k == 7 and pi == npass - 1)
                        gl = emit("tensor", wts if first else [],
                                  (lambda e, j=j, k=k, xop=xop, b=bank, Tss=Tss, f0=f0,
                                          st=first, sp=lastm:
                                   e.matmul(pg[b][:, 0:Tss],
                                            wg2s[0][:, k, :].bitcast(F32) if xop is None
                                            else wg2s[0][:, k, :],
                                            x1v[:, k, f0:f0 + Tss] if xop is None
                                            else xop[:, k, 0:Tss],
                                            start=st, stop=sp)),
                                  inc="tt" if lastm else None)
                wf_lduse["g"][0] = gl
                wts = W("tensor", lu, ("vv", pgu_read[1][bank]))
                ul = None
                for k in range(8):
                    for pi, xop in enumerate(xops):
                        first = (k == 0 and pi == 0)
                        lastm = (k == 7 and pi == npass - 1)
                        ul = emit("tensor", wts if first else [],
                                  (lambda e, j=j, k=k, xop=xop, b=bank, Tss=Tss, f0=f0,
                                          st=first, sp=lastm:
                                   e.matmul(pu[b][:, 0:Tss],
                                            wu2s[0][:, k, :].bitcast(F32) if xop is None
                                            else wu2s[0][:, k, :],
                                            x1v[:, k, f0:f0 + Tss] if xop is None
                                            else xop[:, k, 0:Tss],
                                            start=st, stop=sp)),
                                  inc="tt" if lastm else None)
                wf_lduse["u"][0] = ul
                if len(pending) >= 2:
                    pd_args = pending.pop(0)
                    dn_last = emit_down(*pd_args, tail, Tss)
                    wf_lduse["d"][pd_args[4] % 2] = dn_last
                    hbuf_lastmm[pd_args[1]] = dn_last
                ldd = dmae("gpsimd", W("gpsimd", ("tt", wf_lduse["d"][jd]),
                                       *conv_all_done),
                           (lambda e, hc=hc, jd=jd: e.dma_start(
                               out=wd2s[jd][:, :],
                               in_=wd_r[128 * hc:128 * (hc + 1), :])))
                sl = emit("scalar", W("scalar", ("tt", gl), ("tt", hbuf_lastmm[j])),
                          (lambda e, j=j, b=bank, Tss=Tss: e.activation(
                              hscr[j][:, 0:Tss], pg[b][:, 0:Tss], AF.Silu,
                              scale=1.0 / 127.0)),
                          inc="ss")
                pgu_read[0][bank] = sl
                mu = emit("vector", W("vector", ("ss", sl), ("tt", ul),
                                      ("tt", hbuf_lastmm[j])),
                          (lambda e, j=j, b=bank, Tss=Tss: e.tensor_tensor(
                              hful[j][:, 0:Tss], hscr[j][:, 0:Tss],
                              pu[b][:, 0:Tss], ALU.mult)),
                          inc="vv")
                pgu_read[1][bank] = mu
                if tail:
                    hsp = mu
                else:
                    emit("vector", [],
                         (lambda e, j=j, Tss=Tss: e.tensor_scalar(
                             hscr[j][:, 0:Tss], hful[j][:, 0:Tss], CVELT, None, ALU.mult)),
                         inc="vv", chain=True)
                    emit("vector", [],
                         (lambda e, j=j, Tss=Tss: e.scalar_tensor_tensor(
                             uh[j][:, 0:Tss], hscr[j][:, 0:Tss], 1.0,
                             hful[j][:, 0:Tss], ALU.mult, ALU.subtract)),
                         inc="vv", chain=True)
                    emit("vector", [],
                         (lambda e, j=j, Tss=Tss: e.scalar_tensor_tensor(
                             hh[j][:, 0:Tss], hscr[j][:, 0:Tss], 1.0,
                             uh[j][:, 0:Tss], ALU.mult, ALU.subtract)),
                         inc="vv", chain=True)
                    hsp = emit("vector", [],
                               (lambda e, j=j, Tss=Tss: e.tensor_tensor(
                                   hl[j][:, 0:Tss], hful[j][:, 0:Tss],
                                   hh[j][:, 0:Tss].bitcast(F32), ALU.subtract)),
                               inc="vv", chain=True)
                pending.append((hc, j, hsp, ldd, hc))
            while pending:
                pd_args = pending.pop(0)
                dn_last = emit_down(*pd_args, tail, Tss)
                wf_lduse["d"][pd_args[4] % 2] = dn_last
                hbuf_lastmm[pd_args[1]] = dn_last
            if fi + 1 < len(FSS):
                emit("vector", W("vector", ("tt", wf_lduse["u"][0])), 
                     (lambda e: e.engine_nop()), inc="vv")
                split_done_fi[fi + 1] = emit_split(fi + 1)
            for dc in ([] if tail else range(8)):
                y2_ev = emit("vector", W("vector", ("tt", dn_last)),
                             (lambda e, dc=dc, f0=f0, Tss=Tss: e.scalar_tensor_tensor(
                                 x1v[:, dc, f0:f0 + Tss], pd_slice(dc, Tss), S2,
                                 x1v[:, dc, f0:f0 + Tss], ALU.mult, ALU.add)),
                             inc="vv")
            prev_y2_evac = y2_ev
            sq_tt2 = [0, 0]
            s1 = s2 = 0
            for dc in range(8):
                sl2 = dc % 2
                sq = emit("scalar", W("scalar", ("vv", y2_ev), ("tt", sq_tt2[sl2])),
                          (lambda e, dc=dc, sl2=sl2, f0=f0, Tss=Tss: e.activation(
                              sqf[sl2][:, 0:Tss], x1v[:, dc, f0:f0 + Tss], AF.Square)),
                          inc="ss")
                s1 = emit("tensor", W("tensor", ("vv", y2_ev)),
                          (lambda e, dc=dc, f0=f0, Tss=Tss: e.matmul(
                              pstatA[:, 0:Tss], ones_sb[:, 0:1],
                              x1v[:, dc, f0:f0 + Tss],
                              start=(dc == 0), stop=(dc == 7))),
                          inc="tt")
                s2 = emit("tensor", W("tensor", ("ss", sq)),
                          (lambda e, dc=dc, sl2=sl2, Tss=Tss: e.matmul(
                              pstatB[:, 0:Tss], ones_sb[:, 0:1], sqf[sl2][:, 0:Tss],
                              start=(dc == 0), stop=(dc == 7))),
                          inc="tt")
                sq_tt2[sl2] = s2
            emit("vector", W("vector", ("tt", s1), ("tt", s2)),
                 (lambda e, Tss=Tss: e.tensor_scalar(
                     srow[:, 0:Tss], pstatA[:, 0:Tss], 1.0 / 1024.0, None, ALU.mult)),
                 inc="vv")
            emit("vector", [],
                 (lambda e, Tss=Tss: e.tensor_scalar(
                     srow[:, 384:384 + Tss], pstatB[:, 0:Tss], 1.0 / 1024.0, None,
                     ALU.mult)),
                 inc="vv")
            emit("vector", [],
                 (lambda e, Tss=Tss: e.tensor_tensor(
                     srow[:, 768:768 + Tss], srow[:, 0:Tss], srow[:, 0:Tss],
                     ALU.mult)),
                 inc="vv", chain=True)
            emit("vector", [],
                 (lambda e, Tss=Tss: e.tensor_tensor(
                     srow[:, 1152:1152 + Tss], srow[:, 384:384 + Tss],
                     srow[:, 768:768 + Tss], ALU.subtract)),
                 inc="vv", chain=True)
            rows2_v = emit("vector", [],
                           (lambda e, Tss=Tss: e.tensor_scalar(
                               srow[:, 1152:1152 + Tss], srow[:, 1152:1152 + Tss],
                               EPS, None, ALU.add)),
                           inc="vv", chain=True)
            _taps = []
            emit("scalar", W("scalar", ("vv", rows2_v), *_taps),
                 (lambda e, Tss=Tss: e.activation(srow[:, 768:768 + Tss],
                                                  srow[:, 1152:1152 + Tss], AF.Ln)),
                 inc="ss")
            rr = emit("scalar", [],
                      (lambda e, Tss=Tss: e.activation(srow[:, 384:384 + Tss],
                                                       srow[:, 768:768 + Tss], AF.Exp,
                                                       scale=-0.5)),
                      inc="ss", chain=True)
            bb = emit("vector", W("vector", ("ss", rr)),
                      (lambda e, Tss=Tss: e.scalar_tensor_tensor(
                          srow[:, 1152:1152 + Tss], srow[:, 0:Tss], -1.0,
                          srow[:, 384:384 + Tss], ALU.mult, ALU.mult)),
                      inc="vv", chain=True)
            bc1 = emit("tensor", W("tensor", ("ss", rr)),
                       (lambda e, Tss=Tss: e.matmul(pAb[:, 0:Tss], ones_sb[0:1, :],
                                                    srow[0:1, 384:384 + Tss],
                                                    start=True, stop=True)),
                       inc="tt")
            bc2 = emit("tensor", W("tensor", ("vv", bb)),
                       (lambda e, Tss=Tss: e.matmul(pBb[:, 0:Tss], ones_sb[0:1, :],
                                                    srow[0:1, 1152:1152 + Tss],
                                                    start=True, stop=True)),
                       inc="tt")
            ev = 0
            for dc in range(8):
                emit("vector", W("vector", ("tt", bc1), ("tt", bc2),
                                 *( [out_dma[fi - 1]] if fi >= 1 else [] )),
                     (lambda e, dc=dc, f0=f0, Tss=Tss: e.tensor_tensor(
                         sqf[dc % 2][:, 0:Tss], x1v[:, dc, f0:f0 + Tss],
                         pAb[:, 0:Tss], ALU.mult)),
                     inc="vv")
                ev = emit("vector", [],
                          (lambda e, dc=dc, Tss=Tss: e.tensor_tensor(
                              outsb[:, dc, 0:Tss], sqf[dc % 2][:, 0:Tss],
                              pBb[:, 0:Tss], ALU.add)),
                          inc="vv", chain=True)
            prev_out_evac = ev
            prev_ln2rows_vv = ev
            dm = dmae("gpsimd", W("gpsimd", ("vv", ev)),
                      (lambda e, f0=f0, Tss=Tss: e.dma_start(
                          out=out_d[:, f0:f0 + Tss].rearrange("(c p) t -> p c t", p=128),
                          in_=outsb[:, :, 0:Tss])))
            out_dma.append(dm)

        finals = [(L, cnt[L]) for L in GP_LANES if cnt[L] > 0]
        def _final(e, finals=finals):
            for L, v in finals:
                e.wait_ge(sems[L], v)
        ops["gpsimd"].append(_final)

        @block.sync
        def _(eng):
            for fn in ops["sync"]:
                fn(eng)

        @block.gpsimd
        def _(eng):
            for fn in ops["gpsimd"]:
                fn(eng)

        @block.tensor
        def _(eng):
            for fn in ops["tensor"]:
                fn(eng)

        @block.vector
        def _(eng):
            for fn in ops["vector"]:
                fn(eng)

        @block.scalar
        def _(eng):
            for fn in ops["scalar"]:
                fn(eng)

    return nc


_NC_CACHE = None


def kernel(L0_attn_x, L0_attn_Wq, L0_attn_bq, L0_attn_Wk, L0_attn_bk,
           L0_attn_Wv, L0_attn_bv, L0_attn_Wo, L0_attn_bo,
           L0_ffn_w_gate, L0_ffn_w_up, L0_ffn_w_down):
    global _NC_CACHE
    x = np.ascontiguousarray(np.asarray(L0_attn_x, dtype=np.float32))
    B, L, _ = x.shape

    wqT = np.ascontiguousarray(np.asarray(L0_attn_Wq, np.float32).T)
    wkT = np.ascontiguousarray(np.asarray(L0_attn_Wk, np.float32).T)
    wvT = np.ascontiguousarray(np.asarray(L0_attn_Wv, np.float32).T)
    woT = np.ascontiguousarray(np.asarray(L0_attn_Wo, np.float32).T)
    biasm = np.zeros((128, 24), np.float32)
    biasm[:, 0:8] = np.asarray(L0_attn_bq, np.float32).reshape(8, 128).T
    biasm[:, 8:16] = np.asarray(L0_attn_bk, np.float32).reshape(8, 128).T
    biasm[:, 16:24] = np.asarray(L0_attn_bo, np.float32).reshape(8, 128).T
    bvrow = np.ascontiguousarray(np.asarray(L0_attn_bv, np.float32).reshape(1, D))
    wgTi = np.ascontiguousarray(np.asarray(L0_ffn_w_gate, np.int32).T)
    wuTi = np.ascontiguousarray(np.asarray(L0_ffn_w_up, np.int32).T)
    wdTi = np.ascontiguousarray(np.asarray(L0_ffn_w_down, np.int32).T)

    ii = np.arange(96)
    mLm = (ii[:, None] // 16 == ii[None, :] // 16).astype(np.float32)
    mAm = ((ii[:, None] // 48 == ii[None, :] // 48)
           & (ii[:, None] % 3 == ii[None, :] % 3)).astype(np.float32)
    masks_np = np.ascontiguousarray(np.concatenate([mLm, mAm], axis=1))
    identc = np.zeros((128, 256), np.float32)
    identc[:, 0:128] = np.eye(128, dtype=np.float32)
    identc[:, 128:256] = 1.0

    in_maps = []
    for c in range(8):
        s, h = c // 2, c % 2
        p0 = h * NTOK
        xs = np.zeros((NTOK, D), np.float32)
        n_real = min(L - p0, NTOK)
        xs[:n_real] = x[s, p0:p0 + n_real]
        in_maps.append(dict(xT=np.ascontiguousarray(xs.T), wqT=wqT, wkT=wkT,
                            wvT=wvT, woT=woT, biasm=biasm, bvrow=bvrow,
                            masks=masks_np, identc=identc,
                            wgTi=wgTi, wuTi=wuTi, wdTi=wdTi))

    if _NC_CACHE is None:
        _NC_CACHE = build_program()
    res = run_bass_kernel_spmd(_NC_CACHE, in_maps, core_ids=list(range(8)))

    out = np.empty((B, L, D), np.float32)
    for c in range(8):
        s, h = c // 2, c % 2
        p0 = h * NTOK
        n_real = min(L - p0, NTOK)
        oT = np.asarray(res.results[c]["out"])
        out[s, p0:p0 + n_real] = oT.T[:n_real]
    return out


# version 27
# speedup vs baseline: 1.0223x; 1.0223x over previous
import numpy as np
import contextlib
import concourse.bass as bass
import concourse.mybir as mybir
from concourse.bass_utils import run_bass_kernel_spmd

F32 = mybir.dt.float32
F32R = mybir.dt.float32r
I32 = mybir.dt.int32
AF = mybir.ActivationFunctionType
ALU = mybir.AluOpType
AX = mybir.AxisListType

D = 1024
NTOK = 2064
SLABS = [(0, 384), (384, 384), (768, 384), (1152, 384), (1536, 384), (1920, 144)]
FSS = [(i * 256, 256) for i in range(8)] + [(2048, 16)]
CVELT = 8193.0
S2 = 1.0 / 16129.0
QSCALE = 0.125
EPS = 1e-5
DEBUG_TAPS = False


def tiles_of(T):
    out, b = [], 0
    while b < T:
        tt = 96 if T - b >= 96 else T - b
        out.append((b, tt))
        b += tt
    return out


def build_program():
    nc = bass.Bass("TRN2", target_bir_lowering=False, detect_race_conditions=False,
                   dynamic_dma_scratch_size=16384)

    xT = nc.dram_tensor("xT", [D, NTOK], F32, kind="ExternalInput").ap()
    wqT = nc.dram_tensor("wqT", [D, D], F32, kind="ExternalInput").ap()
    wkT = nc.dram_tensor("wkT", [D, D], F32, kind="ExternalInput").ap()
    wvT = nc.dram_tensor("wvT", [D, D], F32, kind="ExternalInput").ap()
    woT = nc.dram_tensor("woT", [D, D], F32, kind="ExternalInput").ap()
    biasm = nc.dram_tensor("biasm", [128, 24], F32, kind="ExternalInput").ap()
    bvrow = nc.dram_tensor("bvrow", [1, D], F32, kind="ExternalInput").ap()
    masks = nc.dram_tensor("masks", [96, 192], F32, kind="ExternalInput").ap()
    identc = nc.dram_tensor("identc", [128, 256], F32, kind="ExternalInput").ap()
    wgTi = nc.dram_tensor("wgTi", [D, 3072], I32, kind="ExternalInput").ap()
    wuTi = nc.dram_tensor("wuTi", [D, 3072], I32, kind="ExternalInput").ap()
    wdTi = nc.dram_tensor("wdTi", [3072, D], I32, kind="ExternalInput").ap()
    out_d = nc.dram_tensor("out", [D, NTOK], F32, kind="ExternalOutput").ap()
    wg_r = nc.dram_tensor("wg_r", [24, D, 128], F32R, kind="Internal").ap()
    wu_r = nc.dram_tensor("wu_r", [24, D, 128], F32R, kind="Internal").ap()
    wd_r = nc.dram_tensor("wd_r", [3072, D], F32R, kind="Internal").ap()
    wg_rt = nc.dram_tensor("wg_rt", [D, 3072], F32R, kind="Internal").ap()
    wu_rt = nc.dram_tensor("wu_rt", [D, 3072], F32R, kind="Internal").ap()
    ops = {k: [] for k in ("sync", "gpsimd", "tensor", "vector", "scalar")}
    SYNC_LANES = ["q%d" % i for i in range(6)]
    GP_LANES = ["g%d" % i for i in range(3)]
    cnt = {"tt": 0, "vv": 0, "ss": 0}
    for nm_ in SYNC_LANES + GP_LANES + ["gc"]:
        cnt[nm_] = 0
    lane_rr = {"sync": 0, "gpsimd": 0}
    seen = {k: {} for k in ops}
    sems = {}

    def W(eng, *waits):
        w = []
        for name, val in waits:
            if val <= 0 or seen[eng].get(name, -1) >= val:
                continue
            seen[eng][name] = val
            w.append((name, val))
        return w

    def emit(eng, waits, build, inc=None, inc_amt=1, chain=False):
        if chain:
            own = {"vector": "vv", "scalar": "ss", "tensor": "tt",
                   "gpsimd": "gc"}[eng]
            waits = list(waits) + W(eng, (own, cnt[own]))
        def fn(e, waits=waits, build=build, inc=inc, inc_amt=inc_amt):
            for name, val in waits:
                e.wait_ge(sems[name], val)
            ins = build(e)
            if inc is not None:
                ins.then_inc(sems[inc], inc_amt)
        ops[eng].append(fn)
        if inc is not None:
            cnt[inc] += inc_amt
            return cnt[inc]
        return None

    def dmae(eng, waits, build):
        lanes = SYNC_LANES if eng == "sync" else GP_LANES
        L = lanes[lane_rr[eng] % len(lanes)]
        lane_rr[eng] += 1
        waits = list(waits) + W(eng, (L, cnt[L]))
        val = emit(eng, waits, build, inc=L, inc_amt=16)
        return (L, val)

    es = contextlib.ExitStack()
    with es:
        sb = lambda name, shape, dt=F32: es.enter_context(nc.sbuf_tensor(name, shape, dt))
        x1f = sb("x1f", [128, 8 * NTOK])
        qsb = sb("qsb", [128, 8 * 384])
        ksb = sb("ksb", [128, 8 * 384])
        vsb = sb("vsb", [128, 4 * 1024])
        laT = sb("laT", [128, 8 * 384])
        arena_e = sb("arena_e", [128, 3840])
        wbuf = sb("wbuf", [128, 4 * 1024])
        x1hT = sb("x1hT", [128, 2048], F32R)
        x1lT = sb("x1lT", [128, 2048], F32R)
        hhT = sb("hhT", [128, 768], F32R)
        hlT = sb("hlT", [128, 768], F32R)
        wgfT = sb("wgfT", [128, 1024], F32R)
        wufT = sb("wufT", [128, 1024], F32R)
        wdfT = sb("wdfT", [128, 2048], F32R)
        id_sb = sb("id_sb", [128, 128])
        ones_sb = sb("ones_sb", [128, 128])
        mask_sb = sb("mask_sb", [96, 192])
        bias_sb = sb("bias_sb", [128, 32])
        bv_sb = sb("bv_sb", [1, D])
        nm_sb = sb("nm_sb", [96, 2])
        sums_sb = sb("sums_sb", [96, 64])
        srow = sb("srow", [1, 1536])

        x1v = x1f[:].rearrange("p (c t) -> p c t", c=8)
        lav = laT[:].rearrange("p (c t) -> p c t", c=8)
        mL = mask_sb[:, 0:96]
        mA = mask_sb[:, 96:192]

        esb = arena_e[0:96, 0:1536].rearrange("p (s x) -> p s x", s=16)
        eAb = arena_e[0:96, 1536:3072].rearrange("p (s x) -> p s x", s=16)
        psbv = esb
        ptsb8 = arena_e[0:96, 3072:3840].rearrange("p (g x) -> p g x", g=8)
        ytv = arena_e[:, 0:3072].rearrange("p (c t) -> p c t", c=8)
        sqscr = [arena_e[:, 3072 + 384 * i: 3072 + 384 * (i + 1)] for i in range(2)]
        sqf = [qsb[:, 0:256], qsb[:, 256:512]]
        x1h = x1hT[:].rearrange("p (c t) -> p c t", c=8)
        x1l = x1lT[:].rearrange("p (c t) -> p c t", c=8)
        x1h_f = x1hT[:].bitcast(F32).rearrange("p (c t) -> p c t", c=8)

        wslot = [wbuf[:, 1024 * i: 1024 * (i + 1)].rearrange("p (c o) -> p c o", c=8)
                 for i in range(4)]
        wg2s = [wgfT[:].rearrange("p (c j) -> p c j", c=8)]
        wu2s = [wufT[:].rearrange("p (c j) -> p c j", c=8)]
        wd2s = [wdfT[:, 1024 * i:1024 * (i + 1)] for i in range(2)]
        hscr = [qsb[:, 512 + 256 * i: 512 + 256 * (i + 1)] for i in range(3)]
        hful = [qsb[:, 1280 + 256 * i: 1280 + 256 * (i + 1)] for i in range(3)]
        uh = [qsb[:, 2048 + 256 * i: 2048 + 256 * (i + 1)] for i in range(3)]
        hh = [hhT[:, 256 * i: 256 * (i + 1)] for i in range(3)]
        hl = [hlT[:, 256 * i: 256 * (i + 1)] for i in range(3)]
        tscr = vsb[:, 0:2048].rearrange("p (c t) -> p c t", c=8)
        uscr = vsb[:, 2048:4096].rearrange("p (c t) -> p c t", c=8)
        outsb = laT[:, 0:2048].rearrange("p (c t) -> p c t", c=8)

        pBIG = es.enter_context(nc.psum_tensor("pBIG", [128, 2048], F32))
        pB4 = es.enter_context(nc.psum_tensor("pB4", [128, 512], F32))
        pB5 = es.enter_context(nc.psum_tensor("pB5", [128, 512], F32))
        pB67 = es.enter_context(nc.psum_tensor("pB67", [128, 1024], F32))
        pq = [pBIG[:, 0:512], pBIG[:, 512:1024]]
        pv = [pBIG[0:96, 1024:1536], pBIG[0:96, 1536:2048]]
        psv = pBIG[0:96, 0:1024].rearrange("p (s x) -> p s x", s=8)
        SLOT2HEAD = [0, 2, 4, 6, 1, 3, 5, 7, 8, 10, 12, 14, 9, 11, 13, 15]
        po = pq
        pstat1 = pBIG[0:1, 1024:1504]
        pstat2 = pBIG[0:1, 1536:2016]
        ptp = [pB4[:].rearrange("p (i x) -> p i x", i=4),
               pB5[:].rearrange("p (i x) -> p i x", i=4)]
        pfav = pB67[:].rearrange("p (c x) -> p c x", c=8)
        pAb = pB4
        pBb = pB5
        pg = [pBIG[:, 0:512], pBIG[:, 512:1024]]
        pu = [pBIG[:, 1024:1536], pBIG[:, 1536:2048]]

        def pd_slice(dc, Tss):
            if dc < 2:
                return pB4[:, 256 * dc: 256 * dc + Tss]
            if dc < 4:
                return pB5[:, 256 * (dc - 2): 256 * (dc - 2) + Tss]
            return pB67[:, 256 * (dc - 4): 256 * (dc - 4) + Tss]
        pstatA = pB67[0:1, 0:256]
        pstatB = pB67[0:1, 512:768]

        for name in ["tt", "vv", "ss", "gc"] + SYNC_LANES + GP_LANES:
            sems[name] = es.enter_context(nc.semaphore(name))
        block = es.enter_context(nc.Block())

        consts_ld = []
        for dst, src in ((id_sb[:], identc[:, 0:128]), (ones_sb[:], identc[:, 128:256]),
                         (mask_sb[:], masks), (bias_sb[:, 0:24], biasm), (bv_sb[:], bvrow)):
            consts_ld.append(dmae("sync", [],
                                  (lambda e, d=dst, s=src: e.dma_start(out=d, in_=s))))

        bqs_prep = None
        x_lds = []
        for t0, T in ((0, 1032), (1032, 1032)):
            x_lds.append(dmae("sync", [],
                              (lambda e, t0=t0, T=T: e.dma_start(
                                  out=x1v[:, :, t0:t0 + T],
                                  in_=xT[:, t0:t0 + T].rearrange("(c p) t -> p c t", p=128)))))

        bqs_prep = emit("vector", W("vector", *consts_ld),
                        (lambda e: e.tensor_scalar(bias_sb[:, 24:32], bias_sb[:, 0:8],
                                                   QSCALE, None, ALU.mult)),
                        inc="vv")

        conv_toks = []
        for wsrc, wdst in ((wgTi, wg_r), (wuTi, wu_r)):
            for hc in range(24):
                conv_toks.append(dmae("gpsimd", [],
                                      (lambda e, s=wsrc, d=wdst, hc=hc: e.dma_start(
                                          out=d[hc, :, :],
                                          in_=s[:, 128 * hc:128 * (hc + 1)]))))
        for qtr in range(4):
            conv_toks.append(dmae("gpsimd", [],
                                  (lambda e, qtr=qtr: e.dma_start(
                                      out=wd_r[768 * qtr:768 * (qtr + 1), :],
                                      in_=wdTi[768 * qtr:768 * (qtr + 1), :]))))
        for wsrc, wdst in ((wgTi, wg_rt), (wuTi, wu_rt)):
            for qtr in range(4):
                conv_toks.append(dmae("gpsimd", [],
                                      (lambda e, s=wsrc, d=wdst, qtr=qtr:
                                       e.dma_start(
                                           out=d[256 * qtr:256 * (qtr + 1), :],
                                           in_=s[256 * qtr:256 * (qtr + 1), :]))))
        conv_all_done = [(L, cnt[L]) for L in GP_LANES]

        wslot_last_use = [0, 0, 0, 0]
        evac_qk = [0, 0]
        prev_exp_ss = 0
        prev_nmax_vv = 0
        prev_slab_rows_vv = 0
        prev_apply_vv = 0
        prev_pfa_evac_vv = 0
        prev_ptsb_evac = [0, 0, 0, 0]
        av_groupslot = [0, 0]
        prev_tile_trans_tt = 0
        prev_yt_evac = [0, 0]

        def emit_qk_chunk(name, wT, outb, bcol, do_scale, oc, jt0, jT):
            st_ = qkstate
            ldc, slot_j = load_eighth_g(wT, oc, st_["nq"] % 4), st_["nq"] % 4
            st_["nq"] += 1
            bank = oc % 2
            wts = W("tensor", ldc, *x_lds) + W("tensor", ("ss", evac_qk[bank]),
                    ("vv", prev_yt_evac[bank]), ("vv", prev_nmax_vv),
                    ("ss", prev_exp_ss))
            last = None
            for k in range(8):
                last = emit("tensor", wts if k == 0 else [],
                            (lambda e, j=slot_j, k=k, b=bank, T=jT, t0=jt0:
                             e.matmul(pq[b][:, 0:T],
                                      wslot[j][:, k, 0:128],
                                      x1v[:, k, t0:t0 + T],
                                      start=(k == 0), stop=(k == 7))),
                            inc="tt" if k == 7 else None)
            wslot_last_use[slot_j] = last
            ev = emit("scalar", W("scalar", ("tt", last), ("vv", bqs_prep)),
                      (lambda e, oc=oc, b=bank, T=jT, outb=outb, bcol=bcol,
                              sc=do_scale:
                       e.activation(outb[:, 384 * oc:384 * oc + T],
                                    pq[b][:, 0:T], AF.Identity,
                                    bias=bias_sb[:, (24 if sc else bcol) + oc:
                                                 (24 if sc else bcol) + oc + 1],
                                    scale=QSCALE if sc else 1.0)),
                      inc="ss")
            evac_qk[bank] = ev
            qkstate["done"] = ev

        def qk_jobs_for(t0_, T_):
            return ([("q", wqT, qsb, 0, True, oc, t0_, T_) for oc in range(8)] +
                    [("k", wkT, ksb, 8, False, oc, t0_, T_) for oc in range(8)])

        qkstate = {"nq": 0, "done": 0}
        load_eighth_g = None
        pending_qk = qk_jobs_for(*SLABS[0])

        for si, (t0, T) in enumerate(SLABS):
            TILES = tiles_of(T)

            def load_eighth(wT, oc, slot_j):
                return dmae("sync", W("sync", ("tt", wslot_last_use[slot_j])),
                            (lambda e, wT=wT, oc=oc, j=slot_j: e.dma_start(
                                out=wslot[j][:, :, :],
                                in_=wT[:, 128 * oc:128 * (oc + 1)].rearrange(
                                    "(c p) o -> p c o", p=128))))

            load_eighth_g = load_eighth
            for job in pending_qk:
                emit_qk_chunk(*job)
            qk_evac_done = qkstate["done"]
            pending_next = (qk_jobs_for(*SLABS[si + 1]) if si + 1 < len(SLABS)
                            else [])

            def emit_v_tile(ti):
                nonlocal_state = None
                tb, TT = TILES[ti]
                for ot in range(8):
                    ldc, slot_j = load_eighth(wvT, ot, qkstate["nq"] % 4), qkstate["nq"] % 4
                    qkstate["nq"] += 1
                    b = vstate["bank"]
                    wts = W("tensor", ldc,
                            ("ss", pv_rd[b]), ("vv", prev_slab_rows_vv))
                    for k in range(8):
                        emit("tensor", wts if k == 0 else [],
                             (lambda e, j=slot_j, k=k, tb=tb, TT=TT, b=b, t0=t0:
                              e.matmul(pv[b][0:TT, 0:128],
                                       x1v[:, k, t0 + tb:t0 + tb + TT],
                                       wslot[j][:, k, 0:128],
                                       start=(k == 0), stop=False)),
                             inc=None)
                    last = emit("tensor", [],
                                (lambda e, ot=ot, tb=tb, TT=TT, b=b:
                                 e.matmul(pv[b][0:TT, 0:128],
                                          ones_sb[0:1, 0:TT],
                                          bv_sb[0:1, 128 * ot:128 * (ot + 1)],
                                          start=False, stop=True)),
                                inc="tt")
                    wslot_last_use[slot_j] = last
                    ev = emit("scalar", W("scalar", ("tt", last)),
                              (lambda e, ti=ti, ot=ot, TT=TT, b=b:
                               e.copy(vsb[0:TT, 1024 * ti + 128 * ot:1024 * ti + 128 * (ot + 1)],
                                      pv[b][0:TT, 0:128])),
                              inc="ss")
                    pv_rd[b] = ev
                    v_evac_tile[ti] = ev
                    vstate["bank"] ^= 1

            vstate = {"bank": 0}
            v_evac_tile = {}
            pv_rd = [0, 0]
            emit_v_tile(0)

            for ti, (tb, TT) in enumerate(TILES):
                for half in range(2):
                    first_wts = W("tensor", ("ss", qk_evac_done),
                                  ("ss", prev_exp_ss), ("vv", prev_nmax_vv))
                    sc_done = 0
                    order = [0, 4, 1, 5, 2, 6, 3, 7]
                    for oi, sl8 in enumerate(order):
                        s = 8 * half + sl8
                        h = SLOT2HEAD[s]
                        pp = h % 2
                        c = h // 2
                        pos = sl8 % 4
                        sc_done = emit(
                            "tensor", first_wts if oi == 0 else [],
                            (lambda e, sl8=sl8, pp=pp, c=c, tb=tb, TT=TT,
                                    st=(pos == 0), sp=(pos == 3):
                             e.matmul(psv[0:TT, sl8, 0:TT],
                                      qsb[64 * pp:64 * pp + 64, 384 * c + tb:384 * c + tb + TT],
                                      ksb[64 * pp:64 * pp + 64, 384 * c + tb:384 * c + tb + TT],
                                      start=st, stop=sp,
                                      tile_position=(64 * pp, 0))),
                            inc="tt")
                    nm = emit("vector", W("vector", ("tt", sc_done)),
                              (lambda e, TT=TT, half=half: e.tensor_reduce(
                                  nm_sb[0:TT, half:half + 1], psv[0:TT, :, 0:TT],
                                  AX.XY, ALU.max, negate=True)),
                              inc="vv")
                    prev_nmax_vv = nm
                    ex = emit("scalar", W("scalar", ("vv", nm),
                                          ("tt", prev_tile_trans_tt)),
                              (lambda e, TT=TT, half=half: e.activation(
                                  esb[0:TT, 8 * half:8 * half + 8, 0:TT],
                                  psv[0:TT, :, 0:TT], AF.Exp,
                                  bias=nm_sb[0:TT, half:half + 1], scale=1.0)),
                              inc="ss")
                    prev_exp_ss = ex
                if ti + 1 < len(TILES):
                    emit_v_tile(ti + 1)
                emit("vector", W("vector", ("ss", prev_exp_ss)),
                     (lambda e, TT=TT: e.tensor_tensor(
                         eAb[0:TT, :, 0:TT], esb[0:TT, :, 0:TT],
                         mA[0:TT, 0:TT].unsqueeze(1).broadcast_to((TT, 16, TT)),
                         ALU.mult)),
                     inc="vv")
                emit("vector", [],
                     (lambda e, TT=TT: e.tensor_tensor(
                         esb[0:TT, :, 0:TT], esb[0:TT, :, 0:TT],
                         mL[0:TT, 0:TT].unsqueeze(1).broadcast_to((TT, 16, TT)),
                         ALU.mult)),
                     inc="vv")
                emit("vector", [],
                     (lambda e, TT=TT: e.tensor_reduce(
                         sums_sb[0:TT, 0:16], esb[0:TT, :, 0:TT], AX.X, ALU.add)),
                     inc="vv")
                emit("vector", [],
                     (lambda e, TT=TT: e.tensor_reduce(
                         sums_sb[0:TT, 16:32], eAb[0:TT, :, 0:TT], AX.X, ALU.add)),
                     inc="vv")
                emit("vector", [],
                     (lambda e, TT=TT: e.reciprocal(sums_sb[0:TT, 32:48],
                                                    sums_sb[0:TT, 0:16])),
                     inc="vv", chain=True)
                emit("vector", [],
                     (lambda e, TT=TT: e.reciprocal(sums_sb[0:TT, 48:64],
                                                    sums_sb[0:TT, 16:32])),
                     inc="vv", chain=True)
                p_done = {}
                for s in range(16):
                    emit("vector", [],
                         (lambda e, s=s, TT=TT: e.tensor_scalar(
                             psbv[0:TT, s, 0:TT], esb[0:TT, s, 0:TT],
                             sums_sb[0:TT, 32 + s:33 + s], None, ALU.mult)),
                         inc="vv", chain=(s == 0))
                for s in range(16):
                    p_done[s] = emit("vector", [],
                                     (lambda e, s=s, TT=TT: e.scalar_tensor_tensor(
                                         psbv[0:TT, s, 0:TT], eAb[0:TT, s, 0:TT],
                                         sums_sb[0:TT, 48 + s:49 + s],
                                         psbv[0:TT, s, 0:TT], ALU.mult, ALU.add)),
                                     inc="vv")
                av_first_wts = W("tensor", ("vv", prev_pfa_evac_vv))
                trans_last = 0
                started = set()
                last_touch = {}
                for s in range(16):
                    h = SLOT2HEAD[s]
                    last_touch[(6 + (h // 2) // 4, h % 2)] = s
                evac_group = {}
                for g in range(4):
                    par = g % 2
                    gs = g % 2
                    for i in range(4):
                        s = 4 * g + i
                        wts = W("tensor", ("vv", p_done[s]),
                                ("tt", av_groupslot[gs]))
                        trans_last = emit(
                            "tensor", wts,
                            (lambda e, s=s, par=par, i=i, TT=TT: e.transpose(
                                ptp[par][0:96, i, 0:TT], psbv[0:TT, s, 0:96],
                                id_sb[0:TT, 0:TT])),
                            inc="tt")
                    ev = emit("vector", W("vector", ("tt", trans_last)),
                              (lambda e, gs=gs, par=par, TT=TT:
                               e.tensor_copy(ptsb8[0:TT, 4 * gs:4 * gs + 4, 0:TT],
                                             ptp[par][0:TT, :, 0:TT])),
                              inc="vv")
                    prev_ptsb_evac[gs] = ev
                    evac_group[g] = ev
                    av_last = 0
                    for i in range(4):
                        s = 4 * g + i
                        h = SLOT2HEAD[s]
                        pp = h % 2
                        c = h // 2
                        key = (6 + c // 4, pp)
                        startb = key not in started
                        started.add(key)
                        stopb = (last_touch[key] == s)
                        wts = W("tensor", ("vv", ev), ("ss", v_evac_tile[ti])) + (
                            av_first_wts if s == 0 else [])
                        av_last = emit("tensor", wts,
                             (lambda e, s=s, h=h, pp=pp, c=c, ti=ti, TT=TT, gs=gs, i=i,
                                     stb=startb, spb=stopb:
                              e.matmul(pfav[64 * pp:64 * pp + 64, c, 0:TT],
                                       vsb[0:TT, 1024 * ti + 64 * h:1024 * ti + 64 * h + 64],
                                       ptsb8[0:TT, 4 * gs + i, 0:TT],
                                       start=stb, stop=spb,
                                       tile_position=(0, 64 * pp))),
                             inc="tt")
                    av_groupslot[gs] = av_last
                    pass
                prev_tile_trans_tt = trans_last
                av_all = cnt["tt"]
                prev_pfa_evac_vv = emit(
                    "vector", W("vector", ("tt", av_all)),
                    (lambda e, tb=tb, TT=TT: e.tensor_copy(
                        lav[:, :, tb:tb + TT], pfav[:, :, 0:TT])),
                    inc="vv")

            pending_qk = pending_next
            o_first_wts = W("tensor", ("vv", prev_pfa_evac_vv),
                            ("ss", prev_exp_ss), ("vv", prev_nmax_vv))
            yt_done = 0
            for oc in range(8):
                ldc, slot_j = load_eighth(woT, oc, qkstate["nq"] % 4), qkstate["nq"] % 4
                qkstate["nq"] += 1
                bank = oc % 2
                wts = W("tensor", ldc, ("ss", qkstate["done"]),
                        ("vv", prev_yt_evac[bank])) + (o_first_wts if oc == 0 else [])
                last = None
                for k in range(8):
                    last = emit("tensor", wts if k == 0 else [],
                                (lambda e, j=slot_j, k=k, b=bank, T=T:
                                 e.matmul(po[b][:, 0:T],
                                          wslot[j][:, k, 0:128],
                                          lav[:, k, 0:T],
                                          start=(k == 0), stop=(k == 7))),
                                inc="tt" if k == 7 else None)
                wslot_last_use[slot_j] = last
                ev = emit("vector", W("vector", ("tt", last)),
                          (lambda e, oc=oc, b=bank, T=T, t0=t0: e.scalar_tensor_tensor(
                              ytv[:, oc, 0:T], po[b][:, 0:T],
                              bias_sb[:, 16 + oc:17 + oc], x1v[:, oc, t0:t0 + T],
                              ALU.add, ALU.add)),
                          inc="vv")
                prev_yt_evac[bank] = ev
                yt_done = ev
            sq_tt = [0, 0]
            st1 = st2 = 0
            for oc in range(8):
                sl = oc % 2
                sq = emit("scalar", W("scalar", ("vv", prev_yt_evac[oc % 2]),
                                      ("tt", sq_tt[sl])),
                          (lambda e, oc=oc, sl=sl, T=T: e.activation(
                              sqscr[sl][:, 0:T], ytv[:, oc, 0:T], AF.Square)),
                          inc="ss")
                st1 = emit("tensor", W("tensor", ("vv", prev_yt_evac[oc % 2]),
                                       ("vv", prev_slab_rows_vv)),
                           (lambda e, oc=oc, T=T: e.matmul(
                               pstat1[:, 0:T], ones_sb[:, 0:1], ytv[:, oc, 0:T],
                               start=(oc == 0), stop=(oc == 7))),
                           inc="tt")
                st2 = emit("tensor", W("tensor", ("ss", sq)),
                           (lambda e, sl=sl, oc=oc, T=T: e.matmul(
                               pstat2[:, 0:T], ones_sb[:, 0:1], sqscr[sl][:, 0:T],
                               start=(oc == 0), stop=(oc == 7))),
                           inc="tt")
                sq_tt[sl] = st2
            emit("vector", W("vector", ("tt", st1), ("tt", st2)),
                 (lambda e, T=T: e.tensor_scalar(srow[:, 0:T], pstat1[:, 0:T],
                                                 1.0 / 1024.0, None, ALU.mult)),
                 inc="vv")
            emit("vector", [],
                 (lambda e, T=T: e.tensor_scalar(srow[:, 384:384 + T], pstat2[:, 0:T],
                                                 1.0 / 1024.0, None, ALU.mult)),
                 inc="vv")
            emit("vector", [],
                 (lambda e, T=T: e.tensor_tensor(srow[:, 768:768 + T], srow[:, 0:T],
                                                 srow[:, 0:T], ALU.mult)),
                 inc="vv", chain=True)
            emit("vector", [],
                 (lambda e, T=T: e.tensor_tensor(
                     srow[:, 1152:1152 + T], srow[:, 384:384 + T],
                     srow[:, 768:768 + T], ALU.subtract)),
                 inc="vv", chain=True)
            rows_v = emit("vector", [],
                          (lambda e, T=T: e.tensor_scalar(
                              srow[:, 1152:1152 + T], srow[:, 1152:1152 + T],
                              EPS, None, ALU.add)),
                          inc="vv", chain=True)
            emit("scalar", W("scalar", ("vv", rows_v)),
                 (lambda e, T=T: e.activation(srow[:, 768:768 + T],
                                              srow[:, 1152:1152 + T], AF.Ln)),
                 inc="ss")
            r_row = emit("scalar", [],
                         (lambda e, T=T: e.activation(srow[:, 384:384 + T],
                                                      srow[:, 768:768 + T], AF.Exp,
                                                      scale=-0.5)),
                         inc="ss", chain=True)
            b_row = emit("vector", W("vector", ("ss", r_row)),
                         (lambda e, T=T: e.scalar_tensor_tensor(
                             srow[:, 1152:1152 + T], srow[:, 0:T], -1.0,
                             srow[:, 384:384 + T], ALU.mult, ALU.mult)),
                         inc="vv", chain=True)
            bc1 = emit("tensor", W("tensor", ("ss", r_row), ("vv", prev_apply_vv),
                                   ("vv", prev_ptsb_evac[0]), ("vv", prev_ptsb_evac[1]),
                                   ("vv", prev_ptsb_evac[2]), ("vv", prev_ptsb_evac[3])),
                       (lambda e, T=T: e.matmul(pAb[:, 0:T], ones_sb[0:1, :],
                                                srow[0:1, 384:384 + T],
                                                start=True, stop=True)),
                       inc="tt")
            bc2 = emit("tensor", W("tensor", ("vv", b_row)),
                       (lambda e, T=T: e.matmul(pBb[:, 0:T], ones_sb[0:1, :],
                                                srow[0:1, 1152:1152 + T],
                                                start=True, stop=True)),
                       inc="tt")
            ap_v = 0
            for oc in range(8):
                sl = oc % 2
                emit("vector", W("vector", ("tt", bc1), ("tt", bc2)),
                     (lambda e, oc=oc, sl=sl, T=T: e.tensor_tensor(
                         sqscr[sl][:, 0:T], ytv[:, oc, 0:T], pAb[:, 0:T], ALU.mult)),
                     inc="vv")
                ap_v = emit("vector", [],
                            (lambda e, oc=oc, sl=sl, T=T, t0=t0: e.tensor_tensor(
                                x1v[:, oc, t0:t0 + T], sqscr[sl][:, 0:T], pBb[:, 0:T],
                                ALU.add)),
                            inc="vv", chain=True)
            prev_apply_vv = ap_v
            prev_slab_rows_vv = rows_v

        slab_done_vv = prev_apply_vv
        o_last_tt = cnt["tt"]
        wf_lduse = {"g": [o_last_tt, o_last_tt], "u": [o_last_tt, o_last_tt],
                    "d": [o_last_tt, o_last_tt]}
        prev_y2_evac = 0
        prev_ln2rows_vv = 0
        hbuf_lastmm = [o_last_tt, o_last_tt, o_last_tt]
        pgu_read = [[0, 0], [0, 0]]
        out_dma = []
        prev_out_evac = 0
        def emit_split(fi):
            f0, Tss = FSS[fi]
            if Tss < 256:
                return cnt["vv"]
            emit("vector", [],
                 (lambda e, f0=f0, Tss=Tss: e.tensor_scalar(
                     tscr[:, :, 0:Tss], x1v[:, :, f0:f0 + Tss], CVELT, None, ALU.mult)),
                 inc="vv")
            emit("vector", [],
                 (lambda e, f0=f0, Tss=Tss: e.scalar_tensor_tensor(
                     uscr[:, :, 0:Tss], tscr[:, :, 0:Tss], 1.0,
                     x1v[:, :, f0:f0 + Tss], ALU.mult, ALU.subtract)),
                 inc="vv", chain=True)
            emit("vector", [],
                 (lambda e, Tss=Tss: e.scalar_tensor_tensor(
                     x1h[:, :, 0:Tss], tscr[:, :, 0:Tss], 1.0,
                     uscr[:, :, 0:Tss], ALU.mult, ALU.subtract)),
                 inc="vv", chain=True)
            return emit("vector", [],
                        (lambda e, f0=f0, Tss=Tss: e.tensor_tensor(
                            x1l[:, :, 0:Tss], x1v[:, :, f0:f0 + Tss],
                            x1h_f[:, :, 0:Tss], ALU.subtract)),
                        inc="vv", chain=True)

        def emit_tail_ffn():
            tf_vv = cnt["vv"]
            tf_ss = cnt["ss"]
            gbuf = [wgfT[:, 0:512], wgfT[:, 512:1024]]
            ubuf = [wufT[:, 0:512], wufT[:, 512:1024]]
            g_last = {0: wf_lduse["g"][0], 1: wf_lduse["g"][0]}
            u_last = {0: wf_lduse["u"][0], 1: wf_lduse["u"][0]}
            ldd_t = {0: wf_lduse["d"][0], 1: wf_lduse["d"][1]}
            g_sil = qsb[:, 2048:2560]
            hp_t = qsb[:, 1280:1792]
            hT_sb = qsb[:, 512:896]
            y_sc = qsb[:, 512:1536]
            ptr = [pB67[:, 256:272], pB67[:, 768:784]]
            silu_rd = [0, 0]
            mult_rd = [0, 0]
            tr_evac = {}
            mus = {}
            st_t = {"dn": 0, "first": True}

            def emit_td(c):
                for q in range(4):
                    hc = 4 * c + q
                    sp_ = hc % 2
                    trl = emit("tensor",
                               W("tensor", ("vv", mus[c]),
                                 ("vv", tr_evac.get(hc - 2, 0))),
                               (lambda e, q=q, sp_=sp_: e.transpose(
                                   ptr[sp_][:, 0:16],
                                   hp_t[0:16, 128 * q:128 * (q + 1)],
                                   id_sb[0:16, 0:16])),
                               inc="tt")
                    ev = emit("vector", W("vector", ("tt", trl)),
                              (lambda e, hc=hc, sp_=sp_: e.tensor_copy(
                                  hT_sb[:, 16 * hc:16 * hc + 16],
                                  ptr[sp_][:, 0:16])),
                              inc="vv")
                    tr_evac[hc] = ev
                    jd = hc % 2
                    ldd = dmae("gpsimd",
                               W("gpsimd", ("tt", ldd_t[jd]), *conv_all_done),
                               (lambda e, hc=hc, jd=jd: e.dma_start(
                                   out=wd2s[jd][:, :],
                                   in_=wd_r[128 * hc:128 * (hc + 1), :])))
                    dnl = None
                    for h in range(2):
                        dnl = emit("tensor",
                                   (W("tensor", ldd, ("vv", ev))
                                    if h == 0 else []),
                                   (lambda e, hc=hc, jd=jd, h=h: e.matmul(
                                       (pB4 if h == 0 else pB5)[0:16, 0:512],
                                       hT_sb[:, 16 * hc:16 * hc + 16],
                                       wd2s[jd][:, 512 * h:512 * (h + 1)]
                                       .bitcast(F32),
                                       start=(hc == 0), stop=(hc == 23))),
                                   inc="tt")
                    ldd_t[jd] = dnl
                    st_t["dn"] = dnl

            for c in range(6):
                par = c % 2
                gl = ul = None
                for k in range(8):
                    kp = k % 2
                    ldg = dmae("sync", W("sync", ("tt", g_last[kp])),
                               (lambda e, c=c, k=k, kp=kp: e.dma_start(
                                   out=gbuf[kp],
                                   in_=wg_rt[128 * k:128 * (k + 1),
                                             512 * c:512 * (c + 1)])))
                    wts = W("tensor", ldg)
                    if k == 0:
                        wts += W("tensor", ("ss", silu_rd[par]))
                        if st_t["first"]:
                            wts += W("tensor", ("vv", tf_vv), ("ss", tf_ss))
                            st_t["first"] = False
                    gl = emit("tensor", wts,
                              (lambda e, k=k, kp=kp, par=par: e.matmul(
                                  pg[par][0:16, 0:512],
                                  x1v[:, k, 2048:2064],
                                  gbuf[kp].bitcast(F32),
                                  start=(k == 0), stop=(k == 7))),
                              inc="tt")
                    g_last[kp] = gl
                for k in range(8):
                    kp = k % 2
                    ldu = dmae("sync", W("sync", ("tt", u_last[kp])),
                               (lambda e, c=c, k=k, kp=kp: e.dma_start(
                                   out=ubuf[kp],
                                   in_=wu_rt[128 * k:128 * (k + 1),
                                             512 * c:512 * (c + 1)])))
                    wts = W("tensor", ldu)
                    if k == 0:
                        wts += W("tensor", ("vv", mult_rd[par]))
                    ul = emit("tensor", wts,
                              (lambda e, k=k, kp=kp, par=par: e.matmul(
                                  pu[par][0:16, 0:512],
                                  x1v[:, k, 2048:2064],
                                  ubuf[kp].bitcast(F32),
                                  start=(k == 0), stop=(k == 7))),
                              inc="tt")
                    u_last[kp] = ul
                if c >= 1:
                    emit_td(c - 1)
                sl = emit("scalar", W("scalar", ("tt", gl), ("vv", tf_vv)),
                          (lambda e, par=par: e.activation(
                              g_sil[0:16, 0:512], pg[par][0:16, 0:512],
                              AF.Silu, scale=1.0 / 127.0)),
                          inc="ss")
                silu_rd[par] = sl
                mu = emit("vector", W("vector", ("ss", sl), ("tt", ul)),
                          (lambda e, par=par: e.tensor_tensor(
                              hp_t[0:16, 0:512], g_sil[0:16, 0:512],
                              pu[par][0:16, 0:512], ALU.mult)),
                          inc="vv")
                mult_rd[par] = mu
                mus[c] = mu
            emit_td(5)
            ye = emit("vector", W("vector", ("tt", st_t["dn"])),
                      (lambda e: e.tensor_copy(y_sc[0:16, 0:512],
                                               pB4[0:16, 0:512])),
                      inc="vv")
            ye = emit("vector", [],
                      (lambda e: e.tensor_copy(y_sc[0:16, 512:1024],
                                               pB5[0:16, 0:512])),
                      inc="vv", chain=True)
            y2_ev = 0
            ytr = {}
            for dc in range(8):
                sp_ = dc % 2
                trl = emit("tensor",
                           W("tensor", ("vv", ye), ("vv", ytr.get(dc - 2, 0))),
                           (lambda e, dc=dc, sp_=sp_: e.transpose(
                               ptr[sp_][:, 0:16],
                               y_sc[0:16, 128 * dc:128 * (dc + 1)],
                               id_sb[0:16, 0:16])),
                           inc="tt")
                y2_ev = emit("vector", W("vector", ("tt", trl)),
                             (lambda e, dc=dc, sp_=sp_: e.scalar_tensor_tensor(
                                 x1v[:, dc, 2048:2064], ptr[sp_][:, 0:16], S2,
                                 x1v[:, dc, 2048:2064], ALU.mult, ALU.add)),
                             inc="vv", chain=True)
                ytr[dc] = y2_ev
            return y2_ev

        split_done_fi = {0: emit_split(0)}
        for fi, (f0, Tss) in enumerate(FSS):
            tail = (Tss < 256)
            split_done = split_done_fi[fi]
            dn_last = 0
            down_args = [None]
            def emit_down(hc_d, j_d, hsp_d, ldd_d, hc_w, tail, Tss):
                wts = W("tensor", ldd_d, ("vv", hsp_d), ("vv", slab_done_vv),
                        ("vv", prev_y2_evac), ("vv", prev_ln2rows_vv))
                hops = [hful[j_d]] if tail else [hh[j_d], hl[j_d]]
                nh = len(hops)
                dl = None
                for dc in range(8):
                    for pi, hop in enumerate(hops):
                        st = (hc_d == 0 and pi == 0 and dc % 2 == 0)
                        sp = (hc_d == 23 and pi == nh - 1 and dc % 2 == 1)
                        dl = emit(
                            "tensor", wts if (dc == 0 and pi == 0) else [],
                            (lambda e, hc_w=hc_w, dc=dc, hop=hop, Tss=Tss, st=st, sp=sp,
                                    tl=tail:
                             e.matmul(pd_slice(dc, Tss),
                                      wd2s[hc_w % 2][:, 128 * dc:128 * (dc + 1)].bitcast(F32)
                                      if tl else wd2s[hc_w % 2][:, 128 * dc:128 * (dc + 1)],
                                      hop[:, 0:Tss], start=st, stop=sp)),
                            inc="tt" if (dc == 7 and pi == nh - 1) else None)
                return dl

            y2_ev = emit_tail_ffn() if tail else 0
            pending = []
            for hc in ([] if tail else range(24)):
                j = hc % 3
                jd = hc % 2
                lg = dmae("sync", W("sync", ("tt", wf_lduse["g"][0]),
                                    *conv_all_done),
                          (lambda e, hc=hc: e.dma_start(
                              out=wg2s[0][:, :, :],
                              in_=wg_r[hc, :, :].rearrange("(c p) j -> p c j", p=128))))
                lu = dmae("sync", W("sync", ("tt", wf_lduse["u"][0])),
                          (lambda e, hc=hc: e.dma_start(
                              out=wu2s[0][:, :, :],
                              in_=wu_r[hc, :, :].rearrange("(c p) j -> p c j", p=128))))
                bank = hc % 2
                if tail:
                    xops = [None]
                else:
                    xops = [x1h, x1l]
                npass = len(xops)
                wts = W("tensor", lg, ("vv", split_done),
                        ("ss", prev_exp_ss), ("vv", prev_nmax_vv),
                        ("vv", prev_yt_evac[0]), ("vv", prev_yt_evac[1]),
                        ("ss", pgu_read[0][bank]))
                if tail:
                    wts += W("tensor", ("vv", slab_done_vv))
                gl = None
                for k in range(8):
                    for pi, xop in enumerate(xops):
                        first = (k == 0 and pi == 0)
                        lastm = (k == 7 and pi == npass - 1)
                        gl = emit("tensor", wts if first else [],
                                  (lambda e, j=j, k=k, xop=xop, b=bank, Tss=Tss, f0=f0,
                                          st=first, sp=lastm:
                                   e.matmul(pg[b][:, 0:Tss],
                                            wg2s[0][:, k, :].bitcast(F32) if xop is None
                                            else wg2s[0][:, k, :],
                                            x1v[:, k, f0:f0 + Tss] if xop is None
                                            else xop[:, k, 0:Tss],
                                            start=st, stop=sp)),
                                  inc="tt" if lastm else None)
                wf_lduse["g"][0] = gl
                wts = W("tensor", lu, ("vv", pgu_read[1][bank]))
                ul = None
                for k in range(8):
                    for pi, xop in enumerate(xops):
                        first = (k == 0 and pi == 0)
                        lastm = (k == 7 and pi == npass - 1)
                        ul = emit("tensor", wts if first else [],
                                  (lambda e, j=j, k=k, xop=xop, b=bank, Tss=Tss, f0=f0,
                                          st=first, sp=lastm:
                                   e.matmul(pu[b][:, 0:Tss],
                                            wu2s[0][:, k, :].bitcast(F32) if xop is None
                                            else wu2s[0][:, k, :],
                                            x1v[:, k, f0:f0 + Tss] if xop is None
                                            else xop[:, k, 0:Tss],
                                            start=st, stop=sp)),
                                  inc="tt" if lastm else None)
                wf_lduse["u"][0] = ul
                if len(pending) >= 2:
                    pd_args = pending.pop(0)
                    dn_last = emit_down(*pd_args, tail, Tss)
                    wf_lduse["d"][pd_args[4] % 2] = dn_last
                    hbuf_lastmm[pd_args[1]] = dn_last
                ldd = dmae("gpsimd", W("gpsimd", ("tt", wf_lduse["d"][jd]),
                                       *conv_all_done),
                           (lambda e, hc=hc, jd=jd: e.dma_start(
                               out=wd2s[jd][:, :],
                               in_=wd_r[128 * hc:128 * (hc + 1), :])))
                sl = emit("scalar", W("scalar", ("tt", gl), ("tt", hbuf_lastmm[j])),
                          (lambda e, j=j, b=bank, Tss=Tss: e.activation(
                              hscr[j][:, 0:Tss], pg[b][:, 0:Tss], AF.Silu,
                              scale=1.0 / 127.0)),
                          inc="ss")
                pgu_read[0][bank] = sl
                mu = emit("vector", W("vector", ("ss", sl), ("tt", ul),
                                      ("tt", hbuf_lastmm[j])),
                          (lambda e, j=j, b=bank, Tss=Tss: e.tensor_tensor(
                              hful[j][:, 0:Tss], hscr[j][:, 0:Tss],
                              pu[b][:, 0:Tss], ALU.mult)),
                          inc="vv")
                pgu_read[1][bank] = mu
                if tail:
                    hsp = mu
                else:
                    emit("vector", [],
                         (lambda e, j=j, Tss=Tss: e.tensor_scalar(
                             hscr[j][:, 0:Tss], hful[j][:, 0:Tss], CVELT, None, ALU.mult)),
                         inc="vv", chain=True)
                    emit("vector", [],
                         (lambda e, j=j, Tss=Tss: e.scalar_tensor_tensor(
                             uh[j][:, 0:Tss], hscr[j][:, 0:Tss], 1.0,
                             hful[j][:, 0:Tss], ALU.mult, ALU.subtract)),
                         inc="vv", chain=True)
                    emit("vector", [],
                         (lambda e, j=j, Tss=Tss: e.scalar_tensor_tensor(
                             hh[j][:, 0:Tss], hscr[j][:, 0:Tss], 1.0,
                             uh[j][:, 0:Tss], ALU.mult, ALU.subtract)),
                         inc="vv", chain=True)
                    hsp = emit("vector", [],
                               (lambda e, j=j, Tss=Tss: e.tensor_tensor(
                                   hl[j][:, 0:Tss], hful[j][:, 0:Tss],
                                   hh[j][:, 0:Tss].bitcast(F32), ALU.subtract)),
                               inc="vv", chain=True)
                pending.append((hc, j, hsp, ldd, hc))
            while pending:
                pd_args = pending.pop(0)
                dn_last = emit_down(*pd_args, tail, Tss)
                wf_lduse["d"][pd_args[4] % 2] = dn_last
                hbuf_lastmm[pd_args[1]] = dn_last
            if fi + 1 < len(FSS):
                emit("vector", W("vector", ("tt", wf_lduse["u"][0])), 
                     (lambda e: e.engine_nop()), inc="vv")
                split_done_fi[fi + 1] = emit_split(fi + 1)
            for dc in ([] if tail else range(8)):
                y2_ev = emit("vector", W("vector", ("tt", dn_last)),
                             (lambda e, dc=dc, f0=f0, Tss=Tss: e.scalar_tensor_tensor(
                                 x1v[:, dc, f0:f0 + Tss], pd_slice(dc, Tss), S2,
                                 x1v[:, dc, f0:f0 + Tss], ALU.mult, ALU.add)),
                             inc="vv")
            prev_y2_evac = y2_ev
            sq_tt2 = [0, 0]
            s1 = s2 = 0
            for dc in range(8):
                sl2 = dc % 2
                sq = emit("scalar", W("scalar", ("vv", y2_ev), ("tt", sq_tt2[sl2])),
                          (lambda e, dc=dc, sl2=sl2, f0=f0, Tss=Tss: e.activation(
                              sqf[sl2][:, 0:Tss], x1v[:, dc, f0:f0 + Tss], AF.Square)),
                          inc="ss")
                s1 = emit("tensor", W("tensor", ("vv", y2_ev)),
                          (lambda e, dc=dc, f0=f0, Tss=Tss: e.matmul(
                              pstatA[:, 0:Tss], ones_sb[:, 0:1],
                              x1v[:, dc, f0:f0 + Tss],
                              start=(dc == 0), stop=(dc == 7))),
                          inc="tt")
                s2 = emit("tensor", W("tensor", ("ss", sq)),
                          (lambda e, dc=dc, sl2=sl2, Tss=Tss: e.matmul(
                              pstatB[:, 0:Tss], ones_sb[:, 0:1], sqf[sl2][:, 0:Tss],
                              start=(dc == 0), stop=(dc == 7))),
                          inc="tt")
                sq_tt2[sl2] = s2
            emit("vector", W("vector", ("tt", s1), ("tt", s2)),
                 (lambda e, Tss=Tss: e.tensor_scalar(
                     srow[:, 0:Tss], pstatA[:, 0:Tss], 1.0 / 1024.0, None, ALU.mult)),
                 inc="vv")
            emit("vector", [],
                 (lambda e, Tss=Tss: e.tensor_scalar(
                     srow[:, 384:384 + Tss], pstatB[:, 0:Tss], 1.0 / 1024.0, None,
                     ALU.mult)),
                 inc="vv")
            emit("vector", [],
                 (lambda e, Tss=Tss: e.tensor_tensor(
                     srow[:, 768:768 + Tss], srow[:, 0:Tss], srow[:, 0:Tss],
                     ALU.mult)),
                 inc="vv", chain=True)
            emit("vector", [],
                 (lambda e, Tss=Tss: e.tensor_tensor(
                     srow[:, 1152:1152 + Tss], srow[:, 384:384 + Tss],
                     srow[:, 768:768 + Tss], ALU.subtract)),
                 inc="vv", chain=True)
            rows2_v = emit("vector", [],
                           (lambda e, Tss=Tss: e.tensor_scalar(
                               srow[:, 1152:1152 + Tss], srow[:, 1152:1152 + Tss],
                               EPS, None, ALU.add)),
                           inc="vv", chain=True)
            _taps = []
            emit("scalar", W("scalar", ("vv", rows2_v), *_taps),
                 (lambda e, Tss=Tss: e.activation(srow[:, 768:768 + Tss],
                                                  srow[:, 1152:1152 + Tss], AF.Ln)),
                 inc="ss")
            rr = emit("scalar", [],
                      (lambda e, Tss=Tss: e.activation(srow[:, 384:384 + Tss],
                                                       srow[:, 768:768 + Tss], AF.Exp,
                                                       scale=-0.5)),
                      inc="ss", chain=True)
            bb = emit("vector", W("vector", ("ss", rr)),
                      (lambda e, Tss=Tss: e.scalar_tensor_tensor(
                          srow[:, 1152:1152 + Tss], srow[:, 0:Tss], -1.0,
                          srow[:, 384:384 + Tss], ALU.mult, ALU.mult)),
                      inc="vv", chain=True)
            bc1 = emit("tensor", W("tensor", ("ss", rr)),
                       (lambda e, Tss=Tss: e.matmul(pAb[:, 0:Tss], ones_sb[0:1, :],
                                                    srow[0:1, 384:384 + Tss],
                                                    start=True, stop=True)),
                       inc="tt")
            bc2 = emit("tensor", W("tensor", ("vv", bb)),
                       (lambda e, Tss=Tss: e.matmul(pBb[:, 0:Tss], ones_sb[0:1, :],
                                                    srow[0:1, 1152:1152 + Tss],
                                                    start=True, stop=True)),
                       inc="tt")
            ev = 0
            for dc in range(8):
                emit("vector", W("vector", ("tt", bc1), ("tt", bc2),
                                 *( [out_dma[fi - 1]] if fi >= 1 else [] )),
                     (lambda e, dc=dc, f0=f0, Tss=Tss: e.tensor_tensor(
                         sqf[dc % 2][:, 0:Tss], x1v[:, dc, f0:f0 + Tss],
                         pAb[:, 0:Tss], ALU.mult)),
                     inc="vv")
                ev = emit("vector", [],
                          (lambda e, dc=dc, Tss=Tss: e.tensor_tensor(
                              outsb[:, dc, 0:Tss], sqf[dc % 2][:, 0:Tss],
                              pBb[:, 0:Tss], ALU.add)),
                          inc="vv", chain=True)
            prev_out_evac = ev
            prev_ln2rows_vv = ev
            dm = dmae("gpsimd", W("gpsimd", ("vv", ev)),
                      (lambda e, f0=f0, Tss=Tss: e.dma_start(
                          out=out_d[:, f0:f0 + Tss].rearrange("(c p) t -> p c t", p=128),
                          in_=outsb[:, :, 0:Tss])))
            out_dma.append(dm)

        finals = [(L, cnt[L]) for L in GP_LANES if cnt[L] > 0]
        def _final(e, finals=finals):
            for L, v in finals:
                e.wait_ge(sems[L], v)
        ops["gpsimd"].append(_final)

        @block.sync
        def _(eng):
            for fn in ops["sync"]:
                fn(eng)

        @block.gpsimd
        def _(eng):
            for fn in ops["gpsimd"]:
                fn(eng)

        @block.tensor
        def _(eng):
            for fn in ops["tensor"]:
                fn(eng)

        @block.vector
        def _(eng):
            for fn in ops["vector"]:
                fn(eng)

        @block.scalar
        def _(eng):
            for fn in ops["scalar"]:
                fn(eng)

    return nc


_NC_CACHE = None


def kernel(L0_attn_x, L0_attn_Wq, L0_attn_bq, L0_attn_Wk, L0_attn_bk,
           L0_attn_Wv, L0_attn_bv, L0_attn_Wo, L0_attn_bo,
           L0_ffn_w_gate, L0_ffn_w_up, L0_ffn_w_down):
    global _NC_CACHE
    x = np.ascontiguousarray(np.asarray(L0_attn_x, dtype=np.float32))
    B, L, _ = x.shape

    wqT = np.ascontiguousarray(np.asarray(L0_attn_Wq, np.float32).T)
    wkT = np.ascontiguousarray(np.asarray(L0_attn_Wk, np.float32).T)
    wvT = np.ascontiguousarray(np.asarray(L0_attn_Wv, np.float32).T)
    woT = np.ascontiguousarray(np.asarray(L0_attn_Wo, np.float32).T)
    biasm = np.zeros((128, 24), np.float32)
    biasm[:, 0:8] = np.asarray(L0_attn_bq, np.float32).reshape(8, 128).T
    biasm[:, 8:16] = np.asarray(L0_attn_bk, np.float32).reshape(8, 128).T
    biasm[:, 16:24] = np.asarray(L0_attn_bo, np.float32).reshape(8, 128).T
    bvrow = np.ascontiguousarray(np.asarray(L0_attn_bv, np.float32).reshape(1, D))
    wgTi = np.ascontiguousarray(np.asarray(L0_ffn_w_gate, np.int32).T)
    wuTi = np.ascontiguousarray(np.asarray(L0_ffn_w_up, np.int32).T)
    wdTi = np.ascontiguousarray(np.asarray(L0_ffn_w_down, np.int32).T)

    ii = np.arange(96)
    mLm = (ii[:, None] // 16 == ii[None, :] // 16).astype(np.float32)
    mAm = ((ii[:, None] // 48 == ii[None, :] // 48)
           & (ii[:, None] % 3 == ii[None, :] % 3)).astype(np.float32)
    masks_np = np.ascontiguousarray(np.concatenate([mLm, mAm], axis=1))
    identc = np.zeros((128, 256), np.float32)
    identc[:, 0:128] = np.eye(128, dtype=np.float32)
    identc[:, 128:256] = 1.0

    in_maps = []
    for c in range(8):
        s, h = c // 2, c % 2
        p0 = h * NTOK
        xs = np.zeros((NTOK, D), np.float32)
        n_real = min(L - p0, NTOK)
        xs[:n_real] = x[s, p0:p0 + n_real]
        in_maps.append(dict(xT=np.ascontiguousarray(xs.T), wqT=wqT, wkT=wkT,
                            wvT=wvT, woT=woT, biasm=biasm, bvrow=bvrow,
                            masks=masks_np, identc=identc,
                            wgTi=wgTi, wuTi=wuTi, wdTi=wdTi))

    if _NC_CACHE is None:
        _NC_CACHE = build_program()
    res = run_bass_kernel_spmd(_NC_CACHE, in_maps, core_ids=list(range(8)))

    out = np.empty((B, L, D), np.float32)
    for c in range(8):
        s, h = c // 2, c % 2
        p0 = h * NTOK
        n_real = min(L - p0, NTOK)
        oT = np.asarray(res.results[c]["out"])
        out[s, p0:p0 + n_real] = oT.T[:n_real]
    return out


# version 28
# speedup vs baseline: 1.0401x; 1.0174x over previous
import numpy as np
import contextlib
import concourse.bass as bass
import concourse.mybir as mybir
from concourse.bass_utils import run_bass_kernel_spmd

F32 = mybir.dt.float32
F32R = mybir.dt.float32r
I32 = mybir.dt.int32
AF = mybir.ActivationFunctionType
ALU = mybir.AluOpType
AX = mybir.AxisListType

D = 1024
NTOK = 2064
SLABS = [(0, 384), (384, 384), (768, 384), (1152, 384), (1536, 384), (1920, 144)]
FSS = [(i * 256, 256) for i in range(8)] + [(2048, 16)]
CVELT = 8193.0
S2 = 1.0 / 16129.0
QSCALE = 0.125
EPS = 1e-5
DEBUG_TAPS = False


def tiles_of(T):
    out, b = [], 0
    while b < T:
        tt = 96 if T - b >= 96 else T - b
        out.append((b, tt))
        b += tt
    return out


def build_program():
    nc = bass.Bass("TRN2", target_bir_lowering=False, detect_race_conditions=False,
                   dynamic_dma_scratch_size=16384)

    xT = nc.dram_tensor("xT", [D, NTOK], F32, kind="ExternalInput").ap()
    wqT = nc.dram_tensor("wqT", [D, D], F32, kind="ExternalInput").ap()
    wkT = nc.dram_tensor("wkT", [D, D], F32, kind="ExternalInput").ap()
    wvT = nc.dram_tensor("wvT", [D, D], F32, kind="ExternalInput").ap()
    woT = nc.dram_tensor("woT", [D, D], F32, kind="ExternalInput").ap()
    biasm = nc.dram_tensor("biasm", [128, 24], F32, kind="ExternalInput").ap()
    bvrow = nc.dram_tensor("bvrow", [1, D], F32, kind="ExternalInput").ap()
    masks = nc.dram_tensor("masks", [96, 192], F32, kind="ExternalInput").ap()
    identc = nc.dram_tensor("identc", [128, 256], F32, kind="ExternalInput").ap()
    wgTi = nc.dram_tensor("wgTi", [D, 3072], I32, kind="ExternalInput").ap()
    wuTi = nc.dram_tensor("wuTi", [D, 3072], I32, kind="ExternalInput").ap()
    wdTi = nc.dram_tensor("wdTi", [3072, D], I32, kind="ExternalInput").ap()
    out_d = nc.dram_tensor("out", [D, NTOK], F32, kind="ExternalOutput").ap()
    wg_r = nc.dram_tensor("wg_r", [24, D, 128], F32R, kind="Internal").ap()
    wu_r = nc.dram_tensor("wu_r", [24, D, 128], F32R, kind="Internal").ap()
    wd_r = nc.dram_tensor("wd_r", [3072, D], F32R, kind="Internal").ap()
    wg_rt = nc.dram_tensor("wg_rt", [D, 3072], F32R, kind="Internal").ap()
    wu_rt = nc.dram_tensor("wu_rt", [D, 3072], F32R, kind="Internal").ap()
    ops = {k: [] for k in ("sync", "gpsimd", "tensor", "vector", "scalar")}
    SYNC_LANES = ["q%d" % i for i in range(6)]
    GP_LANES = ["g%d" % i for i in range(3)]
    cnt = {"tt": 0, "vv": 0, "ss": 0}
    for nm_ in SYNC_LANES + GP_LANES + ["gc"]:
        cnt[nm_] = 0
    lane_rr = {"sync": 0, "gpsimd": 0}
    seen = {k: {} for k in ops}
    sems = {}

    def W(eng, *waits):
        w = []
        for name, val in waits:
            if val <= 0 or seen[eng].get(name, -1) >= val:
                continue
            seen[eng][name] = val
            w.append((name, val))
        return w

    def emit(eng, waits, build, inc=None, inc_amt=1, chain=False):
        if chain:
            own = {"vector": "vv", "scalar": "ss", "tensor": "tt",
                   "gpsimd": "gc"}[eng]
            waits = list(waits) + W(eng, (own, cnt[own]))
        def fn(e, waits=waits, build=build, inc=inc, inc_amt=inc_amt):
            for name, val in waits:
                e.wait_ge(sems[name], val)
            ins = build(e)
            if inc is not None:
                ins.then_inc(sems[inc], inc_amt)
        ops[eng].append(fn)
        if inc is not None:
            cnt[inc] += inc_amt
            return cnt[inc]
        return None

    def dmae(eng, waits, build):
        lanes = SYNC_LANES if eng == "sync" else GP_LANES
        L = lanes[lane_rr[eng] % len(lanes)]
        lane_rr[eng] += 1
        waits = list(waits) + W(eng, (L, cnt[L]))
        val = emit(eng, waits, build, inc=L, inc_amt=16)
        return (L, val)

    es = contextlib.ExitStack()
    with es:
        sb = lambda name, shape, dt=F32: es.enter_context(nc.sbuf_tensor(name, shape, dt))
        x1f = sb("x1f", [128, 8 * NTOK])
        qsb = sb("qsb", [128, 8 * 384])
        ksb = sb("ksb", [128, 8 * 384])
        vsb = sb("vsb", [128, 4 * 1024])
        laT = sb("laT", [128, 8 * 384])
        arena_e = sb("arena_e", [128, 3840])
        wbuf = sb("wbuf", [128, 4 * 1024])
        x1hT = sb("x1hT", [128, 2048], F32R)
        x1lT = sb("x1lT", [128, 2048], F32R)
        hhT = sb("hhT", [128, 768], F32R)
        hlT = sb("hlT", [128, 768], F32R)
        wgfT = sb("wgfT", [128, 1024], F32R)
        wufT = sb("wufT", [128, 1024], F32R)
        wtail = sb("wtail", [128, 1024], F32R)
        hTh = sb("hTh", [128, 384], F32R)
        hTl = sb("hTl", [128, 384], F32R)
        wdfT = sb("wdfT", [128, 2048], F32R)
        id_sb = sb("id_sb", [128, 128])
        ones_sb = sb("ones_sb", [128, 128])
        mask_sb = sb("mask_sb", [96, 192])
        bias_sb = sb("bias_sb", [128, 32])
        bv_sb = sb("bv_sb", [1, D])
        nm_sb = sb("nm_sb", [96, 2])
        sums_sb = sb("sums_sb", [96, 64])
        srow = sb("srow", [1, 1536])

        x1v = x1f[:].rearrange("p (c t) -> p c t", c=8)
        lav = laT[:].rearrange("p (c t) -> p c t", c=8)
        mL = mask_sb[:, 0:96]
        mA = mask_sb[:, 96:192]

        esb = arena_e[0:96, 0:1536].rearrange("p (s x) -> p s x", s=16)
        eAb = arena_e[0:96, 1536:3072].rearrange("p (s x) -> p s x", s=16)
        psbv = esb
        ptsb8 = arena_e[0:96, 3072:3840].rearrange("p (g x) -> p g x", g=8)
        ytv = arena_e[:, 0:3072].rearrange("p (c t) -> p c t", c=8)
        sqscr = [arena_e[:, 3072 + 384 * i: 3072 + 384 * (i + 1)] for i in range(2)]
        sqf = [qsb[:, 0:256], qsb[:, 256:512]]
        x1h = x1hT[:].rearrange("p (c t) -> p c t", c=8)
        x1l = x1lT[:].rearrange("p (c t) -> p c t", c=8)
        x1h_f = x1hT[:].bitcast(F32).rearrange("p (c t) -> p c t", c=8)

        wslot = [wbuf[:, 1024 * i: 1024 * (i + 1)].rearrange("p (c o) -> p c o", c=8)
                 for i in range(4)]
        wg2s = [wgfT[:].rearrange("p (c j) -> p c j", c=8)]
        wu2s = [wufT[:].rearrange("p (c j) -> p c j", c=8)]
        wd2s = [wdfT[:, 1024 * i:1024 * (i + 1)] for i in range(2)]
        hscr = [qsb[:, 512 + 256 * i: 512 + 256 * (i + 1)] for i in range(3)]
        hful = [qsb[:, 1280 + 256 * i: 1280 + 256 * (i + 1)] for i in range(3)]
        uh = [qsb[:, 2048 + 256 * i: 2048 + 256 * (i + 1)] for i in range(3)]
        hh = [hhT[:, 256 * i: 256 * (i + 1)] for i in range(3)]
        hl = [hlT[:, 256 * i: 256 * (i + 1)] for i in range(3)]
        tscr = vsb[:, 0:2048].rearrange("p (c t) -> p c t", c=8)
        uscr = vsb[:, 2048:4096].rearrange("p (c t) -> p c t", c=8)
        outsb = laT[:, 0:2048].rearrange("p (c t) -> p c t", c=8)

        pBIG = es.enter_context(nc.psum_tensor("pBIG", [128, 2048], F32))
        pB4 = es.enter_context(nc.psum_tensor("pB4", [128, 512], F32))
        pB5 = es.enter_context(nc.psum_tensor("pB5", [128, 512], F32))
        pB67 = es.enter_context(nc.psum_tensor("pB67", [128, 1024], F32))
        pq = [pBIG[:, 0:512], pBIG[:, 512:1024]]
        pv = [pBIG[0:96, 1024:1536], pBIG[0:96, 1536:2048]]
        psv = pBIG[0:96, 0:1024].rearrange("p (s x) -> p s x", s=8)
        SLOT2HEAD = [0, 2, 4, 6, 1, 3, 5, 7, 8, 10, 12, 14, 9, 11, 13, 15]
        po = pq
        pstat1 = pBIG[0:1, 1024:1504]
        pstat2 = pBIG[0:1, 1536:2016]
        ptp = [pB4[:].rearrange("p (i x) -> p i x", i=4),
               pB5[:].rearrange("p (i x) -> p i x", i=4)]
        pfav = pB67[:].rearrange("p (c x) -> p c x", c=8)
        pAb = pB4
        pBb = pB5
        pg = [pBIG[:, 0:512], pBIG[:, 512:1024]]
        pu = [pBIG[:, 1024:1536], pBIG[:, 1536:2048]]

        def pd_slice(dc, Tss):
            if dc < 2:
                return pB4[:, 256 * dc: 256 * dc + Tss]
            if dc < 4:
                return pB5[:, 256 * (dc - 2): 256 * (dc - 2) + Tss]
            return pB67[:, 256 * (dc - 4): 256 * (dc - 4) + Tss]
        pstatA = pB67[0:1, 0:256]
        pstatB = pB67[0:1, 512:768]

        for name in ["tt", "vv", "ss", "gc"] + SYNC_LANES + GP_LANES:
            sems[name] = es.enter_context(nc.semaphore(name))
        block = es.enter_context(nc.Block())

        consts_ld = []
        for dst, src in ((id_sb[:], identc[:, 0:128]), (ones_sb[:], identc[:, 128:256]),
                         (mask_sb[:], masks), (bias_sb[:, 0:24], biasm), (bv_sb[:], bvrow)):
            consts_ld.append(dmae("sync", [],
                                  (lambda e, d=dst, s=src: e.dma_start(out=d, in_=s))))

        bqs_prep = None
        x_lds = []
        for t0, T in ((0, 1032), (1032, 1032)):
            x_lds.append(dmae("sync", [],
                              (lambda e, t0=t0, T=T: e.dma_start(
                                  out=x1v[:, :, t0:t0 + T],
                                  in_=xT[:, t0:t0 + T].rearrange("(c p) t -> p c t", p=128)))))

        bqs_prep = emit("vector", W("vector", *consts_ld),
                        (lambda e: e.tensor_scalar(bias_sb[:, 24:32], bias_sb[:, 0:8],
                                                   QSCALE, None, ALU.mult)),
                        inc="vv")

        conv_toks = []
        for wsrc, wdst in ((wgTi, wg_r), (wuTi, wu_r)):
            for hc in range(24):
                conv_toks.append(dmae("gpsimd", [],
                                      (lambda e, s=wsrc, d=wdst, hc=hc: e.dma_start(
                                          out=d[hc, :, :],
                                          in_=s[:, 128 * hc:128 * (hc + 1)]))))
        for qtr in range(4):
            conv_toks.append(dmae("gpsimd", [],
                                  (lambda e, qtr=qtr: e.dma_start(
                                      out=wd_r[768 * qtr:768 * (qtr + 1), :],
                                      in_=wdTi[768 * qtr:768 * (qtr + 1), :]))))
        for wsrc, wdst in ((wgTi, wg_rt), (wuTi, wu_rt)):
            for qtr in range(4):
                conv_toks.append(dmae("gpsimd", [],
                                      (lambda e, s=wsrc, d=wdst, qtr=qtr:
                                       e.dma_start(
                                           out=d[256 * qtr:256 * (qtr + 1), :],
                                           in_=s[256 * qtr:256 * (qtr + 1), :]))))
        conv_all_done = [(L, cnt[L]) for L in GP_LANES]

        wslot_last_use = [0, 0, 0, 0]
        evac_qk = [0, 0]
        prev_exp_ss = 0
        prev_nmax_vv = 0
        prev_slab_rows_vv = 0
        prev_apply_vv = 0
        prev_pfa_evac_vv = 0
        prev_ptsb_evac = [0, 0, 0, 0]
        av_groupslot = [0, 0]
        prev_tile_trans_tt = 0
        prev_yt_evac = [0, 0]

        def emit_qk_chunk(name, wT, outb, bcol, do_scale, oc, jt0, jT):
            st_ = qkstate
            ldc, slot_j = load_eighth_g(wT, oc, st_["nq"] % 4), st_["nq"] % 4
            st_["nq"] += 1
            bank = oc % 2
            wts = W("tensor", ldc, *x_lds) + W("tensor", ("ss", evac_qk[bank]),
                    ("vv", prev_yt_evac[bank]), ("vv", prev_nmax_vv),
                    ("ss", prev_exp_ss))
            last = None
            for k in range(8):
                last = emit("tensor", wts if k == 0 else [],
                            (lambda e, j=slot_j, k=k, b=bank, T=jT, t0=jt0:
                             e.matmul(pq[b][:, 0:T],
                                      wslot[j][:, k, 0:128],
                                      x1v[:, k, t0:t0 + T],
                                      start=(k == 0), stop=(k == 7))),
                            inc="tt" if k == 7 else None)
            wslot_last_use[slot_j] = last
            ev = emit("scalar", W("scalar", ("tt", last), ("vv", bqs_prep)),
                      (lambda e, oc=oc, b=bank, T=jT, outb=outb, bcol=bcol,
                              sc=do_scale:
                       e.activation(outb[:, 384 * oc:384 * oc + T],
                                    pq[b][:, 0:T], AF.Identity,
                                    bias=bias_sb[:, (24 if sc else bcol) + oc:
                                                 (24 if sc else bcol) + oc + 1],
                                    scale=QSCALE if sc else 1.0)),
                      inc="ss")
            evac_qk[bank] = ev
            qkstate["done"] = ev

        def qk_jobs_for(t0_, T_):
            return ([("q", wqT, qsb, 0, True, oc, t0_, T_) for oc in range(8)] +
                    [("k", wkT, ksb, 8, False, oc, t0_, T_) for oc in range(8)])

        qkstate = {"nq": 0, "done": 0}
        load_eighth_g = None
        pending_qk = qk_jobs_for(*SLABS[0])

        for si, (t0, T) in enumerate(SLABS):
            TILES = tiles_of(T)

            def load_eighth(wT, oc, slot_j):
                return dmae("sync", W("sync", ("tt", wslot_last_use[slot_j])),
                            (lambda e, wT=wT, oc=oc, j=slot_j: e.dma_start(
                                out=wslot[j][:, :, :],
                                in_=wT[:, 128 * oc:128 * (oc + 1)].rearrange(
                                    "(c p) o -> p c o", p=128))))

            load_eighth_g = load_eighth
            for job in pending_qk:
                emit_qk_chunk(*job)
            qk_evac_done = qkstate["done"]
            pending_next = (qk_jobs_for(*SLABS[si + 1]) if si + 1 < len(SLABS)
                            else [])

            def emit_v_tile(ti):
                nonlocal_state = None
                tb, TT = TILES[ti]
                for ot in range(8):
                    ldc, slot_j = load_eighth(wvT, ot, qkstate["nq"] % 4), qkstate["nq"] % 4
                    qkstate["nq"] += 1
                    b = vstate["bank"]
                    wts = W("tensor", ldc,
                            ("ss", pv_rd[b]), ("vv", prev_slab_rows_vv))
                    for k in range(8):
                        emit("tensor", wts if k == 0 else [],
                             (lambda e, j=slot_j, k=k, tb=tb, TT=TT, b=b, t0=t0:
                              e.matmul(pv[b][0:TT, 0:128],
                                       x1v[:, k, t0 + tb:t0 + tb + TT],
                                       wslot[j][:, k, 0:128],
                                       start=(k == 0), stop=False)),
                             inc=None)
                    last = emit("tensor", [],
                                (lambda e, ot=ot, tb=tb, TT=TT, b=b:
                                 e.matmul(pv[b][0:TT, 0:128],
                                          ones_sb[0:1, 0:TT],
                                          bv_sb[0:1, 128 * ot:128 * (ot + 1)],
                                          start=False, stop=True)),
                                inc="tt")
                    wslot_last_use[slot_j] = last
                    ev = emit("scalar", W("scalar", ("tt", last)),
                              (lambda e, ti=ti, ot=ot, TT=TT, b=b:
                               e.copy(vsb[0:TT, 1024 * ti + 128 * ot:1024 * ti + 128 * (ot + 1)],
                                      pv[b][0:TT, 0:128])),
                              inc="ss")
                    pv_rd[b] = ev
                    v_evac_tile[ti] = ev
                    vstate["bank"] ^= 1

            vstate = {"bank": 0}
            v_evac_tile = {}
            pv_rd = [0, 0]
            emit_v_tile(0)

            for ti, (tb, TT) in enumerate(TILES):
                for half in range(2):
                    first_wts = W("tensor", ("ss", qk_evac_done),
                                  ("ss", prev_exp_ss), ("vv", prev_nmax_vv))
                    sc_done = 0
                    order = [0, 4, 1, 5, 2, 6, 3, 7]
                    for oi, sl8 in enumerate(order):
                        s = 8 * half + sl8
                        h = SLOT2HEAD[s]
                        pp = h % 2
                        c = h // 2
                        pos = sl8 % 4
                        sc_done = emit(
                            "tensor", first_wts if oi == 0 else [],
                            (lambda e, sl8=sl8, pp=pp, c=c, tb=tb, TT=TT,
                                    st=(pos == 0), sp=(pos == 3):
                             e.matmul(psv[0:TT, sl8, 0:TT],
                                      qsb[64 * pp:64 * pp + 64, 384 * c + tb:384 * c + tb + TT],
                                      ksb[64 * pp:64 * pp + 64, 384 * c + tb:384 * c + tb + TT],
                                      start=st, stop=sp,
                                      tile_position=(64 * pp, 0))),
                            inc="tt")
                    nm = emit("vector", W("vector", ("tt", sc_done)),
                              (lambda e, TT=TT, half=half: e.tensor_reduce(
                                  nm_sb[0:TT, half:half + 1], psv[0:TT, :, 0:TT],
                                  AX.XY, ALU.max, negate=True)),
                              inc="vv")
                    prev_nmax_vv = nm
                    ex = emit("scalar", W("scalar", ("vv", nm),
                                          ("tt", prev_tile_trans_tt)),
                              (lambda e, TT=TT, half=half: e.activation(
                                  esb[0:TT, 8 * half:8 * half + 8, 0:TT],
                                  psv[0:TT, :, 0:TT], AF.Exp,
                                  bias=nm_sb[0:TT, half:half + 1], scale=1.0)),
                              inc="ss")
                    prev_exp_ss = ex
                if ti + 1 < len(TILES):
                    emit_v_tile(ti + 1)
                emit("vector", W("vector", ("ss", prev_exp_ss)),
                     (lambda e, TT=TT: e.tensor_tensor(
                         eAb[0:TT, :, 0:TT], esb[0:TT, :, 0:TT],
                         mA[0:TT, 0:TT].unsqueeze(1).broadcast_to((TT, 16, TT)),
                         ALU.mult)),
                     inc="vv")
                emit("vector", [],
                     (lambda e, TT=TT: e.tensor_tensor(
                         esb[0:TT, :, 0:TT], esb[0:TT, :, 0:TT],
                         mL[0:TT, 0:TT].unsqueeze(1).broadcast_to((TT, 16, TT)),
                         ALU.mult)),
                     inc="vv")
                emit("vector", [],
                     (lambda e, TT=TT: e.tensor_reduce(
                         sums_sb[0:TT, 0:16], esb[0:TT, :, 0:TT], AX.X, ALU.add)),
                     inc="vv")
                emit("vector", [],
                     (lambda e, TT=TT: e.tensor_reduce(
                         sums_sb[0:TT, 16:32], eAb[0:TT, :, 0:TT], AX.X, ALU.add)),
                     inc="vv")
                emit("vector", [],
                     (lambda e, TT=TT: e.reciprocal(sums_sb[0:TT, 32:48],
                                                    sums_sb[0:TT, 0:16])),
                     inc="vv", chain=True)
                emit("vector", [],
                     (lambda e, TT=TT: e.reciprocal(sums_sb[0:TT, 48:64],
                                                    sums_sb[0:TT, 16:32])),
                     inc="vv", chain=True)
                p_done = {}
                for s in range(16):
                    emit("vector", [],
                         (lambda e, s=s, TT=TT: e.tensor_scalar(
                             psbv[0:TT, s, 0:TT], esb[0:TT, s, 0:TT],
                             sums_sb[0:TT, 32 + s:33 + s], None, ALU.mult)),
                         inc="vv", chain=(s == 0))
                for s in range(16):
                    p_done[s] = emit("vector", [],
                                     (lambda e, s=s, TT=TT: e.scalar_tensor_tensor(
                                         psbv[0:TT, s, 0:TT], eAb[0:TT, s, 0:TT],
                                         sums_sb[0:TT, 48 + s:49 + s],
                                         psbv[0:TT, s, 0:TT], ALU.mult, ALU.add)),
                                     inc="vv")
                av_first_wts = W("tensor", ("vv", prev_pfa_evac_vv))
                trans_last = 0
                started = set()
                last_touch = {}
                for s in range(16):
                    h = SLOT2HEAD[s]
                    last_touch[(6 + (h // 2) // 4, h % 2)] = s
                evac_group = {}
                for g in range(4):
                    par = g % 2
                    gs = g % 2
                    for i in range(4):
                        s = 4 * g + i
                        wts = W("tensor", ("vv", p_done[s]),
                                ("tt", av_groupslot[gs]))
                        trans_last = emit(
                            "tensor", wts,
                            (lambda e, s=s, par=par, i=i, TT=TT: e.transpose(
                                ptp[par][0:96, i, 0:TT], psbv[0:TT, s, 0:96],
                                id_sb[0:TT, 0:TT])),
                            inc="tt")
                    ev = emit("vector", W("vector", ("tt", trans_last)),
                              (lambda e, gs=gs, par=par, TT=TT:
                               e.tensor_copy(ptsb8[0:TT, 4 * gs:4 * gs + 4, 0:TT],
                                             ptp[par][0:TT, :, 0:TT])),
                              inc="vv")
                    prev_ptsb_evac[gs] = ev
                    evac_group[g] = ev
                    av_last = 0
                    for i in range(4):
                        s = 4 * g + i
                        h = SLOT2HEAD[s]
                        pp = h % 2
                        c = h // 2
                        key = (6 + c // 4, pp)
                        startb = key not in started
                        started.add(key)
                        stopb = (last_touch[key] == s)
                        wts = W("tensor", ("vv", ev), ("ss", v_evac_tile[ti])) + (
                            av_first_wts if s == 0 else [])
                        av_last = emit("tensor", wts,
                             (lambda e, s=s, h=h, pp=pp, c=c, ti=ti, TT=TT, gs=gs, i=i,
                                     stb=startb, spb=stopb:
                              e.matmul(pfav[64 * pp:64 * pp + 64, c, 0:TT],
                                       vsb[0:TT, 1024 * ti + 64 * h:1024 * ti + 64 * h + 64],
                                       ptsb8[0:TT, 4 * gs + i, 0:TT],
                                       start=stb, stop=spb,
                                       tile_position=(0, 64 * pp))),
                             inc="tt")
                    av_groupslot[gs] = av_last
                    pass
                prev_tile_trans_tt = trans_last
                av_all = cnt["tt"]
                prev_pfa_evac_vv = emit(
                    "vector", W("vector", ("tt", av_all)),
                    (lambda e, tb=tb, TT=TT: e.tensor_copy(
                        lav[:, :, tb:tb + TT], pfav[:, :, 0:TT])),
                    inc="vv")

            pending_qk = pending_next
            o_first_wts = W("tensor", ("vv", prev_pfa_evac_vv),
                            ("ss", prev_exp_ss), ("vv", prev_nmax_vv))
            yt_done = 0
            for oc in range(8):
                ldc, slot_j = load_eighth(woT, oc, qkstate["nq"] % 4), qkstate["nq"] % 4
                qkstate["nq"] += 1
                bank = oc % 2
                wts = W("tensor", ldc, ("ss", qkstate["done"]),
                        ("vv", prev_yt_evac[bank])) + (o_first_wts if oc == 0 else [])
                last = None
                for k in range(8):
                    last = emit("tensor", wts if k == 0 else [],
                                (lambda e, j=slot_j, k=k, b=bank, T=T:
                                 e.matmul(po[b][:, 0:T],
                                          wslot[j][:, k, 0:128],
                                          lav[:, k, 0:T],
                                          start=(k == 0), stop=(k == 7))),
                                inc="tt" if k == 7 else None)
                wslot_last_use[slot_j] = last
                ev = emit("vector", W("vector", ("tt", last)),
                          (lambda e, oc=oc, b=bank, T=T, t0=t0: e.scalar_tensor_tensor(
                              ytv[:, oc, 0:T], po[b][:, 0:T],
                              bias_sb[:, 16 + oc:17 + oc], x1v[:, oc, t0:t0 + T],
                              ALU.add, ALU.add)),
                          inc="vv")
                prev_yt_evac[bank] = ev
                yt_done = ev
            sq_tt = [0, 0]
            st1 = st2 = 0
            for oc in range(8):
                sl = oc % 2
                sq = emit("scalar", W("scalar", ("vv", prev_yt_evac[oc % 2]),
                                      ("tt", sq_tt[sl])),
                          (lambda e, oc=oc, sl=sl, T=T: e.activation(
                              sqscr[sl][:, 0:T], ytv[:, oc, 0:T], AF.Square)),
                          inc="ss")
                st1 = emit("tensor", W("tensor", ("vv", prev_yt_evac[oc % 2]),
                                       ("vv", prev_slab_rows_vv)),
                           (lambda e, oc=oc, T=T: e.matmul(
                               pstat1[:, 0:T], ones_sb[:, 0:1], ytv[:, oc, 0:T],
                               start=(oc == 0), stop=(oc == 7))),
                           inc="tt")
                st2 = emit("tensor", W("tensor", ("ss", sq)),
                           (lambda e, sl=sl, oc=oc, T=T: e.matmul(
                               pstat2[:, 0:T], ones_sb[:, 0:1], sqscr[sl][:, 0:T],
                               start=(oc == 0), stop=(oc == 7))),
                           inc="tt")
                sq_tt[sl] = st2
            emit("vector", W("vector", ("tt", st1), ("tt", st2)),
                 (lambda e, T=T: e.tensor_scalar(srow[:, 0:T], pstat1[:, 0:T],
                                                 1.0 / 1024.0, None, ALU.mult)),
                 inc="vv")
            emit("vector", [],
                 (lambda e, T=T: e.tensor_scalar(srow[:, 384:384 + T], pstat2[:, 0:T],
                                                 1.0 / 1024.0, None, ALU.mult)),
                 inc="vv")
            emit("vector", [],
                 (lambda e, T=T: e.tensor_tensor(srow[:, 768:768 + T], srow[:, 0:T],
                                                 srow[:, 0:T], ALU.mult)),
                 inc="vv", chain=True)
            emit("vector", [],
                 (lambda e, T=T: e.tensor_tensor(
                     srow[:, 1152:1152 + T], srow[:, 384:384 + T],
                     srow[:, 768:768 + T], ALU.subtract)),
                 inc="vv", chain=True)
            rows_v = emit("vector", [],
                          (lambda e, T=T: e.tensor_scalar(
                              srow[:, 1152:1152 + T], srow[:, 1152:1152 + T],
                              EPS, None, ALU.add)),
                          inc="vv", chain=True)
            emit("scalar", W("scalar", ("vv", rows_v)),
                 (lambda e, T=T: e.activation(srow[:, 768:768 + T],
                                              srow[:, 1152:1152 + T], AF.Ln)),
                 inc="ss")
            r_row = emit("scalar", [],
                         (lambda e, T=T: e.activation(srow[:, 384:384 + T],
                                                      srow[:, 768:768 + T], AF.Exp,
                                                      scale=-0.5)),
                         inc="ss", chain=True)
            b_row = emit("vector", W("vector", ("ss", r_row)),
                         (lambda e, T=T: e.scalar_tensor_tensor(
                             srow[:, 1152:1152 + T], srow[:, 0:T], -1.0,
                             srow[:, 384:384 + T], ALU.mult, ALU.mult)),
                         inc="vv", chain=True)
            bc1 = emit("tensor", W("tensor", ("ss", r_row), ("vv", prev_apply_vv),
                                   ("vv", prev_ptsb_evac[0]), ("vv", prev_ptsb_evac[1]),
                                   ("vv", prev_ptsb_evac[2]), ("vv", prev_ptsb_evac[3])),
                       (lambda e, T=T: e.matmul(pAb[:, 0:T], ones_sb[0:1, :],
                                                srow[0:1, 384:384 + T],
                                                start=True, stop=True)),
                       inc="tt")
            bc2 = emit("tensor", W("tensor", ("vv", b_row)),
                       (lambda e, T=T: e.matmul(pBb[:, 0:T], ones_sb[0:1, :],
                                                srow[0:1, 1152:1152 + T],
                                                start=True, stop=True)),
                       inc="tt")
            ap_v = 0
            for oc in range(8):
                sl = oc % 2
                emit("vector", W("vector", ("tt", bc1), ("tt", bc2)),
                     (lambda e, oc=oc, sl=sl, T=T: e.tensor_tensor(
                         sqscr[sl][:, 0:T], ytv[:, oc, 0:T], pAb[:, 0:T], ALU.mult)),
                     inc="vv")
                ap_v = emit("vector", [],
                            (lambda e, oc=oc, sl=sl, T=T, t0=t0: e.tensor_tensor(
                                x1v[:, oc, t0:t0 + T], sqscr[sl][:, 0:T], pBb[:, 0:T],
                                ALU.add)),
                            inc="vv", chain=True)
            prev_apply_vv = ap_v
            prev_slab_rows_vv = rows_v

        slab_done_vv = prev_apply_vv
        o_last_tt = cnt["tt"]
        wf_lduse = {"g": [o_last_tt, o_last_tt], "u": [o_last_tt, o_last_tt],
                    "d": [o_last_tt, o_last_tt]}
        prev_y2_evac = 0
        prev_ln2rows_vv = 0
        hbuf_lastmm = [o_last_tt, o_last_tt, o_last_tt]
        pgu_read = [[0, 0], [0, 0]]
        out_dma = []
        prev_out_evac = 0
        def emit_split(fi):
            f0, Tss = FSS[fi]
            emit("vector", [],
                 (lambda e, f0=f0, Tss=Tss: e.tensor_scalar(
                     tscr[:, :, 0:Tss], x1v[:, :, f0:f0 + Tss], CVELT, None, ALU.mult)),
                 inc="vv")
            emit("vector", [],
                 (lambda e, f0=f0, Tss=Tss: e.scalar_tensor_tensor(
                     uscr[:, :, 0:Tss], tscr[:, :, 0:Tss], 1.0,
                     x1v[:, :, f0:f0 + Tss], ALU.mult, ALU.subtract)),
                 inc="vv", chain=True)
            emit("vector", [],
                 (lambda e, Tss=Tss: e.scalar_tensor_tensor(
                     x1h[:, :, 0:Tss], tscr[:, :, 0:Tss], 1.0,
                     uscr[:, :, 0:Tss], ALU.mult, ALU.subtract)),
                 inc="vv", chain=True)
            return emit("vector", [],
                        (lambda e, f0=f0, Tss=Tss: e.tensor_tensor(
                            x1l[:, :, 0:Tss], x1v[:, :, f0:f0 + Tss],
                            x1h_f[:, :, 0:Tss], ALU.subtract)),
                        inc="vv", chain=True)

        def emit_tail_ffn():
            tf_vv = cnt["vv"]
            tf_ss = cnt["ss"]
            gtiles = [wgfT[:, 0:512], wgfT[:, 512:1024], wtail[:, 0:512]]
            utiles = [wufT[:, 0:512], wufT[:, 512:1024], wtail[:, 512:1024]]
            g_last = {0: wf_lduse["g"][0], 1: wf_lduse["g"][0], 2: 0}
            u_last = {0: wf_lduse["u"][0], 1: wf_lduse["u"][0], 2: 0}
            ldd_t = {0: wf_lduse["d"][0], 1: wf_lduse["d"][1]}
            g_sil = qsb[:, 2048:2560]
            hp_t = qsb[:, 1280:1792]
            hT_sb = qsb[:, 512:896]
            tsc = qsb[:, 896:960]
            usc = qsb[:, 960:1024]
            y_sc = qsb[:, 512:1536]
            ptr = [pB67[:, 256:272], pB67[:, 768:784]]
            silu_rd = [0, 0]
            mult_rd = [0, 0]
            tr_evac = {}
            mus = {}
            st_t = {"dn": 0, "first": True}

            def emit_td(c):
                for q in range(4):
                    hc = 4 * c + q
                    sp_ = hc % 2
                    trl = emit("tensor",
                               W("tensor", ("vv", mus[c]),
                                 ("vv", tr_evac.get(hc - 2, 0))),
                               (lambda e, q=q, sp_=sp_: e.transpose(
                                   ptr[sp_][:, 0:16],
                                   hp_t[0:16, 128 * q:128 * (q + 1)],
                                   id_sb[0:16, 0:16])),
                               inc="tt")
                    ev = emit("vector", W("vector", ("tt", trl)),
                              (lambda e, hc=hc, sp_=sp_: e.tensor_copy(
                                  hT_sb[:, 16 * hc:16 * hc + 16],
                                  ptr[sp_][:, 0:16])),
                              inc="vv")
                    tr_evac[hc] = ev
                c0 = 64 * c
                emit("vector", [],
                     (lambda e, c0=c0: e.tensor_scalar(
                         tsc[:, 0:64], hT_sb[:, c0:c0 + 64], CVELT, None,
                         ALU.mult)),
                     inc="vv", chain=True)
                emit("vector", [],
                     (lambda e, c0=c0: e.scalar_tensor_tensor(
                         usc[:, 0:64], tsc[:, 0:64], 1.0,
                         hT_sb[:, c0:c0 + 64], ALU.mult, ALU.subtract)),
                     inc="vv", chain=True)
                emit("vector", [],
                     (lambda e, c0=c0: e.scalar_tensor_tensor(
                         hTh[:, c0:c0 + 64], tsc[:, 0:64], 1.0,
                         usc[:, 0:64], ALU.mult, ALU.subtract)),
                     inc="vv", chain=True)
                spl = emit("vector", [],
                           (lambda e, c0=c0: e.tensor_tensor(
                               hTl[:, c0:c0 + 64], hT_sb[:, c0:c0 + 64],
                               hTh[:, c0:c0 + 64].bitcast(F32),
                               ALU.subtract)),
                           inc="vv", chain=True)
                for q in range(4):
                    hc = 4 * c + q
                    jd = hc % 2
                    ldd = dmae("gpsimd",
                               W("gpsimd", ("tt", ldd_t[jd]), *conv_all_done),
                               (lambda e, hc=hc, jd=jd: e.dma_start(
                                   out=wd2s[jd][:, :],
                                   in_=wd_r[128 * hc:128 * (hc + 1), :])))
                    dnl = None
                    for h in range(2):
                        for p_, hop in enumerate((hTh, hTl)):
                            dnl = emit(
                                "tensor",
                                (W("tensor", ldd, ("vv", spl))
                                 if (h == 0 and p_ == 0) else []),
                                (lambda e, hc=hc, jd=jd, h=h, hop=hop,
                                        st=(hc == 0 and p_ == 0),
                                        sp=(hc == 23 and p_ == 1): e.matmul(
                                    (pB4 if h == 0 else pB5)[0:16, 0:512],
                                    hop[:, 16 * hc:16 * hc + 16],
                                    wd2s[jd][:, 512 * h:512 * (h + 1)],
                                    start=st, stop=sp)),
                                inc="tt" if (h == 1 and p_ == 1) else None)
                    ldd_t[jd] = dnl
                    st_t["dn"] = dnl

            for c in range(6):
                par = c % 2
                gl = ul = None
                for k in range(8):
                    kp = k % 3
                    ldg = dmae("sync", W("sync", ("tt", g_last[kp])),
                               (lambda e, c=c, k=k, kp=kp: e.dma_start(
                                   out=gtiles[kp],
                                   in_=wg_rt[128 * k:128 * (k + 1),
                                             512 * c:512 * (c + 1)])))
                    for p_, xop in enumerate((x1h, x1l)):
                        wts = W("tensor", ldg) if p_ == 0 else []
                        if k == 0 and p_ == 0:
                            wts += W("tensor", ("ss", silu_rd[par]),
                                     ("vv", split_done_fi[8]))
                            if st_t["first"]:
                                wts += W("tensor", ("vv", tf_vv),
                                         ("ss", tf_ss))
                                st_t["first"] = False
                        gl = emit("tensor", wts,
                                  (lambda e, k=k, kp=kp, par=par, xop=xop,
                                          st=(k == 0 and p_ == 0),
                                          sp=(k == 7 and p_ == 1): e.matmul(
                                      pg[par][0:16, 0:512],
                                      xop[:, k, 0:16], gtiles[kp],
                                      start=st, stop=sp)),
                                  inc="tt" if p_ == 1 else None)
                    g_last[kp] = gl
                for k in range(8):
                    kp = k % 3
                    ldu = dmae("sync", W("sync", ("tt", u_last[kp])),
                               (lambda e, c=c, k=k, kp=kp: e.dma_start(
                                   out=utiles[kp],
                                   in_=wu_rt[128 * k:128 * (k + 1),
                                             512 * c:512 * (c + 1)])))
                    for p_, xop in enumerate((x1h, x1l)):
                        wts = W("tensor", ldu) if p_ == 0 else []
                        if k == 0 and p_ == 0:
                            wts += W("tensor", ("vv", mult_rd[par]))
                        ul = emit("tensor", wts,
                                  (lambda e, k=k, kp=kp, par=par, xop=xop,
                                          st=(k == 0 and p_ == 0),
                                          sp=(k == 7 and p_ == 1): e.matmul(
                                      pu[par][0:16, 0:512],
                                      xop[:, k, 0:16], utiles[kp],
                                      start=st, stop=sp)),
                                  inc="tt" if p_ == 1 else None)
                    u_last[kp] = ul
                if c >= 1:
                    emit_td(c - 1)
                sl = emit("scalar", W("scalar", ("tt", gl), ("vv", tf_vv)),
                          (lambda e, par=par: e.activation(
                              g_sil[0:16, 0:512], pg[par][0:16, 0:512],
                              AF.Silu, scale=1.0 / 127.0)),
                          inc="ss")
                silu_rd[par] = sl
                mu = emit("vector", W("vector", ("ss", sl), ("tt", ul)),
                          (lambda e, par=par: e.tensor_tensor(
                              hp_t[0:16, 0:512], g_sil[0:16, 0:512],
                              pu[par][0:16, 0:512], ALU.mult)),
                          inc="vv")
                mult_rd[par] = mu
                mus[c] = mu
            emit_td(5)
            ye = emit("vector", W("vector", ("tt", st_t["dn"])),
                      (lambda e: e.tensor_copy(y_sc[0:16, 0:512],
                                               pB4[0:16, 0:512])),
                      inc="vv")
            ye = emit("vector", [],
                      (lambda e: e.tensor_copy(y_sc[0:16, 512:1024],
                                               pB5[0:16, 0:512])),
                      inc="vv", chain=True)
            y2_ev = 0
            ytr = {}
            for dc in range(8):
                sp_ = dc % 2
                trl = emit("tensor",
                           W("tensor", ("vv", ye), ("vv", ytr.get(dc - 2, 0))),
                           (lambda e, dc=dc, sp_=sp_: e.transpose(
                               ptr[sp_][:, 0:16],
                               y_sc[0:16, 128 * dc:128 * (dc + 1)],
                               id_sb[0:16, 0:16])),
                           inc="tt")
                y2_ev = emit("vector", W("vector", ("tt", trl)),
                             (lambda e, dc=dc, sp_=sp_: e.scalar_tensor_tensor(
                                 x1v[:, dc, 2048:2064], ptr[sp_][:, 0:16], S2,
                                 x1v[:, dc, 2048:2064], ALU.mult, ALU.add)),
                             inc="vv", chain=True)
                ytr[dc] = y2_ev
            return y2_ev

        split_done_fi = {0: emit_split(0)}
        for fi, (f0, Tss) in enumerate(FSS):
            tail = (Tss < 256)
            split_done = split_done_fi[fi]
            dn_last = 0
            down_args = [None]
            def emit_down(hc_d, j_d, hsp_d, ldd_d, hc_w, tail, Tss):
                wts = W("tensor", ldd_d, ("vv", hsp_d), ("vv", slab_done_vv),
                        ("vv", prev_y2_evac), ("vv", prev_ln2rows_vv))
                hops = [hful[j_d]] if tail else [hh[j_d], hl[j_d]]
                nh = len(hops)
                dl = None
                for dc in range(8):
                    for pi, hop in enumerate(hops):
                        st = (hc_d == 0 and pi == 0 and dc % 2 == 0)
                        sp = (hc_d == 23 and pi == nh - 1 and dc % 2 == 1)
                        dl = emit(
                            "tensor", wts if (dc == 0 and pi == 0) else [],
                            (lambda e, hc_w=hc_w, dc=dc, hop=hop, Tss=Tss, st=st, sp=sp,
                                    tl=tail:
                             e.matmul(pd_slice(dc, Tss),
                                      wd2s[hc_w % 2][:, 128 * dc:128 * (dc + 1)].bitcast(F32)
                                      if tl else wd2s[hc_w % 2][:, 128 * dc:128 * (dc + 1)],
                                      hop[:, 0:Tss], start=st, stop=sp)),
                            inc="tt" if (dc == 7 and pi == nh - 1) else None)
                return dl

            y2_ev = emit_tail_ffn() if tail else 0
            pending = []
            for hc in ([] if tail else range(24)):
                j = hc % 3
                jd = hc % 2
                lg = dmae("sync", W("sync", ("tt", wf_lduse["g"][0]),
                                    *conv_all_done),
                          (lambda e, hc=hc: e.dma_start(
                              out=wg2s[0][:, :, :],
                              in_=wg_r[hc, :, :].rearrange("(c p) j -> p c j", p=128))))
                lu = dmae("sync", W("sync", ("tt", wf_lduse["u"][0])),
                          (lambda e, hc=hc: e.dma_start(
                              out=wu2s[0][:, :, :],
                              in_=wu_r[hc, :, :].rearrange("(c p) j -> p c j", p=128))))
                bank = hc % 2
                if tail:
                    xops = [None]
                else:
                    xops = [x1h, x1l]
                npass = len(xops)
                wts = W("tensor", lg, ("vv", split_done),
                        ("ss", prev_exp_ss), ("vv", prev_nmax_vv),
                        ("vv", prev_yt_evac[0]), ("vv", prev_yt_evac[1]),
                        ("ss", pgu_read[0][bank]))
                if tail:
                    wts += W("tensor", ("vv", slab_done_vv))
                gl = None
                for k in range(8):
                    for pi, xop in enumerate(xops):
                        first = (k == 0 and pi == 0)
                        lastm = (k == 7 and pi == npass - 1)
                        gl = emit("tensor", wts if first else [],
                                  (lambda e, j=j, k=k, xop=xop, b=bank, Tss=Tss, f0=f0,
                                          st=first, sp=lastm:
                                   e.matmul(pg[b][:, 0:Tss],
                                            wg2s[0][:, k, :].bitcast(F32) if xop is None
                                            else wg2s[0][:, k, :],
                                            x1v[:, k, f0:f0 + Tss] if xop is None
                                            else xop[:, k, 0:Tss],
                                            start=st, stop=sp)),
                                  inc="tt" if lastm else None)
                wf_lduse["g"][0] = gl
                wts = W("tensor", lu, ("vv", pgu_read[1][bank]))
                ul = None
                for k in range(8):
                    for pi, xop in enumerate(xops):
                        first = (k == 0 and pi == 0)
                        lastm = (k == 7 and pi == npass - 1)
                        ul = emit("tensor", wts if first else [],
                                  (lambda e, j=j, k=k, xop=xop, b=bank, Tss=Tss, f0=f0,
                                          st=first, sp=lastm:
                                   e.matmul(pu[b][:, 0:Tss],
                                            wu2s[0][:, k, :].bitcast(F32) if xop is None
                                            else wu2s[0][:, k, :],
                                            x1v[:, k, f0:f0 + Tss] if xop is None
                                            else xop[:, k, 0:Tss],
                                            start=st, stop=sp)),
                                  inc="tt" if lastm else None)
                wf_lduse["u"][0] = ul
                if len(pending) >= 2:
                    pd_args = pending.pop(0)
                    dn_last = emit_down(*pd_args, tail, Tss)
                    wf_lduse["d"][pd_args[4] % 2] = dn_last
                    hbuf_lastmm[pd_args[1]] = dn_last
                ldd = dmae("gpsimd", W("gpsimd", ("tt", wf_lduse["d"][jd]),
                                       *conv_all_done),
                           (lambda e, hc=hc, jd=jd: e.dma_start(
                               out=wd2s[jd][:, :],
                               in_=wd_r[128 * hc:128 * (hc + 1), :])))
                sl = emit("scalar", W("scalar", ("tt", gl), ("tt", hbuf_lastmm[j])),
                          (lambda e, j=j, b=bank, Tss=Tss: e.activation(
                              hscr[j][:, 0:Tss], pg[b][:, 0:Tss], AF.Silu,
                              scale=1.0 / 127.0)),
                          inc="ss")
                pgu_read[0][bank] = sl
                mu = emit("vector", W("vector", ("ss", sl), ("tt", ul),
                                      ("tt", hbuf_lastmm[j])),
                          (lambda e, j=j, b=bank, Tss=Tss: e.tensor_tensor(
                              hful[j][:, 0:Tss], hscr[j][:, 0:Tss],
                              pu[b][:, 0:Tss], ALU.mult)),
                          inc="vv")
                pgu_read[1][bank] = mu
                if tail:
                    hsp = mu
                else:
                    emit("vector", [],
                         (lambda e, j=j, Tss=Tss: e.tensor_scalar(
                             hscr[j][:, 0:Tss], hful[j][:, 0:Tss], CVELT, None, ALU.mult)),
                         inc="vv", chain=True)
                    emit("vector", [],
                         (lambda e, j=j, Tss=Tss: e.scalar_tensor_tensor(
                             uh[j][:, 0:Tss], hscr[j][:, 0:Tss], 1.0,
                             hful[j][:, 0:Tss], ALU.mult, ALU.subtract)),
                         inc="vv", chain=True)
                    emit("vector", [],
                         (lambda e, j=j, Tss=Tss: e.scalar_tensor_tensor(
                             hh[j][:, 0:Tss], hscr[j][:, 0:Tss], 1.0,
                             uh[j][:, 0:Tss], ALU.mult, ALU.subtract)),
                         inc="vv", chain=True)
                    hsp = emit("vector", [],
                               (lambda e, j=j, Tss=Tss: e.tensor_tensor(
                                   hl[j][:, 0:Tss], hful[j][:, 0:Tss],
                                   hh[j][:, 0:Tss].bitcast(F32), ALU.subtract)),
                               inc="vv", chain=True)
                pending.append((hc, j, hsp, ldd, hc))
            while pending:
                pd_args = pending.pop(0)
                dn_last = emit_down(*pd_args, tail, Tss)
                wf_lduse["d"][pd_args[4] % 2] = dn_last
                hbuf_lastmm[pd_args[1]] = dn_last
            if fi + 1 < len(FSS):
                emit("vector", W("vector", ("tt", wf_lduse["u"][0])), 
                     (lambda e: e.engine_nop()), inc="vv")
                split_done_fi[fi + 1] = emit_split(fi + 1)
            for dc in ([] if tail else range(8)):
                y2_ev = emit("vector", W("vector", ("tt", dn_last)),
                             (lambda e, dc=dc, f0=f0, Tss=Tss: e.scalar_tensor_tensor(
                                 x1v[:, dc, f0:f0 + Tss], pd_slice(dc, Tss), S2,
                                 x1v[:, dc, f0:f0 + Tss], ALU.mult, ALU.add)),
                             inc="vv")
            prev_y2_evac = y2_ev
            sq_tt2 = [0, 0]
            s1 = s2 = 0
            for dc in range(8):
                sl2 = dc % 2
                sq = emit("scalar", W("scalar", ("vv", y2_ev), ("tt", sq_tt2[sl2])),
                          (lambda e, dc=dc, sl2=sl2, f0=f0, Tss=Tss: e.activation(
                              sqf[sl2][:, 0:Tss], x1v[:, dc, f0:f0 + Tss], AF.Square)),
                          inc="ss")
                s1 = emit("tensor", W("tensor", ("vv", y2_ev)),
                          (lambda e, dc=dc, f0=f0, Tss=Tss: e.matmul(
                              pstatA[:, 0:Tss], ones_sb[:, 0:1],
                              x1v[:, dc, f0:f0 + Tss],
                              start=(dc == 0), stop=(dc == 7))),
                          inc="tt")
                s2 = emit("tensor", W("tensor", ("ss", sq)),
                          (lambda e, dc=dc, sl2=sl2, Tss=Tss: e.matmul(
                              pstatB[:, 0:Tss], ones_sb[:, 0:1], sqf[sl2][:, 0:Tss],
                              start=(dc == 0), stop=(dc == 7))),
                          inc="tt")
                sq_tt2[sl2] = s2
            emit("vector", W("vector", ("tt", s1), ("tt", s2)),
                 (lambda e, Tss=Tss: e.tensor_scalar(
                     srow[:, 0:Tss], pstatA[:, 0:Tss], 1.0 / 1024.0, None, ALU.mult)),
                 inc="vv")
            emit("vector", [],
                 (lambda e, Tss=Tss: e.tensor_scalar(
                     srow[:, 384:384 + Tss], pstatB[:, 0:Tss], 1.0 / 1024.0, None,
                     ALU.mult)),
                 inc="vv")
            emit("vector", [],
                 (lambda e, Tss=Tss: e.tensor_tensor(
                     srow[:, 768:768 + Tss], srow[:, 0:Tss], srow[:, 0:Tss],
                     ALU.mult)),
                 inc="vv", chain=True)
            emit("vector", [],
                 (lambda e, Tss=Tss: e.tensor_tensor(
                     srow[:, 1152:1152 + Tss], srow[:, 384:384 + Tss],
                     srow[:, 768:768 + Tss], ALU.subtract)),
                 inc="vv", chain=True)
            rows2_v = emit("vector", [],
                           (lambda e, Tss=Tss: e.tensor_scalar(
                               srow[:, 1152:1152 + Tss], srow[:, 1152:1152 + Tss],
                               EPS, None, ALU.add)),
                           inc="vv", chain=True)
            _taps = []
            emit("scalar", W("scalar", ("vv", rows2_v), *_taps),
                 (lambda e, Tss=Tss: e.activation(srow[:, 768:768 + Tss],
                                                  srow[:, 1152:1152 + Tss], AF.Ln)),
                 inc="ss")
            rr = emit("scalar", [],
                      (lambda e, Tss=Tss: e.activation(srow[:, 384:384 + Tss],
                                                       srow[:, 768:768 + Tss], AF.Exp,
                                                       scale=-0.5)),
                      inc="ss", chain=True)
            bb = emit("vector", W("vector", ("ss", rr)),
                      (lambda e, Tss=Tss: e.scalar_tensor_tensor(
                          srow[:, 1152:1152 + Tss], srow[:, 0:Tss], -1.0,
                          srow[:, 384:384 + Tss], ALU.mult, ALU.mult)),
                      inc="vv", chain=True)
            bc1 = emit("tensor", W("tensor", ("ss", rr)),
                       (lambda e, Tss=Tss: e.matmul(pAb[:, 0:Tss], ones_sb[0:1, :],
                                                    srow[0:1, 384:384 + Tss],
                                                    start=True, stop=True)),
                       inc="tt")
            bc2 = emit("tensor", W("tensor", ("vv", bb)),
                       (lambda e, Tss=Tss: e.matmul(pBb[:, 0:Tss], ones_sb[0:1, :],
                                                    srow[0:1, 1152:1152 + Tss],
                                                    start=True, stop=True)),
                       inc="tt")
            ev = 0
            for dc in range(8):
                emit("vector", W("vector", ("tt", bc1), ("tt", bc2),
                                 *( [out_dma[fi - 1]] if fi >= 1 else [] )),
                     (lambda e, dc=dc, f0=f0, Tss=Tss: e.tensor_tensor(
                         sqf[dc % 2][:, 0:Tss], x1v[:, dc, f0:f0 + Tss],
                         pAb[:, 0:Tss], ALU.mult)),
                     inc="vv")
                ev = emit("vector", [],
                          (lambda e, dc=dc, Tss=Tss: e.tensor_tensor(
                              outsb[:, dc, 0:Tss], sqf[dc % 2][:, 0:Tss],
                              pBb[:, 0:Tss], ALU.add)),
                          inc="vv", chain=True)
            prev_out_evac = ev
            prev_ln2rows_vv = ev
            dm = dmae("gpsimd", W("gpsimd", ("vv", ev)),
                      (lambda e, f0=f0, Tss=Tss: e.dma_start(
                          out=out_d[:, f0:f0 + Tss].rearrange("(c p) t -> p c t", p=128),
                          in_=outsb[:, :, 0:Tss])))
            out_dma.append(dm)

        finals = [(L, cnt[L]) for L in GP_LANES if cnt[L] > 0]
        def _final(e, finals=finals):
            for L, v in finals:
                e.wait_ge(sems[L], v)
        ops["gpsimd"].append(_final)

        @block.sync
        def _(eng):
            for fn in ops["sync"]:
                fn(eng)

        @block.gpsimd
        def _(eng):
            for fn in ops["gpsimd"]:
                fn(eng)

        @block.tensor
        def _(eng):
            for fn in ops["tensor"]:
                fn(eng)

        @block.vector
        def _(eng):
            for fn in ops["vector"]:
                fn(eng)

        @block.scalar
        def _(eng):
            for fn in ops["scalar"]:
                fn(eng)

    return nc


_NC_CACHE = None


def kernel(L0_attn_x, L0_attn_Wq, L0_attn_bq, L0_attn_Wk, L0_attn_bk,
           L0_attn_Wv, L0_attn_bv, L0_attn_Wo, L0_attn_bo,
           L0_ffn_w_gate, L0_ffn_w_up, L0_ffn_w_down):
    global _NC_CACHE
    x = np.ascontiguousarray(np.asarray(L0_attn_x, dtype=np.float32))
    B, L, _ = x.shape

    wqT = np.ascontiguousarray(np.asarray(L0_attn_Wq, np.float32).T)
    wkT = np.ascontiguousarray(np.asarray(L0_attn_Wk, np.float32).T)
    wvT = np.ascontiguousarray(np.asarray(L0_attn_Wv, np.float32).T)
    woT = np.ascontiguousarray(np.asarray(L0_attn_Wo, np.float32).T)
    biasm = np.zeros((128, 24), np.float32)
    biasm[:, 0:8] = np.asarray(L0_attn_bq, np.float32).reshape(8, 128).T
    biasm[:, 8:16] = np.asarray(L0_attn_bk, np.float32).reshape(8, 128).T
    biasm[:, 16:24] = np.asarray(L0_attn_bo, np.float32).reshape(8, 128).T
    bvrow = np.ascontiguousarray(np.asarray(L0_attn_bv, np.float32).reshape(1, D))
    wgTi = np.ascontiguousarray(np.asarray(L0_ffn_w_gate, np.int32).T)
    wuTi = np.ascontiguousarray(np.asarray(L0_ffn_w_up, np.int32).T)
    wdTi = np.ascontiguousarray(np.asarray(L0_ffn_w_down, np.int32).T)

    ii = np.arange(96)
    mLm = (ii[:, None] // 16 == ii[None, :] // 16).astype(np.float32)
    mAm = ((ii[:, None] // 48 == ii[None, :] // 48)
           & (ii[:, None] % 3 == ii[None, :] % 3)).astype(np.float32)
    masks_np = np.ascontiguousarray(np.concatenate([mLm, mAm], axis=1))
    identc = np.zeros((128, 256), np.float32)
    identc[:, 0:128] = np.eye(128, dtype=np.float32)
    identc[:, 128:256] = 1.0

    in_maps = []
    for c in range(8):
        s, h = c // 2, c % 2
        p0 = h * NTOK
        xs = np.zeros((NTOK, D), np.float32)
        n_real = min(L - p0, NTOK)
        xs[:n_real] = x[s, p0:p0 + n_real]
        in_maps.append(dict(xT=np.ascontiguousarray(xs.T), wqT=wqT, wkT=wkT,
                            wvT=wvT, woT=woT, biasm=biasm, bvrow=bvrow,
                            masks=masks_np, identc=identc,
                            wgTi=wgTi, wuTi=wuTi, wdTi=wdTi))

    if _NC_CACHE is None:
        _NC_CACHE = build_program()
    res = run_bass_kernel_spmd(_NC_CACHE, in_maps, core_ids=list(range(8)))

    out = np.empty((B, L, D), np.float32)
    for c in range(8):
        s, h = c // 2, c % 2
        p0 = h * NTOK
        n_real = min(L - p0, NTOK)
        oT = np.asarray(res.results[c]["out"])
        out[s, p0:p0 + n_real] = oT.T[:n_real]
    return out


# version 29 (final)
# speedup vs baseline: 1.0459x; 1.0056x over previous
import numpy as np
import contextlib
import concourse.bass as bass
import concourse.mybir as mybir
from concourse.bass_utils import run_bass_kernel_spmd

F32 = mybir.dt.float32
F32R = mybir.dt.float32r
I32 = mybir.dt.int32
AF = mybir.ActivationFunctionType
ALU = mybir.AluOpType
AX = mybir.AxisListType

D = 1024
NTOK = 2064
SLABS = [(0, 384), (384, 384), (768, 384), (1152, 384), (1536, 384), (1920, 144)]
FSS = [(i * 256, 256) for i in range(8)] + [(2048, 16)]
CVELT = 8193.0
S2 = 1.0 / 16129.0
QSCALE = 0.125
EPS = 1e-5
DEBUG_TAPS = False


def tiles_of(T):
    out, b = [], 0
    while b < T:
        tt = 96 if T - b >= 96 else T - b
        out.append((b, tt))
        b += tt
    return out


def build_program():
    nc = bass.Bass("TRN2", target_bir_lowering=False, detect_race_conditions=False,
                   dynamic_dma_scratch_size=16384)

    xT = nc.dram_tensor("xT", [D, NTOK], F32, kind="ExternalInput").ap()
    wqT = nc.dram_tensor("wqT", [D, D], F32, kind="ExternalInput").ap()
    wkT = nc.dram_tensor("wkT", [D, D], F32, kind="ExternalInput").ap()
    wvT = nc.dram_tensor("wvT", [D, D], F32, kind="ExternalInput").ap()
    woT = nc.dram_tensor("woT", [D, D], F32, kind="ExternalInput").ap()
    biasm = nc.dram_tensor("biasm", [128, 24], F32, kind="ExternalInput").ap()
    bvrow = nc.dram_tensor("bvrow", [1, D], F32, kind="ExternalInput").ap()
    masks = nc.dram_tensor("masks", [96, 192], F32, kind="ExternalInput").ap()
    identc = nc.dram_tensor("identc", [128, 256], F32, kind="ExternalInput").ap()
    wgTi = nc.dram_tensor("wgTi", [D, 3072], I32, kind="ExternalInput").ap()
    wuTi = nc.dram_tensor("wuTi", [D, 3072], I32, kind="ExternalInput").ap()
    wdTi = nc.dram_tensor("wdTi", [3072, D], I32, kind="ExternalInput").ap()
    out_d = nc.dram_tensor("out", [D, NTOK], F32, kind="ExternalOutput").ap()
    wg_r = nc.dram_tensor("wg_r", [24, D, 128], F32R, kind="Internal").ap()
    wu_r = nc.dram_tensor("wu_r", [24, D, 128], F32R, kind="Internal").ap()
    wd_r = nc.dram_tensor("wd_r", [3072, D], F32R, kind="Internal").ap()
    wg_rt = nc.dram_tensor("wg_rt", [D, 3072], F32R, kind="Internal").ap()
    wu_rt = nc.dram_tensor("wu_rt", [D, 3072], F32R, kind="Internal").ap()
    ops = {k: [] for k in ("sync", "gpsimd", "tensor", "vector", "scalar")}
    SYNC_LANES = ["q%d" % i for i in range(6)]
    GP_LANES = ["g%d" % i for i in range(3)]
    cnt = {"tt": 0, "vv": 0, "ss": 0}
    for nm_ in SYNC_LANES + GP_LANES + ["gc"]:
        cnt[nm_] = 0
    lane_rr = {"sync": 0, "gpsimd": 0}
    seen = {k: {} for k in ops}
    sems = {}

    def W(eng, *waits):
        w = []
        for name, val in waits:
            if val <= 0 or seen[eng].get(name, -1) >= val:
                continue
            seen[eng][name] = val
            w.append((name, val))
        return w

    def emit(eng, waits, build, inc=None, inc_amt=1, chain=False):
        if chain:
            own = {"vector": "vv", "scalar": "ss", "tensor": "tt",
                   "gpsimd": "gc"}[eng]
            waits = list(waits) + W(eng, (own, cnt[own]))
        def fn(e, waits=waits, build=build, inc=inc, inc_amt=inc_amt):
            for name, val in waits:
                e.wait_ge(sems[name], val)
            ins = build(e)
            if inc is not None:
                ins.then_inc(sems[inc], inc_amt)
        ops[eng].append(fn)
        if inc is not None:
            cnt[inc] += inc_amt
            return cnt[inc]
        return None

    def dmae(eng, waits, build):
        lanes = SYNC_LANES if eng == "sync" else GP_LANES
        L = lanes[lane_rr[eng] % len(lanes)]
        lane_rr[eng] += 1
        waits = list(waits) + W(eng, (L, cnt[L]))
        val = emit(eng, waits, build, inc=L, inc_amt=16)
        return (L, val)

    es = contextlib.ExitStack()
    with es:
        sb = lambda name, shape, dt=F32: es.enter_context(nc.sbuf_tensor(name, shape, dt))
        x1f = sb("x1f", [128, 8 * NTOK])
        qsb = sb("qsb", [128, 8 * 384])
        ksb = sb("ksb", [128, 8 * 384])
        vsb = sb("vsb", [128, 4 * 1024])
        laT = sb("laT", [128, 8 * 384])
        arena_e = sb("arena_e", [128, 3840])
        wbuf = sb("wbuf", [128, 4 * 1024])
        x1hT = sb("x1hT", [128, 2048], F32R)
        x1lT = sb("x1lT", [128, 2048], F32R)
        hhT = sb("hhT", [128, 768], F32R)
        hlT = sb("hlT", [128, 768], F32R)
        wgfT = sb("wgfT", [128, 1024], F32R)
        wufT = sb("wufT", [128, 1024], F32R)
        wtail = sb("wtail", [128, 1024], F32R)
        hTh = sb("hTh", [128, 384], F32R)
        hTl = sb("hTl", [128, 384], F32R)
        wdfT = sb("wdfT", [128, 2048], F32R)
        id_sb = sb("id_sb", [128, 128])
        ones_sb = sb("ones_sb", [128, 128])
        mask_sb = sb("mask_sb", [96, 192])
        bias_sb = sb("bias_sb", [128, 32])
        bv_sb = sb("bv_sb", [1, D])
        nm_sb = sb("nm_sb", [96, 2])
        sums_sb = sb("sums_sb", [96, 64])
        srow = sb("srow", [1, 1536])

        x1v = x1f[:].rearrange("p (c t) -> p c t", c=8)
        lav = laT[:].rearrange("p (c t) -> p c t", c=8)
        mL = mask_sb[:, 0:96]
        mA = mask_sb[:, 96:192]

        esb = arena_e[0:96, 0:1536].rearrange("p (s x) -> p s x", s=16)
        eAb = arena_e[0:96, 1536:3072].rearrange("p (s x) -> p s x", s=16)
        psbv = esb
        ptsb8 = arena_e[0:96, 3072:3840].rearrange("p (g x) -> p g x", g=8)
        ytv = arena_e[:, 0:3072].rearrange("p (c t) -> p c t", c=8)
        sqscr = [arena_e[:, 3072 + 384 * i: 3072 + 384 * (i + 1)] for i in range(2)]
        sqf = [qsb[:, 0:256], qsb[:, 256:512]]
        x1h = x1hT[:].rearrange("p (c t) -> p c t", c=8)
        x1l = x1lT[:].rearrange("p (c t) -> p c t", c=8)
        x1h_f = x1hT[:].bitcast(F32).rearrange("p (c t) -> p c t", c=8)

        wslot = [wbuf[:, 1024 * i: 1024 * (i + 1)].rearrange("p (c o) -> p c o", c=8)
                 for i in range(4)]
        wg2s = [wgfT[:].rearrange("p (c j) -> p c j", c=8)]
        wu2s = [wufT[:].rearrange("p (c j) -> p c j", c=8)]
        wd2s = [wdfT[:, 1024 * i:1024 * (i + 1)] for i in range(2)]
        hscr = [qsb[:, 512 + 256 * i: 512 + 256 * (i + 1)] for i in range(3)]
        hful = [qsb[:, 1280 + 256 * i: 1280 + 256 * (i + 1)] for i in range(3)]
        uh = [qsb[:, 2048 + 256 * i: 2048 + 256 * (i + 1)] for i in range(3)]
        hh = [hhT[:, 256 * i: 256 * (i + 1)] for i in range(3)]
        hl = [hlT[:, 256 * i: 256 * (i + 1)] for i in range(3)]
        tscr = vsb[:, 0:2048].rearrange("p (c t) -> p c t", c=8)
        uscr = vsb[:, 2048:4096].rearrange("p (c t) -> p c t", c=8)
        outsb = laT[:, 0:2048].rearrange("p (c t) -> p c t", c=8)

        pBIG = es.enter_context(nc.psum_tensor("pBIG", [128, 2048], F32))
        pB4 = es.enter_context(nc.psum_tensor("pB4", [128, 512], F32))
        pB5 = es.enter_context(nc.psum_tensor("pB5", [128, 512], F32))
        pB67 = es.enter_context(nc.psum_tensor("pB67", [128, 1024], F32))
        pq = [pBIG[:, 0:512], pBIG[:, 512:1024]]
        pv = [pBIG[0:96, 1024:1536], pBIG[0:96, 1536:2048]]
        psv = pBIG[0:96, 0:1024].rearrange("p (s x) -> p s x", s=8)
        SLOT2HEAD = [0, 2, 4, 6, 1, 3, 5, 7, 8, 10, 12, 14, 9, 11, 13, 15]
        po = pq
        pstat1 = pBIG[0:1, 1024:1504]
        pstat2 = pBIG[0:1, 1536:2016]
        ptp = [pB4[:].rearrange("p (i x) -> p i x", i=4),
               pB5[:].rearrange("p (i x) -> p i x", i=4)]
        pfav = pB67[:].rearrange("p (c x) -> p c x", c=8)
        pAb = pB4
        pBb = pB5
        pg = [pBIG[:, 0:512], pBIG[:, 512:1024]]
        pu = [pBIG[:, 1024:1536], pBIG[:, 1536:2048]]

        def pd_slice(dc, Tss):
            if dc < 2:
                return pB4[:, 256 * dc: 256 * dc + Tss]
            if dc < 4:
                return pB5[:, 256 * (dc - 2): 256 * (dc - 2) + Tss]
            return pB67[:, 256 * (dc - 4): 256 * (dc - 4) + Tss]
        pstatA = pB67[0:1, 0:256]
        pstatB = pB67[0:1, 512:768]

        for name in ["tt", "vv", "ss", "gc"] + SYNC_LANES + GP_LANES:
            sems[name] = es.enter_context(nc.semaphore(name))
        block = es.enter_context(nc.Block())

        consts_ld = []
        for dst, src in ((id_sb[:], identc[:, 0:128]), (ones_sb[:], identc[:, 128:256]),
                         (mask_sb[:], masks), (bias_sb[:, 0:24], biasm), (bv_sb[:], bvrow)):
            consts_ld.append(dmae("sync", [],
                                  (lambda e, d=dst, s=src: e.dma_start(out=d, in_=s))))

        bqs_prep = None
        x_lds = []
        for t0, T in ((0, 1032), (1032, 1032)):
            x_lds.append(dmae("sync", [],
                              (lambda e, t0=t0, T=T: e.dma_start(
                                  out=x1v[:, :, t0:t0 + T],
                                  in_=xT[:, t0:t0 + T].rearrange("(c p) t -> p c t", p=128)))))

        bqs_prep = emit("vector", W("vector", *consts_ld),
                        (lambda e: e.tensor_scalar(bias_sb[:, 24:32], bias_sb[:, 0:8],
                                                   QSCALE, None, ALU.mult)),
                        inc="vv")

        conv_toks = []
        for wsrc, wdst in ((wgTi, wg_r), (wuTi, wu_r)):
            for hc in range(24):
                conv_toks.append(dmae("gpsimd", [],
                                      (lambda e, s=wsrc, d=wdst, hc=hc: e.dma_start(
                                          out=d[hc, :, :],
                                          in_=s[:, 128 * hc:128 * (hc + 1)]))))
        for qtr in range(4):
            conv_toks.append(dmae("gpsimd", [],
                                  (lambda e, qtr=qtr: e.dma_start(
                                      out=wd_r[768 * qtr:768 * (qtr + 1), :],
                                      in_=wdTi[768 * qtr:768 * (qtr + 1), :]))))
        conv_all_done = [(L, cnt[L]) for L in GP_LANES]
        for wsrc, wdst in ((wgTi, wg_rt), (wuTi, wu_rt)):
            for qtr in range(4):
                conv_toks.append(dmae("gpsimd", [],
                                      (lambda e, s=wsrc, d=wdst, qtr=qtr:
                                       e.dma_start(
                                           out=d[256 * qtr:256 * (qtr + 1), :],
                                           in_=s[256 * qtr:256 * (qtr + 1), :]))))
        convt_all_done = [(L, cnt[L]) for L in GP_LANES]

        wslot_last_use = [0, 0, 0, 0]
        evac_qk = [0, 0]
        prev_exp_ss = 0
        prev_nmax_vv = 0
        prev_slab_rows_vv = 0
        prev_apply_vv = 0
        prev_pfa_evac_vv = 0
        prev_ptsb_evac = [0, 0, 0, 0]
        av_groupslot = [0, 0]
        prev_tile_trans_tt = 0
        prev_yt_evac = [0, 0]

        def emit_qk_chunk(name, wT, outb, bcol, do_scale, oc, jt0, jT):
            st_ = qkstate
            ldc, slot_j = load_eighth_g(wT, oc, st_["nq"] % 4), st_["nq"] % 4
            st_["nq"] += 1
            bank = oc % 2
            wts = W("tensor", ldc, *x_lds) + W("tensor", ("ss", evac_qk[bank]),
                    ("vv", prev_yt_evac[bank]), ("vv", prev_nmax_vv),
                    ("ss", prev_exp_ss))
            last = None
            for k in range(8):
                last = emit("tensor", wts if k == 0 else [],
                            (lambda e, j=slot_j, k=k, b=bank, T=jT, t0=jt0:
                             e.matmul(pq[b][:, 0:T],
                                      wslot[j][:, k, 0:128],
                                      x1v[:, k, t0:t0 + T],
                                      start=(k == 0), stop=(k == 7))),
                            inc="tt" if k == 7 else None)
            wslot_last_use[slot_j] = last
            ev = emit("scalar", W("scalar", ("tt", last), ("vv", bqs_prep)),
                      (lambda e, oc=oc, b=bank, T=jT, outb=outb, bcol=bcol,
                              sc=do_scale:
                       e.activation(outb[:, 384 * oc:384 * oc + T],
                                    pq[b][:, 0:T], AF.Identity,
                                    bias=bias_sb[:, (24 if sc else bcol) + oc:
                                                 (24 if sc else bcol) + oc + 1],
                                    scale=QSCALE if sc else 1.0)),
                      inc="ss")
            evac_qk[bank] = ev
            qkstate["done"] = ev

        def qk_jobs_for(t0_, T_):
            return ([("q", wqT, qsb, 0, True, oc, t0_, T_) for oc in range(8)] +
                    [("k", wkT, ksb, 8, False, oc, t0_, T_) for oc in range(8)])

        qkstate = {"nq": 0, "done": 0}
        load_eighth_g = None
        pending_qk = qk_jobs_for(*SLABS[0])

        for si, (t0, T) in enumerate(SLABS):
            TILES = tiles_of(T)

            def load_eighth(wT, oc, slot_j):
                return dmae("sync", W("sync", ("tt", wslot_last_use[slot_j])),
                            (lambda e, wT=wT, oc=oc, j=slot_j: e.dma_start(
                                out=wslot[j][:, :, :],
                                in_=wT[:, 128 * oc:128 * (oc + 1)].rearrange(
                                    "(c p) o -> p c o", p=128))))

            load_eighth_g = load_eighth
            for job in pending_qk:
                emit_qk_chunk(*job)
            qk_evac_done = qkstate["done"]
            pending_next = (qk_jobs_for(*SLABS[si + 1]) if si + 1 < len(SLABS)
                            else [])

            def emit_v_tile(ti):
                nonlocal_state = None
                tb, TT = TILES[ti]
                for ot in range(8):
                    ldc, slot_j = load_eighth(wvT, ot, qkstate["nq"] % 4), qkstate["nq"] % 4
                    qkstate["nq"] += 1
                    b = vstate["bank"]
                    wts = W("tensor", ldc,
                            ("ss", pv_rd[b]), ("vv", prev_slab_rows_vv))
                    for k in range(8):
                        emit("tensor", wts if k == 0 else [],
                             (lambda e, j=slot_j, k=k, tb=tb, TT=TT, b=b, t0=t0:
                              e.matmul(pv[b][0:TT, 0:128],
                                       x1v[:, k, t0 + tb:t0 + tb + TT],
                                       wslot[j][:, k, 0:128],
                                       start=(k == 0), stop=False)),
                             inc=None)
                    last = emit("tensor", [],
                                (lambda e, ot=ot, tb=tb, TT=TT, b=b:
                                 e.matmul(pv[b][0:TT, 0:128],
                                          ones_sb[0:1, 0:TT],
                                          bv_sb[0:1, 128 * ot:128 * (ot + 1)],
                                          start=False, stop=True)),
                                inc="tt")
                    wslot_last_use[slot_j] = last
                    ev = emit("scalar", W("scalar", ("tt", last)),
                              (lambda e, ti=ti, ot=ot, TT=TT, b=b:
                               e.copy(vsb[0:TT, 1024 * ti + 128 * ot:1024 * ti + 128 * (ot + 1)],
                                      pv[b][0:TT, 0:128])),
                              inc="ss")
                    pv_rd[b] = ev
                    v_evac_tile[ti] = ev
                    vstate["bank"] ^= 1

            vstate = {"bank": 0}
            v_evac_tile = {}
            pv_rd = [0, 0]
            emit_v_tile(0)

            for ti, (tb, TT) in enumerate(TILES):
                for half in range(2):
                    first_wts = W("tensor", ("ss", qk_evac_done),
                                  ("ss", prev_exp_ss), ("vv", prev_nmax_vv))
                    sc_done = 0
                    order = [0, 4, 1, 5, 2, 6, 3, 7]
                    for oi, sl8 in enumerate(order):
                        s = 8 * half + sl8
                        h = SLOT2HEAD[s]
                        pp = h % 2
                        c = h // 2
                        pos = sl8 % 4
                        sc_done = emit(
                            "tensor", first_wts if oi == 0 else [],
                            (lambda e, sl8=sl8, pp=pp, c=c, tb=tb, TT=TT,
                                    st=(pos == 0), sp=(pos == 3):
                             e.matmul(psv[0:TT, sl8, 0:TT],
                                      qsb[64 * pp:64 * pp + 64, 384 * c + tb:384 * c + tb + TT],
                                      ksb[64 * pp:64 * pp + 64, 384 * c + tb:384 * c + tb + TT],
                                      start=st, stop=sp,
                                      tile_position=(64 * pp, 0))),
                            inc="tt")
                    nm = emit("vector", W("vector", ("tt", sc_done)),
                              (lambda e, TT=TT, half=half: e.tensor_reduce(
                                  nm_sb[0:TT, half:half + 1], psv[0:TT, :, 0:TT],
                                  AX.XY, ALU.max, negate=True)),
                              inc="vv")
                    prev_nmax_vv = nm
                    ex = emit("scalar", W("scalar", ("vv", nm),
                                          ("tt", prev_tile_trans_tt)),
                              (lambda e, TT=TT, half=half: e.activation(
                                  esb[0:TT, 8 * half:8 * half + 8, 0:TT],
                                  psv[0:TT, :, 0:TT], AF.Exp,
                                  bias=nm_sb[0:TT, half:half + 1], scale=1.0)),
                              inc="ss")
                    prev_exp_ss = ex
                if ti + 1 < len(TILES):
                    emit_v_tile(ti + 1)
                emit("vector", W("vector", ("ss", prev_exp_ss)),
                     (lambda e, TT=TT: e.tensor_tensor(
                         eAb[0:TT, :, 0:TT], esb[0:TT, :, 0:TT],
                         mA[0:TT, 0:TT].unsqueeze(1).broadcast_to((TT, 16, TT)),
                         ALU.mult)),
                     inc="vv")
                emit("vector", [],
                     (lambda e, TT=TT: e.tensor_tensor(
                         esb[0:TT, :, 0:TT], esb[0:TT, :, 0:TT],
                         mL[0:TT, 0:TT].unsqueeze(1).broadcast_to((TT, 16, TT)),
                         ALU.mult)),
                     inc="vv")
                emit("vector", [],
                     (lambda e, TT=TT: e.tensor_reduce(
                         sums_sb[0:TT, 0:16], esb[0:TT, :, 0:TT], AX.X, ALU.add)),
                     inc="vv")
                emit("vector", [],
                     (lambda e, TT=TT: e.tensor_reduce(
                         sums_sb[0:TT, 16:32], eAb[0:TT, :, 0:TT], AX.X, ALU.add)),
                     inc="vv")
                emit("vector", [],
                     (lambda e, TT=TT: e.reciprocal(sums_sb[0:TT, 32:48],
                                                    sums_sb[0:TT, 0:16])),
                     inc="vv", chain=True)
                emit("vector", [],
                     (lambda e, TT=TT: e.reciprocal(sums_sb[0:TT, 48:64],
                                                    sums_sb[0:TT, 16:32])),
                     inc="vv", chain=True)
                p_done = {}
                for s in range(16):
                    emit("vector", [],
                         (lambda e, s=s, TT=TT: e.tensor_scalar(
                             psbv[0:TT, s, 0:TT], esb[0:TT, s, 0:TT],
                             sums_sb[0:TT, 32 + s:33 + s], None, ALU.mult)),
                         inc="vv", chain=(s == 0))
                for s in range(16):
                    p_done[s] = emit("vector", [],
                                     (lambda e, s=s, TT=TT: e.scalar_tensor_tensor(
                                         psbv[0:TT, s, 0:TT], eAb[0:TT, s, 0:TT],
                                         sums_sb[0:TT, 48 + s:49 + s],
                                         psbv[0:TT, s, 0:TT], ALU.mult, ALU.add)),
                                     inc="vv")
                av_first_wts = W("tensor", ("vv", prev_pfa_evac_vv))
                trans_last = 0
                started = set()
                last_touch = {}
                for s in range(16):
                    h = SLOT2HEAD[s]
                    last_touch[(6 + (h // 2) // 4, h % 2)] = s
                evac_group = {}
                for g in range(4):
                    par = g % 2
                    gs = g % 2
                    for i in range(4):
                        s = 4 * g + i
                        wts = W("tensor", ("vv", p_done[s]),
                                ("tt", av_groupslot[gs]))
                        trans_last = emit(
                            "tensor", wts,
                            (lambda e, s=s, par=par, i=i, TT=TT: e.transpose(
                                ptp[par][0:96, i, 0:TT], psbv[0:TT, s, 0:96],
                                id_sb[0:TT, 0:TT])),
                            inc="tt")
                    ev = emit("vector", W("vector", ("tt", trans_last)),
                              (lambda e, gs=gs, par=par, TT=TT:
                               e.tensor_copy(ptsb8[0:TT, 4 * gs:4 * gs + 4, 0:TT],
                                             ptp[par][0:TT, :, 0:TT])),
                              inc="vv")
                    prev_ptsb_evac[gs] = ev
                    evac_group[g] = ev
                    av_last = 0
                    for i in range(4):
                        s = 4 * g + i
                        h = SLOT2HEAD[s]
                        pp = h % 2
                        c = h // 2
                        key = (6 + c // 4, pp)
                        startb = key not in started
                        started.add(key)
                        stopb = (last_touch[key] == s)
                        wts = W("tensor", ("vv", ev), ("ss", v_evac_tile[ti])) + (
                            av_first_wts if s == 0 else [])
                        av_last = emit("tensor", wts,
                             (lambda e, s=s, h=h, pp=pp, c=c, ti=ti, TT=TT, gs=gs, i=i,
                                     stb=startb, spb=stopb:
                              e.matmul(pfav[64 * pp:64 * pp + 64, c, 0:TT],
                                       vsb[0:TT, 1024 * ti + 64 * h:1024 * ti + 64 * h + 64],
                                       ptsb8[0:TT, 4 * gs + i, 0:TT],
                                       start=stb, stop=spb,
                                       tile_position=(0, 64 * pp))),
                             inc="tt")
                    av_groupslot[gs] = av_last
                    pass
                prev_tile_trans_tt = trans_last
                av_all = cnt["tt"]
                prev_pfa_evac_vv = emit(
                    "vector", W("vector", ("tt", av_all)),
                    (lambda e, tb=tb, TT=TT: e.tensor_copy(
                        lav[:, :, tb:tb + TT], pfav[:, :, 0:TT])),
                    inc="vv")

            pending_qk = pending_next
            o_first_wts = W("tensor", ("vv", prev_pfa_evac_vv),
                            ("ss", prev_exp_ss), ("vv", prev_nmax_vv))
            yt_done = 0
            for oc in range(8):
                ldc, slot_j = load_eighth(woT, oc, qkstate["nq"] % 4), qkstate["nq"] % 4
                qkstate["nq"] += 1
                bank = oc % 2
                wts = W("tensor", ldc, ("ss", qkstate["done"]),
                        ("vv", prev_yt_evac[bank])) + (o_first_wts if oc == 0 else [])
                last = None
                for k in range(8):
                    last = emit("tensor", wts if k == 0 else [],
                                (lambda e, j=slot_j, k=k, b=bank, T=T:
                                 e.matmul(po[b][:, 0:T],
                                          wslot[j][:, k, 0:128],
                                          lav[:, k, 0:T],
                                          start=(k == 0), stop=(k == 7))),
                                inc="tt" if k == 7 else None)
                wslot_last_use[slot_j] = last
                ev = emit("vector", W("vector", ("tt", last)),
                          (lambda e, oc=oc, b=bank, T=T, t0=t0: e.scalar_tensor_tensor(
                              ytv[:, oc, 0:T], po[b][:, 0:T],
                              bias_sb[:, 16 + oc:17 + oc], x1v[:, oc, t0:t0 + T],
                              ALU.add, ALU.add)),
                          inc="vv")
                prev_yt_evac[bank] = ev
                yt_done = ev
            sq_tt = [0, 0]
            st1 = st2 = 0
            for oc in range(8):
                sl = oc % 2
                sq = emit("scalar", W("scalar", ("vv", prev_yt_evac[oc % 2]),
                                      ("tt", sq_tt[sl])),
                          (lambda e, oc=oc, sl=sl, T=T: e.activation(
                              sqscr[sl][:, 0:T], ytv[:, oc, 0:T], AF.Square)),
                          inc="ss")
                st1 = emit("tensor", W("tensor", ("vv", prev_yt_evac[oc % 2]),
                                       ("vv", prev_slab_rows_vv)),
                           (lambda e, oc=oc, T=T: e.matmul(
                               pstat1[:, 0:T], ones_sb[:, 0:1], ytv[:, oc, 0:T],
                               start=(oc == 0), stop=(oc == 7))),
                           inc="tt")
                st2 = emit("tensor", W("tensor", ("ss", sq)),
                           (lambda e, sl=sl, oc=oc, T=T: e.matmul(
                               pstat2[:, 0:T], ones_sb[:, 0:1], sqscr[sl][:, 0:T],
                               start=(oc == 0), stop=(oc == 7))),
                           inc="tt")
                sq_tt[sl] = st2
            emit("vector", W("vector", ("tt", st1), ("tt", st2)),
                 (lambda e, T=T: e.tensor_scalar(srow[:, 0:T], pstat1[:, 0:T],
                                                 1.0 / 1024.0, None, ALU.mult)),
                 inc="vv")
            emit("vector", [],
                 (lambda e, T=T: e.tensor_scalar(srow[:, 384:384 + T], pstat2[:, 0:T],
                                                 1.0 / 1024.0, None, ALU.mult)),
                 inc="vv")
            emit("vector", [],
                 (lambda e, T=T: e.tensor_tensor(srow[:, 768:768 + T], srow[:, 0:T],
                                                 srow[:, 0:T], ALU.mult)),
                 inc="vv", chain=True)
            emit("vector", [],
                 (lambda e, T=T: e.tensor_tensor(
                     srow[:, 1152:1152 + T], srow[:, 384:384 + T],
                     srow[:, 768:768 + T], ALU.subtract)),
                 inc="vv", chain=True)
            rows_v = emit("vector", [],
                          (lambda e, T=T: e.tensor_scalar(
                              srow[:, 1152:1152 + T], srow[:, 1152:1152 + T],
                              EPS, None, ALU.add)),
                          inc="vv", chain=True)
            emit("scalar", W("scalar", ("vv", rows_v)),
                 (lambda e, T=T: e.activation(srow[:, 768:768 + T],
                                              srow[:, 1152:1152 + T], AF.Ln)),
                 inc="ss")
            r_row = emit("scalar", [],
                         (lambda e, T=T: e.activation(srow[:, 384:384 + T],
                                                      srow[:, 768:768 + T], AF.Exp,
                                                      scale=-0.5)),
                         inc="ss", chain=True)
            b_row = emit("vector", W("vector", ("ss", r_row)),
                         (lambda e, T=T: e.scalar_tensor_tensor(
                             srow[:, 1152:1152 + T], srow[:, 0:T], -1.0,
                             srow[:, 384:384 + T], ALU.mult, ALU.mult)),
                         inc="vv", chain=True)
            bc1 = emit("tensor", W("tensor", ("ss", r_row), ("vv", prev_apply_vv),
                                   ("vv", prev_ptsb_evac[0]), ("vv", prev_ptsb_evac[1]),
                                   ("vv", prev_ptsb_evac[2]), ("vv", prev_ptsb_evac[3])),
                       (lambda e, T=T: e.matmul(pAb[:, 0:T], ones_sb[0:1, :],
                                                srow[0:1, 384:384 + T],
                                                start=True, stop=True)),
                       inc="tt")
            bc2 = emit("tensor", W("tensor", ("vv", b_row)),
                       (lambda e, T=T: e.matmul(pBb[:, 0:T], ones_sb[0:1, :],
                                                srow[0:1, 1152:1152 + T],
                                                start=True, stop=True)),
                       inc="tt")
            ap_v = 0
            for oc in range(8):
                sl = oc % 2
                emit("vector", W("vector", ("tt", bc1), ("tt", bc2)),
                     (lambda e, oc=oc, sl=sl, T=T: e.tensor_tensor(
                         sqscr[sl][:, 0:T], ytv[:, oc, 0:T], pAb[:, 0:T], ALU.mult)),
                     inc="vv")
                ap_v = emit("vector", [],
                            (lambda e, oc=oc, sl=sl, T=T, t0=t0: e.tensor_tensor(
                                x1v[:, oc, t0:t0 + T], sqscr[sl][:, 0:T], pBb[:, 0:T],
                                ALU.add)),
                            inc="vv", chain=True)
            prev_apply_vv = ap_v
            prev_slab_rows_vv = rows_v

        slab_done_vv = prev_apply_vv
        o_last_tt = cnt["tt"]
        wf_lduse = {"g": [o_last_tt, o_last_tt], "u": [o_last_tt, o_last_tt],
                    "d": [o_last_tt, o_last_tt]}
        prev_y2_evac = 0
        prev_ln2rows_vv = 0
        hbuf_lastmm = [o_last_tt, o_last_tt, o_last_tt]
        pgu_read = [[0, 0], [0, 0]]
        out_dma = []
        prev_out_evac = 0
        def emit_split(fi):
            f0, Tss = FSS[fi]
            emit("vector", [],
                 (lambda e, f0=f0, Tss=Tss: e.tensor_scalar(
                     tscr[:, :, 0:Tss], x1v[:, :, f0:f0 + Tss], CVELT, None, ALU.mult)),
                 inc="vv")
            emit("vector", [],
                 (lambda e, f0=f0, Tss=Tss: e.scalar_tensor_tensor(
                     uscr[:, :, 0:Tss], tscr[:, :, 0:Tss], 1.0,
                     x1v[:, :, f0:f0 + Tss], ALU.mult, ALU.subtract)),
                 inc="vv", chain=True)
            emit("vector", [],
                 (lambda e, Tss=Tss: e.scalar_tensor_tensor(
                     x1h[:, :, 0:Tss], tscr[:, :, 0:Tss], 1.0,
                     uscr[:, :, 0:Tss], ALU.mult, ALU.subtract)),
                 inc="vv", chain=True)
            return emit("vector", [],
                        (lambda e, f0=f0, Tss=Tss: e.tensor_tensor(
                            x1l[:, :, 0:Tss], x1v[:, :, f0:f0 + Tss],
                            x1h_f[:, :, 0:Tss], ALU.subtract)),
                        inc="vv", chain=True)

        def emit_tail_ffn():
            tf_vv = cnt["vv"]
            tf_ss = cnt["ss"]
            gtiles = [wgfT[:, 0:512], wgfT[:, 512:1024], wtail[:, 0:512]]
            utiles = [wufT[:, 0:512], wufT[:, 512:1024], wtail[:, 512:1024]]
            g_last = {0: wf_lduse["g"][0], 1: wf_lduse["g"][0], 2: 0}
            u_last = {0: wf_lduse["u"][0], 1: wf_lduse["u"][0], 2: 0}
            ldd_t = {0: wf_lduse["d"][0], 1: wf_lduse["d"][1]}
            g_sil = qsb[:, 2048:2560]
            hp_t = qsb[:, 1280:1792]
            hT_sb = qsb[:, 512:896]
            tsc = qsb[:, 896:960]
            usc = qsb[:, 960:1024]
            y_sc = qsb[:, 512:1536]
            ptr = [pB67[:, 256:272], pB67[:, 768:784]]
            silu_rd = [0, 0]
            mult_rd = [0, 0]
            tr_evac = {}
            mus = {}
            st_t = {"dn": 0, "first": True}

            def emit_td(c):
                for q in range(4):
                    hc = 4 * c + q
                    sp_ = hc % 2
                    trl = emit("tensor",
                               W("tensor", ("vv", mus[c]),
                                 ("vv", tr_evac.get(hc - 2, 0))),
                               (lambda e, q=q, sp_=sp_: e.transpose(
                                   ptr[sp_][:, 0:16],
                                   hp_t[0:16, 128 * q:128 * (q + 1)],
                                   id_sb[0:16, 0:16])),
                               inc="tt")
                    ev = emit("vector", W("vector", ("tt", trl)),
                              (lambda e, hc=hc, sp_=sp_: e.tensor_copy(
                                  hT_sb[:, 16 * hc:16 * hc + 16],
                                  ptr[sp_][:, 0:16])),
                              inc="vv")
                    tr_evac[hc] = ev
                c0 = 64 * c
                emit("vector", [],
                     (lambda e, c0=c0: e.tensor_scalar(
                         tsc[:, 0:64], hT_sb[:, c0:c0 + 64], CVELT, None,
                         ALU.mult)),
                     inc="vv", chain=True)
                emit("vector", [],
                     (lambda e, c0=c0: e.scalar_tensor_tensor(
                         usc[:, 0:64], tsc[:, 0:64], 1.0,
                         hT_sb[:, c0:c0 + 64], ALU.mult, ALU.subtract)),
                     inc="vv", chain=True)
                emit("vector", [],
                     (lambda e, c0=c0: e.scalar_tensor_tensor(
                         hTh[:, c0:c0 + 64], tsc[:, 0:64], 1.0,
                         usc[:, 0:64], ALU.mult, ALU.subtract)),
                     inc="vv", chain=True)
                spl = emit("vector", [],
                           (lambda e, c0=c0: e.tensor_tensor(
                               hTl[:, c0:c0 + 64], hT_sb[:, c0:c0 + 64],
                               hTh[:, c0:c0 + 64].bitcast(F32),
                               ALU.subtract)),
                           inc="vv", chain=True)
                for q in range(4):
                    hc = 4 * c + q
                    jd = hc % 2
                    ldd = dmae("gpsimd",
                               W("gpsimd", ("tt", ldd_t[jd]), *conv_all_done),
                               (lambda e, hc=hc, jd=jd: e.dma_start(
                                   out=wd2s[jd][:, :],
                                   in_=wd_r[128 * hc:128 * (hc + 1), :])))
                    dnl = None
                    for h in range(2):
                        for p_, hop in enumerate((hTh, hTl)):
                            dnl = emit(
                                "tensor",
                                (W("tensor", ldd, ("vv", spl))
                                 if (h == 0 and p_ == 0) else []),
                                (lambda e, hc=hc, jd=jd, h=h, hop=hop,
                                        st=(hc == 0 and p_ == 0),
                                        sp=(hc == 23 and p_ == 1): e.matmul(
                                    (pB4 if h == 0 else pB5)[0:16, 0:512],
                                    hop[:, 16 * hc:16 * hc + 16],
                                    wd2s[jd][:, 512 * h:512 * (h + 1)],
                                    start=st, stop=sp)),
                                inc="tt" if (h == 1 and p_ == 1) else None)
                    ldd_t[jd] = dnl
                    st_t["dn"] = dnl

            for c in range(6):
                par = c % 2
                gl = ul = None
                for k in range(8):
                    kp = k % 3
                    ldg = dmae("sync", W("sync", ("tt", g_last[kp]),
                                         *convt_all_done),
                               (lambda e, c=c, k=k, kp=kp: e.dma_start(
                                   out=gtiles[kp],
                                   in_=wg_rt[128 * k:128 * (k + 1),
                                             512 * c:512 * (c + 1)])))
                    for p_, xop in enumerate((x1h, x1l)):
                        wts = W("tensor", ldg) if p_ == 0 else []
                        if k == 0 and p_ == 0:
                            wts += W("tensor", ("ss", silu_rd[par]),
                                     ("vv", split_done_fi[8]))
                            if st_t["first"]:
                                wts += W("tensor", ("vv", tf_vv),
                                         ("ss", tf_ss))
                                st_t["first"] = False
                        gl = emit("tensor", wts,
                                  (lambda e, k=k, kp=kp, par=par, xop=xop,
                                          st=(k == 0 and p_ == 0),
                                          sp=(k == 7 and p_ == 1): e.matmul(
                                      pg[par][0:16, 0:512],
                                      xop[:, k, 0:16], gtiles[kp],
                                      start=st, stop=sp)),
                                  inc="tt" if p_ == 1 else None)
                    g_last[kp] = gl
                for k in range(8):
                    kp = k % 3
                    ldu = dmae("sync", W("sync", ("tt", u_last[kp]),
                                         *convt_all_done),
                               (lambda e, c=c, k=k, kp=kp: e.dma_start(
                                   out=utiles[kp],
                                   in_=wu_rt[128 * k:128 * (k + 1),
                                             512 * c:512 * (c + 1)])))
                    for p_, xop in enumerate((x1h, x1l)):
                        wts = W("tensor", ldu) if p_ == 0 else []
                        if k == 0 and p_ == 0:
                            wts += W("tensor", ("vv", mult_rd[par]))
                        ul = emit("tensor", wts,
                                  (lambda e, k=k, kp=kp, par=par, xop=xop,
                                          st=(k == 0 and p_ == 0),
                                          sp=(k == 7 and p_ == 1): e.matmul(
                                      pu[par][0:16, 0:512],
                                      xop[:, k, 0:16], utiles[kp],
                                      start=st, stop=sp)),
                                  inc="tt" if p_ == 1 else None)
                    u_last[kp] = ul
                if c >= 1:
                    emit_td(c - 1)
                sl = emit("scalar", W("scalar", ("tt", gl), ("vv", tf_vv)),
                          (lambda e, par=par: e.activation(
                              g_sil[0:16, 0:512], pg[par][0:16, 0:512],
                              AF.Silu, scale=1.0 / 127.0)),
                          inc="ss")
                silu_rd[par] = sl
                mu = emit("vector", W("vector", ("ss", sl), ("tt", ul)),
                          (lambda e, par=par: e.tensor_tensor(
                              hp_t[0:16, 0:512], g_sil[0:16, 0:512],
                              pu[par][0:16, 0:512], ALU.mult)),
                          inc="vv")
                mult_rd[par] = mu
                mus[c] = mu
            emit_td(5)
            ye = emit("vector", W("vector", ("tt", st_t["dn"])),
                      (lambda e: e.tensor_copy(y_sc[0:16, 0:512],
                                               pB4[0:16, 0:512])),
                      inc="vv")
            ye = emit("vector", [],
                      (lambda e: e.tensor_copy(y_sc[0:16, 512:1024],
                                               pB5[0:16, 0:512])),
                      inc="vv", chain=True)
            y2_ev = 0
            ytr = {}
            for dc in range(8):
                sp_ = dc % 2
                trl = emit("tensor",
                           W("tensor", ("vv", ye), ("vv", ytr.get(dc - 2, 0))),
                           (lambda e, dc=dc, sp_=sp_: e.transpose(
                               ptr[sp_][:, 0:16],
                               y_sc[0:16, 128 * dc:128 * (dc + 1)],
                               id_sb[0:16, 0:16])),
                           inc="tt")
                y2_ev = emit("vector", W("vector", ("tt", trl)),
                             (lambda e, dc=dc, sp_=sp_: e.scalar_tensor_tensor(
                                 x1v[:, dc, 2048:2064], ptr[sp_][:, 0:16], S2,
                                 x1v[:, dc, 2048:2064], ALU.mult, ALU.add)),
                             inc="vv", chain=True)
                ytr[dc] = y2_ev
            return y2_ev

        split_done_fi = {0: emit_split(0)}
        for fi, (f0, Tss) in enumerate(FSS):
            tail = (Tss < 256)
            split_done = split_done_fi[fi]
            dn_last = 0
            down_args = [None]
            def emit_down(hc_d, j_d, hsp_d, ldd_d, hc_w, tail, Tss):
                wts = W("tensor", ldd_d, ("vv", hsp_d), ("vv", slab_done_vv),
                        ("vv", prev_y2_evac), ("vv", prev_ln2rows_vv))
                hops = [hful[j_d]] if tail else [hh[j_d], hl[j_d]]
                nh = len(hops)
                dl = None
                for dc in range(8):
                    for pi, hop in enumerate(hops):
                        st = (hc_d == 0 and pi == 0 and dc % 2 == 0)
                        sp = (hc_d == 23 and pi == nh - 1 and dc % 2 == 1)
                        dl = emit(
                            "tensor", wts if (dc == 0 and pi == 0) else [],
                            (lambda e, hc_w=hc_w, dc=dc, hop=hop, Tss=Tss, st=st, sp=sp,
                                    tl=tail:
                             e.matmul(pd_slice(dc, Tss),
                                      wd2s[hc_w % 2][:, 128 * dc:128 * (dc + 1)].bitcast(F32)
                                      if tl else wd2s[hc_w % 2][:, 128 * dc:128 * (dc + 1)],
                                      hop[:, 0:Tss], start=st, stop=sp)),
                            inc="tt" if (dc == 7 and pi == nh - 1) else None)
                return dl

            y2_ev = emit_tail_ffn() if tail else 0
            pending = []
            for hc in ([] if tail else range(24)):
                j = hc % 3
                jd = hc % 2
                lg = dmae("sync", W("sync", ("tt", wf_lduse["g"][0]),
                                    *conv_all_done),
                          (lambda e, hc=hc: e.dma_start(
                              out=wg2s[0][:, :, :],
                              in_=wg_r[hc, :, :].rearrange("(c p) j -> p c j", p=128))))
                lu = dmae("sync", W("sync", ("tt", wf_lduse["u"][0])),
                          (lambda e, hc=hc: e.dma_start(
                              out=wu2s[0][:, :, :],
                              in_=wu_r[hc, :, :].rearrange("(c p) j -> p c j", p=128))))
                bank = hc % 2
                if tail:
                    xops = [None]
                else:
                    xops = [x1h, x1l]
                npass = len(xops)
                wts = W("tensor", lg, ("vv", split_done),
                        ("ss", prev_exp_ss), ("vv", prev_nmax_vv),
                        ("vv", prev_yt_evac[0]), ("vv", prev_yt_evac[1]),
                        ("ss", pgu_read[0][bank]))
                if tail:
                    wts += W("tensor", ("vv", slab_done_vv))
                gl = None
                for k in range(8):
                    for pi, xop in enumerate(xops):
                        first = (k == 0 and pi == 0)
                        lastm = (k == 7 and pi == npass - 1)
                        gl = emit("tensor", wts if first else [],
                                  (lambda e, j=j, k=k, xop=xop, b=bank, Tss=Tss, f0=f0,
                                          st=first, sp=lastm:
                                   e.matmul(pg[b][:, 0:Tss],
                                            wg2s[0][:, k, :].bitcast(F32) if xop is None
                                            else wg2s[0][:, k, :],
                                            x1v[:, k, f0:f0 + Tss] if xop is None
                                            else xop[:, k, 0:Tss],
                                            start=st, stop=sp)),
                                  inc="tt" if lastm else None)
                wf_lduse["g"][0] = gl
                wts = W("tensor", lu, ("vv", pgu_read[1][bank]))
                ul = None
                for k in range(8):
                    for pi, xop in enumerate(xops):
                        first = (k == 0 and pi == 0)
                        lastm = (k == 7 and pi == npass - 1)
                        ul = emit("tensor", wts if first else [],
                                  (lambda e, j=j, k=k, xop=xop, b=bank, Tss=Tss, f0=f0,
                                          st=first, sp=lastm:
                                   e.matmul(pu[b][:, 0:Tss],
                                            wu2s[0][:, k, :].bitcast(F32) if xop is None
                                            else wu2s[0][:, k, :],
                                            x1v[:, k, f0:f0 + Tss] if xop is None
                                            else xop[:, k, 0:Tss],
                                            start=st, stop=sp)),
                                  inc="tt" if lastm else None)
                wf_lduse["u"][0] = ul
                if len(pending) >= 2:
                    pd_args = pending.pop(0)
                    dn_last = emit_down(*pd_args, tail, Tss)
                    wf_lduse["d"][pd_args[4] % 2] = dn_last
                    hbuf_lastmm[pd_args[1]] = dn_last
                ldd = dmae("gpsimd", W("gpsimd", ("tt", wf_lduse["d"][jd]),
                                       *conv_all_done),
                           (lambda e, hc=hc, jd=jd: e.dma_start(
                               out=wd2s[jd][:, :],
                               in_=wd_r[128 * hc:128 * (hc + 1), :])))
                sl = emit("scalar", W("scalar", ("tt", gl), ("tt", hbuf_lastmm[j])),
                          (lambda e, j=j, b=bank, Tss=Tss: e.activation(
                              hscr[j][:, 0:Tss], pg[b][:, 0:Tss], AF.Silu,
                              scale=1.0 / 127.0)),
                          inc="ss")
                pgu_read[0][bank] = sl
                mu = emit("vector", W("vector", ("ss", sl), ("tt", ul),
                                      ("tt", hbuf_lastmm[j])),
                          (lambda e, j=j, b=bank, Tss=Tss: e.tensor_tensor(
                              hful[j][:, 0:Tss], hscr[j][:, 0:Tss],
                              pu[b][:, 0:Tss], ALU.mult)),
                          inc="vv")
                pgu_read[1][bank] = mu
                if tail:
                    hsp = mu
                else:
                    emit("vector", [],
                         (lambda e, j=j, Tss=Tss: e.tensor_scalar(
                             hscr[j][:, 0:Tss], hful[j][:, 0:Tss], CVELT, None, ALU.mult)),
                         inc="vv", chain=True)
                    emit("vector", [],
                         (lambda e, j=j, Tss=Tss: e.scalar_tensor_tensor(
                             uh[j][:, 0:Tss], hscr[j][:, 0:Tss], 1.0,
                             hful[j][:, 0:Tss], ALU.mult, ALU.subtract)),
                         inc="vv", chain=True)
                    emit("vector", [],
                         (lambda e, j=j, Tss=Tss: e.scalar_tensor_tensor(
                             hh[j][:, 0:Tss], hscr[j][:, 0:Tss], 1.0,
                             uh[j][:, 0:Tss], ALU.mult, ALU.subtract)),
                         inc="vv", chain=True)
                    hsp = emit("vector", [],
                               (lambda e, j=j, Tss=Tss: e.tensor_tensor(
                                   hl[j][:, 0:Tss], hful[j][:, 0:Tss],
                                   hh[j][:, 0:Tss].bitcast(F32), ALU.subtract)),
                               inc="vv", chain=True)
                pending.append((hc, j, hsp, ldd, hc))
            while pending:
                pd_args = pending.pop(0)
                dn_last = emit_down(*pd_args, tail, Tss)
                wf_lduse["d"][pd_args[4] % 2] = dn_last
                hbuf_lastmm[pd_args[1]] = dn_last
            if fi + 1 < len(FSS):
                emit("vector", W("vector", ("tt", wf_lduse["u"][0])), 
                     (lambda e: e.engine_nop()), inc="vv")
                split_done_fi[fi + 1] = emit_split(fi + 1)
            for dc in ([] if tail else range(8)):
                y2_ev = emit("vector", W("vector", ("tt", dn_last)),
                             (lambda e, dc=dc, f0=f0, Tss=Tss: e.scalar_tensor_tensor(
                                 x1v[:, dc, f0:f0 + Tss], pd_slice(dc, Tss), S2,
                                 x1v[:, dc, f0:f0 + Tss], ALU.mult, ALU.add)),
                             inc="vv")
            prev_y2_evac = y2_ev
            sq_tt2 = [0, 0]
            s1 = s2 = 0
            for dc in range(8):
                sl2 = dc % 2
                sq = emit("scalar", W("scalar", ("vv", y2_ev), ("tt", sq_tt2[sl2])),
                          (lambda e, dc=dc, sl2=sl2, f0=f0, Tss=Tss: e.activation(
                              sqf[sl2][:, 0:Tss], x1v[:, dc, f0:f0 + Tss], AF.Square)),
                          inc="ss")
                s1 = emit("tensor", W("tensor", ("vv", y2_ev)),
                          (lambda e, dc=dc, f0=f0, Tss=Tss: e.matmul(
                              pstatA[:, 0:Tss], ones_sb[:, 0:1],
                              x1v[:, dc, f0:f0 + Tss],
                              start=(dc == 0), stop=(dc == 7))),
                          inc="tt")
                s2 = emit("tensor", W("tensor", ("ss", sq)),
                          (lambda e, dc=dc, sl2=sl2, Tss=Tss: e.matmul(
                              pstatB[:, 0:Tss], ones_sb[:, 0:1], sqf[sl2][:, 0:Tss],
                              start=(dc == 0), stop=(dc == 7))),
                          inc="tt")
                sq_tt2[sl2] = s2
            emit("vector", W("vector", ("tt", s1), ("tt", s2)),
                 (lambda e, Tss=Tss: e.tensor_scalar(
                     srow[:, 0:Tss], pstatA[:, 0:Tss], 1.0 / 1024.0, None, ALU.mult)),
                 inc="vv")
            emit("vector", [],
                 (lambda e, Tss=Tss: e.tensor_scalar(
                     srow[:, 384:384 + Tss], pstatB[:, 0:Tss], 1.0 / 1024.0, None,
                     ALU.mult)),
                 inc="vv")
            emit("vector", [],
                 (lambda e, Tss=Tss: e.tensor_tensor(
                     srow[:, 768:768 + Tss], srow[:, 0:Tss], srow[:, 0:Tss],
                     ALU.mult)),
                 inc="vv", chain=True)
            emit("vector", [],
                 (lambda e, Tss=Tss: e.tensor_tensor(
                     srow[:, 1152:1152 + Tss], srow[:, 384:384 + Tss],
                     srow[:, 768:768 + Tss], ALU.subtract)),
                 inc="vv", chain=True)
            rows2_v = emit("vector", [],
                           (lambda e, Tss=Tss: e.tensor_scalar(
                               srow[:, 1152:1152 + Tss], srow[:, 1152:1152 + Tss],
                               EPS, None, ALU.add)),
                           inc="vv", chain=True)
            _taps = []
            emit("scalar", W("scalar", ("vv", rows2_v), *_taps),
                 (lambda e, Tss=Tss: e.activation(srow[:, 768:768 + Tss],
                                                  srow[:, 1152:1152 + Tss], AF.Ln)),
                 inc="ss")
            rr = emit("scalar", [],
                      (lambda e, Tss=Tss: e.activation(srow[:, 384:384 + Tss],
                                                       srow[:, 768:768 + Tss], AF.Exp,
                                                       scale=-0.5)),
                      inc="ss", chain=True)
            bb = emit("vector", W("vector", ("ss", rr)),
                      (lambda e, Tss=Tss: e.scalar_tensor_tensor(
                          srow[:, 1152:1152 + Tss], srow[:, 0:Tss], -1.0,
                          srow[:, 384:384 + Tss], ALU.mult, ALU.mult)),
                      inc="vv", chain=True)
            bc1 = emit("tensor", W("tensor", ("ss", rr)),
                       (lambda e, Tss=Tss: e.matmul(pAb[:, 0:Tss], ones_sb[0:1, :],
                                                    srow[0:1, 384:384 + Tss],
                                                    start=True, stop=True)),
                       inc="tt")
            bc2 = emit("tensor", W("tensor", ("vv", bb)),
                       (lambda e, Tss=Tss: e.matmul(pBb[:, 0:Tss], ones_sb[0:1, :],
                                                    srow[0:1, 1152:1152 + Tss],
                                                    start=True, stop=True)),
                       inc="tt")
            ev = 0
            for dc in range(8):
                emit("vector", W("vector", ("tt", bc1), ("tt", bc2),
                                 *( [out_dma[fi - 1]] if fi >= 1 else [] )),
                     (lambda e, dc=dc, f0=f0, Tss=Tss: e.tensor_tensor(
                         sqf[dc % 2][:, 0:Tss], x1v[:, dc, f0:f0 + Tss],
                         pAb[:, 0:Tss], ALU.mult)),
                     inc="vv")
                ev = emit("vector", [],
                          (lambda e, dc=dc, Tss=Tss: e.tensor_tensor(
                              outsb[:, dc, 0:Tss], sqf[dc % 2][:, 0:Tss],
                              pBb[:, 0:Tss], ALU.add)),
                          inc="vv", chain=True)
            prev_out_evac = ev
            prev_ln2rows_vv = ev
            dm = dmae("gpsimd", W("gpsimd", ("vv", ev)),
                      (lambda e, f0=f0, Tss=Tss: e.dma_start(
                          out=out_d[:, f0:f0 + Tss].rearrange("(c p) t -> p c t", p=128),
                          in_=outsb[:, :, 0:Tss])))
            out_dma.append(dm)

        finals = [(L, cnt[L]) for L in GP_LANES if cnt[L] > 0]
        def _final(e, finals=finals):
            for L, v in finals:
                e.wait_ge(sems[L], v)
        ops["gpsimd"].append(_final)

        @block.sync
        def _(eng):
            for fn in ops["sync"]:
                fn(eng)

        @block.gpsimd
        def _(eng):
            for fn in ops["gpsimd"]:
                fn(eng)

        @block.tensor
        def _(eng):
            for fn in ops["tensor"]:
                fn(eng)

        @block.vector
        def _(eng):
            for fn in ops["vector"]:
                fn(eng)

        @block.scalar
        def _(eng):
            for fn in ops["scalar"]:
                fn(eng)

    return nc


_NC_CACHE = None


def kernel(L0_attn_x, L0_attn_Wq, L0_attn_bq, L0_attn_Wk, L0_attn_bk,
           L0_attn_Wv, L0_attn_bv, L0_attn_Wo, L0_attn_bo,
           L0_ffn_w_gate, L0_ffn_w_up, L0_ffn_w_down):
    global _NC_CACHE
    x = np.ascontiguousarray(np.asarray(L0_attn_x, dtype=np.float32))
    B, L, _ = x.shape

    wqT = np.ascontiguousarray(np.asarray(L0_attn_Wq, np.float32).T)
    wkT = np.ascontiguousarray(np.asarray(L0_attn_Wk, np.float32).T)
    wvT = np.ascontiguousarray(np.asarray(L0_attn_Wv, np.float32).T)
    woT = np.ascontiguousarray(np.asarray(L0_attn_Wo, np.float32).T)
    biasm = np.zeros((128, 24), np.float32)
    biasm[:, 0:8] = np.asarray(L0_attn_bq, np.float32).reshape(8, 128).T
    biasm[:, 8:16] = np.asarray(L0_attn_bk, np.float32).reshape(8, 128).T
    biasm[:, 16:24] = np.asarray(L0_attn_bo, np.float32).reshape(8, 128).T
    bvrow = np.ascontiguousarray(np.asarray(L0_attn_bv, np.float32).reshape(1, D))
    wgTi = np.ascontiguousarray(np.asarray(L0_ffn_w_gate, np.int32).T)
    wuTi = np.ascontiguousarray(np.asarray(L0_ffn_w_up, np.int32).T)
    wdTi = np.ascontiguousarray(np.asarray(L0_ffn_w_down, np.int32).T)

    ii = np.arange(96)
    mLm = (ii[:, None] // 16 == ii[None, :] // 16).astype(np.float32)
    mAm = ((ii[:, None] // 48 == ii[None, :] // 48)
           & (ii[:, None] % 3 == ii[None, :] % 3)).astype(np.float32)
    masks_np = np.ascontiguousarray(np.concatenate([mLm, mAm], axis=1))
    identc = np.zeros((128, 256), np.float32)
    identc[:, 0:128] = np.eye(128, dtype=np.float32)
    identc[:, 128:256] = 1.0

    in_maps = []
    for c in range(8):
        s, h = c // 2, c % 2
        p0 = h * NTOK
        xs = np.zeros((NTOK, D), np.float32)
        n_real = min(L - p0, NTOK)
        xs[:n_real] = x[s, p0:p0 + n_real]
        in_maps.append(dict(xT=np.ascontiguousarray(xs.T), wqT=wqT, wkT=wkT,
                            wvT=wvT, woT=woT, biasm=biasm, bvrow=bvrow,
                            masks=masks_np, identc=identc,
                            wgTi=wgTi, wuTi=wuTi, wdTi=wdTi))

    if _NC_CACHE is None:
        _NC_CACHE = build_program()
    res = run_bass_kernel_spmd(_NC_CACHE, in_maps, core_ids=list(range(8)))

    out = np.empty((B, L, D), np.float32)
    for c in range(8):
        s, h = c // 2, c % 2
        p0 = h * NTOK
        n_real = min(L - p0, NTOK)
        oT = np.asarray(res.results[c]["out"])
        out[s, p0:p0 + n_real] = oT.T[:n_real]
    return out
